# version 01
# baseline (speedup 1.0000x reference)
import numpy as np
import ml_dtypes
import concourse.bass as bass
import concourse.mybir as mybir
from concourse.bass_utils import run_bass_kernel_spmd

F32 = mybir.dt.float32
BF16 = mybir.dt.bfloat16
AF = mybir.ActivationFunctionType
ALU = mybir.AluOpType

NC = 8
B = 2
S = 8192
D = 1024
TPC = 2048
H0 = 16
DH = 64

BF = ml_dtypes.bfloat16



class Sems:

    def __init__(self, nc, ctx, names):
        self.h = {}
        self.n = {}
        for nm in names:
            self.h[nm] = ctx.enter_context(nc.semaphore(nm))
            self.n[nm] = 0

    def inc(self, instr, nm, k=1):
        instr.then_inc(self.h[nm], k)
        self.n[nm] += k

    def wait(self, eng, nm, val=None):
        eng.wait_ge(self.h[nm], self.n[nm] if val is None else val)

    def wait_all(self, eng):
        for nm in self.h:
            eng.wait_ge(self.h[nm], self.n[nm])


PARANOID = False


class DmaStream:

    def __init__(self, nc, ctx, name, nbuf=2):
        self.s = [ctx.enter_context(nc.semaphore(f"{name}{i}")) for i in range(nbuf)]
        self.cnt = [0] * nbuf
        self.nbuf = nbuf

    def issue(self, qeng, idx, out, in_):
        p = idx % self.nbuf
        qeng.dma_start(out=out, in_=in_).then_inc(self.s[p], 16)
        self.cnt[p] += 16

    def wait(self, eng, idx):
        p = idx % self.nbuf
        eng.wait_ge(self.s[p], 16 * (idx // self.nbuf + 1))

    def wait_all(self, eng):
        for p in range(self.nbuf):
            if self.cnt[p]:
                eng.wait_ge(self.s[p], self.cnt[p])


def pwait(sems, eng):
    if PARANOID:
        sems.wait_all(eng)


def run_stage(build, per_core_inputs, trace=False):
    nc = bass.Bass("TRN2", target_bir_lowering=False)
    build(nc)
    res = run_bass_kernel_spmd(nc, per_core_inputs, core_ids=list(range(NC)), trace=trace)
    return res



def build_l0_qkv(nc, q_scale=1.0):
    from contextlib import ExitStack
    ctx = ExitStack()
    xT = nc.dram_tensor("xT", [D, TPC], F32, kind="ExternalInput").ap()
    wqkv = nc.dram_tensor("wqkv", [D, 3 * D], F32, kind="ExternalInput").ap()
    bqkv = nc.dram_tensor("bqkv", [128, 16], F32, kind="ExternalInput").ap()
    bv_row = nc.dram_tensor("bv_row", [1, D], F32, kind="ExternalInput").ap()
    qT = nc.dram_tensor("qT", [D, TPC], BF16, kind="ExternalOutput").ap()
    kT = nc.dram_tensor("kT", [D, TPC], BF16, kind="ExternalOutput").ap()
    vaug = nc.dram_tensor("vaug", [TPC, 16 * 65], BF16, kind="ExternalOutput").ap()

    with ctx:
        xb = ctx.enter_context(nc.sbuf_tensor("xb", [128, 8, TPC], BF16))
        wb = ctx.enter_context(nc.sbuf_tensor("wb", [128, 8, 3 * D], BF16))
        stg = [ctx.enter_context(nc.sbuf_tensor(f"stg{i}", [128, 3 * D], F32)) for i in range(2)]
        bq_sb = ctx.enter_context(nc.sbuf_tensor("bq_sb", [128, 16], F32))
        bvr_f = ctx.enter_context(nc.sbuf_tensor("bvr_f", [1, D], F32))
        bvr_b = ctx.enter_context(nc.sbuf_tensor("bvr_b", [1, D], BF16))
        ones1 = ctx.enter_context(nc.sbuf_tensor("ones1", [1, 128], BF16))
        qk_st = [ctx.enter_context(nc.sbuf_tensor(f"qk_st{i}", [128, TPC], BF16)) for i in range(2)]
        va = ctx.enter_context(nc.sbuf_tensor("va", [128, 16, 1040], BF16))
        ps = [ctx.enter_context(nc.psum_tensor(f"ps{i}", [128, 512], F32)) for i in range(2)]
        block = ctx.enter_context(nc.Block())
        sems = Sems(nc, ctx, ["cv", "ms", "mm", "ev", "vst"])
        ld = DmaStream(nc, ctx, "ld")
        lds = DmaStream(nc, ctx, "lds", 1)
        stp = DmaStream(nc, ctx, "stp")

        NCHUNK = 16

        @block.sync
        def _(sync):
            for i in range(NCHUNK):
                if i >= 2:
                    sync.wait_ge(sems.h["cv"], i - 1)
                if i % 2 == 0:
                    e = i // 2
                    ld.issue(sync, i, stg[i % 2][:, :TPC], xT[e * 128:(e + 1) * 128, :])
                else:
                    e = i // 2
                    ld.issue(sync, i, stg[i % 2][:, :], wqkv[e * 128:(e + 1) * 128, :])
            lds.issue(sync, 0, bq_sb[:], bqkv[:])
            lds.issue(sync, 0, bvr_f[:], bv_row[:])

        @block.vector
        def _(vector):
            for i in range(NCHUNK):
                ld.wait(vector, i)
                e = i // 2
                if i % 2 == 0:
                    sems.inc(vector.tensor_copy(xb[:, e, :], stg[i % 2][:, :TPC]), "cv")
                else:
                    sems.inc(vector.tensor_copy(wb[:, e, :], stg[i % 2][:, :]), "cv")
            lds.wait_all(vector)
            sems.inc(vector.tensor_copy(bvr_b[:], bvr_f[:]), "cv")
            if q_scale != 1.0:
                sems.inc(vector.tensor_scalar(bq_sb[:, 0:8], bq_sb[:, 0:8], q_scale, None, ALU.mult), "cv")

        @block.gpsimd
        def _(gp):
            sems.inc(gp.memset(ones1[:], 1.0), "ms")
            va_h = va[:].rearrange("p t (h c) -> p t h c", c=65)
            sems.inc(gp.memset(va_h[:, :, :, 64:65], 1.0), "ms")

        @block.tensor
        def _(tensor):
            tensor.wait_ge(sems.h["cv"], sems.n["cv"])
            tensor.wait_ge(sems.h["ms"], 1)
            nmm = 0
            for j in range(16):
                for t in range(4):
                    pj = (j * 4 + t) % 2
                    if PARANOID:
                        tensor.wait_ge(sems.h["ev"], 4 * j + t)
                    if j * 4 + t >= 2:
                        tensor.wait_ge(sems.h["ev"], j * 4 + t - 1)
                    for e in range(8):
                        sems.inc(tensor.matmul(ps[pj][:], wb[:, e, j * 128:(j + 1) * 128],
                                               xb[:, e, t * 512:(t + 1) * 512],
                                               start=(e == 0), stop=(e == 7)), "mm")
            for tt in range(16):
                for hf in range(2):
                    i = 64 + tt * 2 + hf
                    pj = i % 2
                    tensor.wait_ge(sems.h["ev"], i - 1)
                    for e in range(8):
                        sems.inc(tensor.matmul(ps[pj][:], xb[:, e, tt * 128:(tt + 1) * 128],
                                               wb[:, e, 2048 + hf * 512: 2048 + (hf + 1) * 512],
                                               start=(e == 0), stop=False), "mm")
                    sems.inc(tensor.matmul(ps[pj][:, :], ones1[:, :], bvr_b[:, hf * 512:(hf + 1) * 512],
                                           start=False, stop=True), "mm")

        @block.scalar
        def _(scalar):
            for j in range(16):
                for t in range(4):
                    pj = (j * 4 + t) % 2
                    if PARANOID:
                        scalar.wait_ge(sems.h["mm"], sems.n["mm"])
                        scalar.wait_ge(sems.h["st"], sems.n["st"] and 0)
                    scalar.wait_ge(sems.h["mm"], 8 * (j * 4 + t + 1))
                    dst = qk_st[j % 2][:, t * 512:(t + 1) * 512]
                    if t == 0 and j >= 2:
                        stp.wait(scalar, j - 2)
                    sems.inc(scalar.activation(dst, ps[pj][:], AF.Identity,
                                               bias=bq_sb[:, j:j + 1],
                                               scale=(q_scale if j < 8 else 1.0)), "ev")
            for tt in range(16):
                for hf in range(2):
                    i = 64 + tt * 2 + hf
                    scalar.wait_ge(sems.h["mm"], 8 * 64 + 9 * (tt * 2 + hf + 1))
                    va_h = va[:, tt, :].rearrange("p (h c) -> p h c", c=65)
                    sems.inc(scalar.activation(va_h[:, hf * 8:(hf + 1) * 8, 0:64], ps[i % 2][:],
                                               AF.Identity, bias=0.0, scale=1.0), "ev")

        @block.gpsimd
        def _(gp):
            for j in range(16):
                gp.wait_ge(sems.h["ev"], 4 * (j + 1))
                dram = qT if j < 8 else kT
                r = (j % 8) * 128
                stp.issue(gp, j, dram[r:r + 128, :], qk_st[j % 2][:, :])
            gp.wait_ge(sems.h["ev"], 64 + 32)
            sems.inc(gp.dma_start(out=vaug[:].rearrange("(t p) c -> p t c", p=128), in_=va[:]), "vst", 16)
            stp.wait_all(gp)
            gp.wait_ge(sems.h["vst"], 16)



def build_l1_qkv(nc):
    build_l0_qkv(nc, q_scale=1.0 / 8.0)


def host_prep_l1_qkv(inputs, res4):
    wq = inputs["L1_attn_Wq"].astype(np.float32).T
    wk = inputs["L1_attn_Wk"].astype(np.float32).T
    wv = inputs["L1_attn_Wv"].astype(np.float32).T
    wcat = np.ascontiguousarray(np.concatenate([wq, wk, wv], axis=1))
    bq = inputs["L1_attn_bq"].astype(np.float32)
    bk = inputs["L1_attn_bk"].astype(np.float32)
    bv = inputs["L1_attn_bv"].astype(np.float32)
    b16 = np.ascontiguousarray(np.concatenate([bq, bk]).reshape(16, 128).T)
    bv_row = np.ascontiguousarray(bv.reshape(1, 1024))
    per_core = []
    for c in range(NC):
        per_core.append({"xT": res4.results[c]["x2T"], "wqkv": wcat, "bqkv": b16, "bv_row": bv_row})
    return per_core


def build_l0_attn(nc):
    from contextlib import ExitStack
    ctx = ExitStack()
    qTd = nc.dram_tensor("qT", [D, TPC], BF16, kind="ExternalInput").ap()
    kTf = nc.dram_tensor("kTf", [D, S], BF16, kind="ExternalInput").ap()
    vaugf = nc.dram_tensor("vaugf", [S, 16 * 65], BF16, kind="ExternalInput").ap()
    aT = nc.dram_tensor("aT", [D, TPC], BF16, kind="ExternalOutput").ap()

    NKT = S // 128
    with ctx:
        qb = ctx.enter_context(nc.sbuf_tensor("qb", [128, 8, TPC], BF16))
        kt_sb = [ctx.enter_context(nc.sbuf_tensor(f"kt{i}", [128, 128], BF16)) for i in range(2)]
        vt_sb = [ctx.enter_context(nc.sbuf_tensor(f"vt{i}", [128, 130], BF16)) for i in range(2)]
        wA = ctx.enter_context(nc.sbuf_tensor("wA", [128, 512], BF16))
        wB = ctx.enter_context(nc.sbuf_tensor("wB", [128, 512], BF16))
        rA = ctx.enter_context(nc.sbuf_tensor("rA", [1, 512], F32))
        rB = ctx.enter_context(nc.sbuf_tensor("rB", [1, 512], F32))
        bA_sb = ctx.enter_context(nc.sbuf_tensor("bA_sb", [64, 512], F32))
        bB_sb = ctx.enter_context(nc.sbuf_tensor("bB_sb", [64, 512], F32))
        ones64 = ctx.enter_context(nc.sbuf_tensor("ones64", [1, 64], F32))
        at_st = [ctx.enter_context(nc.sbuf_tensor(f"at_st{i}", [128, TPC], BF16)) for i in range(2)]
        psA = ctx.enter_context(nc.psum_tensor("psA", [128, 512], F32))
        psB = ctx.enter_context(nc.psum_tensor("psB", [128, 512], F32))
        oA = ctx.enter_context(nc.psum_tensor("oA", [65, 512], F32))
        oB = ctx.enter_context(nc.psum_tensor("oB", [65, 512], F32))
        bA_ps = ctx.enter_context(nc.psum_tensor("bA_ps", [64, 512], F32))
        bB_ps = ctx.enter_context(nc.psum_tensor("bB_ps", [64, 512], F32))
        block = ctx.enter_context(nc.Block())
        sems = Sems(nc, ctx, ["ld", "tm", "ev", "rc", "tb", "bc", "no", "ms"])
        kv = DmaStream(nc, ctx, "kv")
        stp = DmaStream(nc, ctx, "stp")

        def g_of(hp, qc, kt):
            return (hp * 4 + qc) * NKT + kt

        @block.gpsimd
        def _(gp):
            sems.inc(gp.memset(ones64[:], 1.0), "ms")

        @block.sync
        def _(sync):
            for e in range(8):
                sems.inc(sync.dma_start(out=qb[:, e, :], in_=qTd[e * 128:(e + 1) * 128, :]), "ld", 16)
            for hp in range(8):
                for qc in range(4):
                    for kt in range(NKT):
                        g = g_of(hp, qc, kt)
                        if g >= 2:
                            sync.wait_ge(sems.h["tm"], 4 * (g - 1))
                        p = g % 2
                        sync.dma_start(out=kt_sb[p][:],
                                       in_=kTf[hp * 128:(hp + 1) * 128,
                                               kt * 128:(kt + 1) * 128]).then_inc(kv.s[p], 16)
                        sync.dma_start(out=vt_sb[p][:],
                                       in_=vaugf[kt * 128:(kt + 1) * 128,
                                                 130 * hp:130 * (hp + 1)]).then_inc(kv.s[p], 16)
                        kv.cnt[p] += 32

        @block.tensor
        def _(tensor):
            tensor.wait_ge(sems.h["ld"], 8 * 16)
            for hp in range(8):
                for qc in range(4):
                    u = hp * 4 + qc
                    for kt in range(NKT):
                        g = g_of(hp, qc, kt)
                        tensor.wait_ge(kv.s[g % 2], 32 * (g // 2 + 1))
                        if g >= 1:
                            tensor.wait_ge(sems.h["ev"], 2 * g)
                        sems.inc(tensor.matmul(psA[:], kt_sb[g % 2][0:64, :],
                                               qb[0:64, hp, qc * 512:(qc + 1) * 512],
                                               start=True, stop=True, tile_position=(0, 0)), "tm")
                        sems.inc(tensor.matmul(psB[:], kt_sb[g % 2][64:128, :],
                                               qb[64:128, hp, qc * 512:(qc + 1) * 512],
                                               start=True, stop=True, tile_position=(64, 0)), "tm")
                        tensor.wait_ge(sems.h["ev"], 2 * (g + 1))
                        if kt == 0 and u >= 1:
                            tensor.wait_ge(sems.h["no"], 2 * u)
                        sems.inc(tensor.matmul(oA[:], vt_sb[g % 2][:, 0:65], wA[:],
                                               start=(kt == 0), stop=(kt == NKT - 1)), "tm")
                        sems.inc(tensor.matmul(oB[:], vt_sb[g % 2][:, 65:130], wB[:],
                                               start=(kt == 0), stop=(kt == NKT - 1)), "tm")
                    tensor.wait_ge(sems.h["rc"], 2 * (u + 1))
                    tensor.wait_ge(sems.h["ms"], 1)
                    if u >= 1:
                        tensor.wait_ge(sems.h["bc"], 2 * u)
                    sems.inc(tensor.matmul(bA_ps[:], ones64[:], rA[:], start=True, stop=True), "tb")
                    sems.inc(tensor.matmul(bB_ps[:], ones64[:], rB[:], start=True, stop=True), "tb")

        @block.scalar
        def _(scalar):
            for hp in range(8):
                for qc in range(4):
                    u = hp * 4 + qc
                    for kt in range(NKT):
                        g = g_of(hp, qc, kt)
                        scalar.wait_ge(sems.h["tm"], 4 * g + 2)
                        sems.inc(scalar.activation(wA[:], psA[:], AF.Exp, bias=0.0, scale=1.0), "ev")
                        sems.inc(scalar.activation(wB[:], psB[:], AF.Exp, bias=0.0, scale=1.0), "ev")
                    scalar.wait_ge(sems.h["tb"], 2 * (u + 1))
                    if u >= 1:
                        scalar.wait_ge(sems.h["no"], 2 * u)
                    sems.inc(scalar.copy(bA_sb[:], bA_ps[:]), "bc")
                    sems.inc(scalar.copy(bB_sb[:], bB_ps[:]), "bc")

        @block.vector
        def _(vector):
            for hp in range(8):
                for qc in range(4):
                    u = hp * 4 + qc
                    vector.wait_ge(sems.h["tm"], 4 * NKT * (u + 1))
                    sems.inc(vector.reciprocal(rA[:], oA[64:65, :]), "rc")
                    sems.inc(vector.reciprocal(rB[:], oB[64:65, :]), "rc")
                    vector.wait_ge(sems.h["bc"], 2 * (u + 1))
                    if qc == 0 and hp >= 2:
                        stp.wait(vector, hp - 2)
                    sems.inc(vector.scalar_tensor_tensor(
                        at_st[hp % 2][0:64, qc * 512:(qc + 1) * 512], oA[0:64, :], 1.0 / 32.0,
                        bA_sb[:], ALU.mult, ALU.mult), "no")
                    sems.inc(vector.scalar_tensor_tensor(
                        at_st[hp % 2][64:128, qc * 512:(qc + 1) * 512], oB[0:64, :], 1.0 / 32.0,
                        bB_sb[:], ALU.mult, ALU.mult), "no")

        @block.gpsimd
        def _(gp):
            for hp in range(8):
                gp.wait_ge(sems.h["no"], 2 * (hp * 4 + 4))
                stp.issue(gp, hp, aT[hp * 128:(hp + 1) * 128, :], at_st[hp % 2][:])
            stp.wait_all(gp)


def host_prep_l0_attn(inputs, stage1_res):
    per_core = []
    kT_by_batch = []
    va_by_batch = []
    for b in range(B):
        kT_by_batch.append(np.concatenate(
            [stage1_res.results[b * 4 + i]["kT"] for i in range(4)], axis=1))
        va_by_batch.append(np.concatenate(
            [stage1_res.results[b * 4 + i]["vaug"] for i in range(4)], axis=0))
    for c in range(NC):
        b = c // 4
        per_core.append({"qT": stage1_res.results[c]["qT"],
                         "kTf": kT_by_batch[b], "vaugf": va_by_batch[b]})
    return per_core



def build_l1_attn(nc):
    from contextlib import ExitStack
    ctx = ExitStack()
    qTd = nc.dram_tensor("qT", [D, TPC], BF16, kind="ExternalInput").ap()
    kTf = nc.dram_tensor("kTf", [D, S], BF16, kind="ExternalInput").ap()
    vaugf = nc.dram_tensor("vaugf", [S, 16 * 65], BF16, kind="ExternalInput").ap()
    ident = nc.dram_tensor("ident", [128, 128], BF16, kind="ExternalInput").ap()
    aT = nc.dram_tensor("aT", [D, TPC], BF16, kind="ExternalOutput").ap()

    NEG = -3.0e38
    with ctx:
        qb = ctx.enter_context(nc.sbuf_tensor("qb", [128, 8, TPC], BF16))
        Kres = ctx.enter_context(nc.sbuf_tensor("Kres", [128, S], BF16))
        Vres = ctx.enter_context(nc.sbuf_tensor("Vres", [128, 64, 130], BF16))
        scA = ctx.enter_context(nc.sbuf_tensor("scA", [128, S], BF16))
        scB = ctx.enter_context(nc.sbuf_tensor("scB", [128, S], BF16))
        tmpA = ctx.enter_context(nc.sbuf_tensor("tmpA", [128, S], BF16))
        tmpB = ctx.enter_context(nc.sbuf_tensor("tmpB", [128, S], BF16))
        t8 = ctx.enter_context(nc.sbuf_tensor("t8", [128, 8], BF16))
        nt = ctx.enter_context(nc.sbuf_tensor("nt", [128, 1], F32))
        vscr = ctx.enter_context(nc.sbuf_tensor("vscr", [128, 128], F32))
        wT = ctx.enter_context(nc.sbuf_tensor("wT", [128, 128], BF16))
        id_sb = ctx.enter_context(nc.sbuf_tensor("id_sb", [128, 128], BF16))
        rA = ctx.enter_context(nc.sbuf_tensor("rA", [1, 128], F32))
        bA_sb = ctx.enter_context(nc.sbuf_tensor("bA_sb", [64, 128], F32))
        ones64 = ctx.enter_context(nc.sbuf_tensor("ones64", [1, 64], F32))
        at_st = [ctx.enter_context(nc.sbuf_tensor(f"at_st{i}", [128, TPC], BF16)) for i in range(2)]
        psA = ctx.enter_context(nc.psum_tensor("psA", [128, 512], F32))
        psB = ctx.enter_context(nc.psum_tensor("psB", [128, 512], F32))
        tp_ps = ctx.enter_context(nc.psum_tensor("tp_ps", [128, 128], BF16))
        o_ps = ctx.enter_context(nc.psum_tensor("o_ps", [65, 128], F32))
        b_ps = ctx.enter_context(nc.psum_tensor("b_ps", [64, 128], F32))
        block = ctx.enter_context(nc.Block())
        sems = Sems(nc, ctx, ["ld", "kv", "ms", "mm", "ev", "xt", "ex", "mk",
                              "tp", "tv", "av", "rc", "tb", "bc", "no"])
        stp = DmaStream(nc, ctx, "stp")

        @block.gpsimd
        def _(gp):
            sems.inc(gp.memset(ones64[:], 1.0), "ms")

        @block.sync
        def _(sync):
            for e in range(8):
                sems.inc(sync.dma_start(out=qb[:, e, :], in_=qTd[e * 128:(e + 1) * 128, :]), "ld", 16)
            sems.inc(sync.dma_start(out=id_sb[:], in_=ident[:]), "ld", 16)

        for hp in range(8):
            @block.sync
            def _(sync):
                sems.wait(sync, "no")
                sems.inc(sync.dma_start(out=Kres[:], in_=kTf[hp * 128:(hp + 1) * 128, :]), "kv", 16)
                va_src = vaugf[:, 130 * hp:130 * (hp + 1)].rearrange("(kt p) c -> p kt c", p=128)
                for i in range(8):
                    sems.inc(sync.dma_start(out=Vres[:, i * 8:(i + 1) * 8, :],
                                            in_=va_src[:, i * 8:(i + 1) * 8, :]), "kv", 16)

            for qt in range(16):
                for kc in range(16):
                    @block.tensor
                    def _(tensor):
                        if hp == 0 and qt == 0 and kc == 0:
                            tensor.wait_ge(sems.h["ld"], 16 * 9)
                        sems.wait(tensor, "kv")
                        sems.wait(tensor, "ev")
                        sems.inc(tensor.matmul(psA[:], qb[0:64, hp, qt * 128:(qt + 1) * 128],
                                               Kres[0:64, kc * 512:(kc + 1) * 512],
                                               start=True, stop=True, tile_position=(0, 0)), "mm")
                        sems.inc(tensor.matmul(psB[:], qb[64:128, hp, qt * 128:(qt + 1) * 128],
                                               Kres[64:128, kc * 512:(kc + 1) * 512],
                                               start=True, stop=True, tile_position=(64, 0)), "mm")

                    @block.scalar
                    def _(scalar):
                        sems.wait(scalar, "mm")
                        if kc == 0:
                            sems.wait(scalar, "ex")
                        sems.inc(scalar.activation(scA[:, kc * 512:(kc + 1) * 512], psA[:],
                                                   AF.Identity, bias=0.0, scale=1.0), "ev")
                        sems.inc(scalar.activation(scB[:, kc * 512:(kc + 1) * 512], psB[:],
                                                   AF.Identity, bias=0.0, scale=1.0), "ev")

                for head in range(2):
                    sc = scA if head == 0 else scB
                    vlo, vhi = (0, 65) if head == 0 else (65, 130)
                    prow = slice(0, 64) if head == 0 else slice(64, 128)

                    @block.vector
                    def _(vector):
                        sems.wait(vector, "ev")
                        sems.wait(vector, "tp")
                        sems.inc(vector.max(t8[:], sc[:]), "xt")
                        sems.wait(vector, "xt")
                        sems.inc(vector.match_replace(tmpA[:], t8[:], sc[:], NEG), "xt")
                        sems.wait(vector, "xt")
                        sems.inc(vector.max(t8[:], tmpA[:]), "xt")
                        sems.wait(vector, "xt")
                        sems.inc(vector.match_replace(tmpB[:], t8[:], tmpA[:], NEG), "xt")
                        sems.wait(vector, "xt")
                        sems.inc(vector.max(t8[:], tmpB[:]), "xt")
                        sems.wait(vector, "xt")
                        sems.inc(vector.match_replace(tmpA[:], t8[:], tmpB[:], NEG), "xt")
                        sems.wait(vector, "xt")
                        sems.inc(vector.max(t8[:], tmpA[:]), "xt")
                        sems.wait(vector, "xt")
                        sems.inc(vector.tensor_scalar(nt[:], t8[:, 7:8], -1.0, None, ALU.mult), "xt")

                    @block.scalar
                    def _(scalar):
                        sems.wait(scalar, "xt")
                        sems.inc(scalar.activation(tmpA[:], sc[:], AF.Exp, bias=nt[:], scale=1.0), "ex")

                    @block.vector
                    def _(vector):
                        sems.wait(vector, "ex")
                        sems.inc(vector.scalar_tensor_tensor(tmpB[:], tmpA[:], 1.0, tmpA[:],
                                                             ALU.is_ge, ALU.mult), "mk")

                    for kt in range(64):
                        @block.tensor
                        def _(tensor):
                            sems.wait(tensor, "mk")
                            sems.wait(tensor, "tv")
                            if kt == 0:
                                sems.wait(tensor, "no")
                            sems.inc(tensor.transpose(tp_ps[:], tmpB[:, kt * 128:(kt + 1) * 128],
                                                      id_sb[:]), "tp")

                        @block.vector
                        def _(vector):
                            sems.wait(vector, "tp")
                            sems.wait(vector, "av")
                            sems.inc(vector.tensor_copy(wT[:], tp_ps[:]), "tv")

                        @block.tensor
                        def _(tensor):
                            sems.wait(tensor, "tv")
                            sems.inc(tensor.matmul(o_ps[:], Vres[:, kt, vlo:vhi], wT[:],
                                                   start=(kt == 0), stop=(kt == 63)), "av")

                    @block.vector
                    def _(vector):
                        sems.wait(vector, "av")
                        sems.inc(vector.reciprocal(rA[:], o_ps[64:65, :]), "rc")
                        sems.wait(vector, "rc")
                        sems.inc(vector.tensor_copy(vscr[0:1, :128], rA[:]), "rc")

                    @block.tensor
                    def _(tensor):
                        sems.wait(tensor, "rc")
                        sems.wait(tensor, "ms")
                        sems.wait(tensor, "bc")
                        sems.inc(tensor.matmul(b_ps[:], ones64[:], rA[:], start=True, stop=True), "tb")

                    @block.scalar
                    def _(scalar):
                        sems.wait(scalar, "tb")
                        sems.wait(scalar, "no")
                        sems.inc(scalar.copy(bA_sb[:], b_ps[:]), "bc")

                    @block.vector
                    def _(vector):
                        sems.wait(vector, "bc")
                        if qt == 0 and head == 0 and hp >= 2:
                            stp.wait(vector, hp - 2)
                        sems.inc(vector.scalar_tensor_tensor(
                            at_st[hp % 2][prow, qt * 128:(qt + 1) * 128], o_ps[0:64, :], 1.0,
                            bA_sb[:], ALU.mult, ALU.mult), "no")

            @block.gpsimd
            def _(gp):
                sems.wait(gp, "no")
                stp.issue(gp, hp, aT[hp * 128:(hp + 1) * 128, :], at_st[hp % 2][:])

        @block.gpsimd
        def _(gp):
            stp.wait_all(gp)


def host_prep_l1_attn(inputs, res5):
    per_core = []
    kT_by_batch, va_by_batch = [], []
    for b in range(B):
        kT_by_batch.append(np.concatenate(
            [res5.results[b * 4 + i]["kT"] for i in range(4)], axis=1))
        va_by_batch.append(np.concatenate(
            [res5.results[b * 4 + i]["vaug"] for i in range(4)], axis=0))
    ident = np.eye(128, dtype=BF)
    for c in range(NC):
        b = c // 4
        per_core.append({"qT": res5.results[c]["qT"], "kTf": kT_by_batch[b],
                         "vaugf": va_by_batch[b], "ident": ident})
    return per_core



def emit_ln(nc, block, sems, ctx, yf, x_out_cb, eps=1e-5, nchunk=8, ntc=4, tw=512,
            onesc=None, onesr=None, tag="", psum_pool=None, src_cb=None, y2_dtype=F32,
            uniq="", pre_wait=(), bufs=None):
    D_ = 128 * nchunk
    if psum_pool is None:
        sum_ps = [ctx.enter_context(nc.psum_tensor(f"sum_ps{tag}{i}", [1, 512], F32)) for i in range(2)]
        sum2_ps = [ctx.enter_context(nc.psum_tensor(f"sum2_ps{tag}{i}", [1, 512], F32)) for i in range(2)]
        ab_full = ctx.enter_context(nc.psum_tensor(f"ab_ps{tag}", [128, 2, 512], F32))
        abA, abB = ab_full[:, 0, :], ab_full[:, 1, :]
    else:
        sum_ps = [psum_pool[0][0:1, :], psum_pool[1][0:1, :]]
        sum2_ps = [psum_pool[2][0:1, :], psum_pool[3][0:1, :]]
        abA, abB = psum_pool[4][:, :], psum_pool[5][:, :]
    if bufs is None:
        y2stg = [ctx.enter_context(nc.sbuf_tensor(f"y2stg{tag}{uniq}{i}", [128, 512], y2_dtype)) for i in range(2)]
        rowA = ctx.enter_context(nc.sbuf_tensor(f"rowA{tag}{uniq}", [1, 512], F32))
        rowB = ctx.enter_context(nc.sbuf_tensor(f"rowB{tag}{uniq}", [1, 512], F32))
        rtmp = ctx.enter_context(nc.sbuf_tensor(f"rtmp{tag}{uniq}", [1, 512], F32))
        rowS = ctx.enter_context(nc.sbuf_tensor(f"rowS{tag}{uniq}", [1, 512], F32))
        Ab = [ctx.enter_context(nc.sbuf_tensor(f"Ab{tag}{uniq}{i}", [128, 512], F32)) for i in range(2)]
        Bb = [ctx.enter_context(nc.sbuf_tensor(f"Bb{tag}{uniq}{i}", [128, 512], F32)) for i in range(2)]
        bufs = dict(y2stg=y2stg, rowA=rowA, rowB=rowB, rtmp=rtmp, rowS=rowS, Ab=Ab, Bb=Bb)
    else:
        y2stg, rowA, rowB, rtmp, rowS, Ab, Bb = (bufs["y2stg"], bufs["rowA"], bufs["rowB"],
                                                 bufs["rtmp"], bufs["rowS"], bufs["Ab"], bufs["Bb"])

    if src_cb is None:
        src_cb = lambda e, t: yf[:, e, t * tw:(t + 1) * tw]
    onesc_b = None
    for t in range(ntc):
        pt = t % 2
        @block.tensor
        def _(tensor):
            sems.wait(tensor, f"lnf{tag}")
            sems.wait(tensor, f"lnv{tag}")
            for pw in pre_wait:
                sems.wait(tensor, pw)
            for e in range(nchunk):
                sems.inc(tensor.matmul(sum_ps[pt][:, :tw], onesc[:], src_cb(e, t),
                                       start=(e == 0), stop=(e == nchunk - 1)), f"lnm{tag}")

        for e in range(nchunk):
            pe = e % 2
            @block.scalar
            def _(scalar):
                sems.wait(scalar, f"lnm{tag}")
                sems.inc(scalar.activation(y2stg[pe][:, :tw], src_cb(e, t),
                                           AF.Square, bias=0.0, scale=1.0), f"lnsq{tag}")

            @block.tensor
            def _(tensor):
                sems.wait(tensor, f"lnsq{tag}")
                sems.inc(tensor.matmul(sum2_ps[pt][:, :tw], onesc[:], y2stg[pe][:, :tw],
                                       start=(e == 0), stop=(e == nchunk - 1)), f"lnm{tag}")

        @block.vector
        def _(vector):
            sems.wait(vector, f"lnm{tag}")
            sems.inc(vector.tensor_scalar(rowA[:, :tw], sum_ps[pt][:, :tw], 1.0 / D_, None, ALU.mult), f"lnv{tag}")
            sems.inc(vector.tensor_scalar(rtmp[:, :tw], sum2_ps[pt][:, :tw], 1.0 / D_, eps, ALU.mult, ALU.add), f"lnv{tag}")
            sems.inc(vector.scalar_tensor_tensor(rowB[:, :tw], rowA[:, :tw], 1.0, rowA[:, :tw],
                                                 ALU.mult, ALU.mult), f"lnv{tag}")
            sems.inc(vector.tensor_tensor(rtmp[:, :tw], rtmp[:, :tw], rowB[:, :tw], ALU.subtract), f"lnv{tag}")

        @block.scalar
        def _(scalar):
            sems.wait(scalar, f"lnv{tag}")
            sems.inc(scalar.activation(rowS[:, :tw], rtmp[:, :tw], AF.Sqrt, bias=0.0, scale=1.0), f"lnst{tag}")

        @block.vector
        def _(vector):
            sems.wait(vector, f"lnst{tag}")
            sems.inc(vector.reciprocal(rtmp[:, :tw], rowS[:, :tw]), f"lnr{tag}")
            sems.inc(vector.scalar_tensor_tensor(rowB[:, :tw], rowA[:, :tw], -1.0, rtmp[:, :tw],
                                                 ALU.mult, ALU.mult), f"lnr{tag}")

        @block.tensor
        def _(tensor):
            sems.wait(tensor, f"lnr{tag}")
            sems.wait(tensor, f"lnbc{tag}")
            sems.inc(tensor.matmul(abA[:, :tw], onesr[:], rtmp[:, :tw], start=True, stop=True), f"lnb{tag}")
            sems.inc(tensor.matmul(abB[:, :tw], onesr[:], rowB[:, :tw], start=True, stop=True), f"lnb{tag}")

        @block.scalar
        def _(scalar):
            sems.wait(scalar, f"lnb{tag}")
            sems.wait(scalar, f"lnc{tag}")
            sems.inc(scalar.copy(Ab[pt][:, :tw], abA[:, :tw]), f"lnbc{tag}")
            sems.inc(scalar.copy(Bb[pt][:, :tw], abB[:, :tw]), f"lnbc{tag}")

        @block.vector
        def _(vector):
            sems.wait(vector, f"lnbc{tag}")
            for e in range(nchunk):
                x_out_cb(vector, e, t, Ab[pt], Bb[pt])


def build_l0_post(nc, with_pooled=True):
    from contextlib import ExitStack
    ctx = ExitStack()
    aT = nc.dram_tensor("aT", [D, TPC], BF16, kind="ExternalInput").ap()
    wout = nc.dram_tensor("wout", [D, D], F32, kind="ExternalInput").ap()
    bout = nc.dram_tensor("bout", [128, 8], F32, kind="ExternalInput").ap()
    xT = nc.dram_tensor("xT", [D, TPC], F32, kind="ExternalInput").ap()
    x1T = nc.dram_tensor("x1T", [D, TPC], F32, kind="ExternalOutput").ap()
    if with_pooled:
        pooled = nc.dram_tensor("pooled", [128, 8], F32, kind="ExternalOutput").ap()

    with ctx:
        ab = ctx.enter_context(nc.sbuf_tensor("ab", [128, 8, TPC], BF16))
        wb = ctx.enter_context(nc.sbuf_tensor("wb", [128, 8, D], BF16))
        stg = [ctx.enter_context(nc.sbuf_tensor(f"stg{i}", [128, D], F32)) for i in range(2)]
        xstg = [ctx.enter_context(nc.sbuf_tensor(f"xstg{i}", [128, 512], F32)) for i in range(2)]
        bo_sb = ctx.enter_context(nc.sbuf_tensor("bo_sb", [128, 8], F32))
        yf = ctx.enter_context(nc.sbuf_tensor("yf", [128, 8, TPC], F32))
        x1b = yf
        onesc = ctx.enter_context(nc.sbuf_tensor("onesc", [128, 1], F32))
        onesr = ctx.enter_context(nc.sbuf_tensor("onesr", [1, 128], F32))
        ptmp = ctx.enter_context(nc.sbuf_tensor("ptmp", [128, 8], F32))
        pacc = ctx.enter_context(nc.sbuf_tensor("pacc", [128, 8], F32))
        ps = [ctx.enter_context(nc.psum_tensor(f"ps{i}", [128, 512], F32)) for i in range(2)]
        block = ctx.enter_context(nc.Block())
        names = ["cv", "ms", "mm", "ev", "pr", "st"]
        names += ["lnf", "lnm", "lnsq", "lnv", "lnst", "lnr", "lnb", "lnbc", "lnc"]
        sems = Sems(nc, ctx, names)
        ld = DmaStream(nc, ctx, "ld", 1)
        wl = DmaStream(nc, ctx, "wl")
        xl = DmaStream(nc, ctx, "xl")

        @block.gpsimd
        def _(gp):
            sems.inc(gp.memset(onesc[:], 1.0), "ms")
            sems.inc(gp.memset(onesr[:], 1.0), "ms")

        @block.sync
        def _(sync):
            for e in range(8):
                ld.issue(sync, 0, ab[:, e, :], aT[e * 128:(e + 1) * 128, :])
            ld.issue(sync, 0, bo_sb[:], bout[:])
            for e in range(8):
                if e >= 2:
                    sync.wait_ge(sems.h["cv"], e - 1)
                wl.issue(sync, e, stg[e % 2][:], wout[e * 128:(e + 1) * 128, :])

        @block.vector
        def _(vector):
            for e in range(8):
                wl.wait(vector, e)
                sems.inc(vector.tensor_copy(wb[:, e, :], stg[e % 2][:]), "cv")

        for j in range(8):
            for t in range(4):
                g = j * 4 + t
                pj = g % 2

                @block.sync
                def _(sync):
                    if g >= 2:
                        sync.wait_ge(sems.h["ev"], g - 1)
                    xl.issue(sync, g, xstg[pj][:], xT[j * 128:(j + 1) * 128,
                                                     t * 512:(t + 1) * 512])

                @block.tensor
                def _(tensor):
                    if g == 0:
                        tensor.wait_ge(sems.h["cv"], 8)
                        ld.wait_all(tensor)
                    if g >= 2:
                        tensor.wait_ge(sems.h["ev"], g - 1)
                    for e in range(8):
                        sems.inc(tensor.matmul(ps[pj][:], wb[:, e, j * 128:(j + 1) * 128],
                                               ab[:, e, t * 512:(t + 1) * 512],
                                               start=(e == 0), stop=(e == 7)), "mm")

                @block.vector
                def _(vector):
                    sems.wait(vector, "mm")
                    xl.wait(vector, g)
                    sems.inc(vector.scalar_tensor_tensor(yf[:, j, t * 512:(t + 1) * 512], ps[pj][:],
                                                         bo_sb[:, j:j + 1], xstg[pj][:],
                                                         ALU.add, ALU.add), "ev")
        @block.vector
        def _(vector):
            sems.wait(vector, "ev")
            sems.inc(vector.tensor_copy(ptmp[:, 0:1], yf[:, 0, 0:1]), "lnf")

        def consume(vector, e, t, Abuf, Bbuf):
            sems.inc(vector.scalar_tensor_tensor(x1b[:, e, t * 512:(t + 1) * 512],
                                                 yf[:, e, t * 512:(t + 1) * 512], 1.0,
                                                 Abuf[:], ALU.mult, ALU.mult), "pr")
            sems.inc(vector.tensor_tensor(x1b[:, e, t * 512:(t + 1) * 512],
                                          x1b[:, e, t * 512:(t + 1) * 512], Bbuf[:], ALU.add), "lnc")

        emit_ln(nc, block, sems, ctx, yf, consume, onesc=onesc, onesr=onesr, tag="")

        if with_pooled:
            @block.vector
            def _(vector):
                sems.wait(vector, "lnc")
                for e in range(8):
                    for t in range(4):
                        sems.inc(vector.tensor_reduce(ptmp[:, t:t + 1],
                                                      x1b[:, e, t * 512:(t + 1) * 512],
                                                      mybir.AxisListType.X, ALU.add), "pr")
                    sems.wait(vector, "pr")
                    sems.inc(vector.tensor_reduce(pacc[:, e:e + 1], ptmp[:, 0:4],
                                                  mybir.AxisListType.X, ALU.add), "pr")
                sems.wait(vector, "pr")

        @block.gpsimd
        def _(gp):
            sems.wait(gp, "pr")
            sems.wait(gp, "lnc")
            for e in range(8):
                sems.inc(gp.dma_start(out=x1T[e * 128:(e + 1) * 128, :], in_=x1b[:, e, :]), "st", 16)
            if with_pooled:
                sems.inc(gp.dma_start(out=pooled[:], in_=pacc[:]), "st", 16)
            sems.wait(gp, "st")



def build_l0_moe(nc):
    from contextlib import ExitStack
    ctx = ExitStack()
    x1T = nc.dram_tensor("x1T", [D, TPC], F32, kind="ExternalInput").ap()
    pooled_in = [nc.dram_tensor(f"pooled{i}", [128, 8], F32, kind="ExternalInput").ap() for i in range(4)]
    gwT = nc.dram_tensor("gwT", [D, 8], F32, kind="ExternalInput").ap()
    dwT = nc.dram_tensor("dwT", [8, D, 64], F32, kind="ExternalInput").ap()
    db = nc.dram_tensor("db", [64, 8], F32, kind="ExternalInput").ap()
    upa = nc.dram_tensor("upa", [8, 65, D], F32, kind="ExternalInput").ap()
    x2T = nc.dram_tensor("x2T", [D, TPC], F32, kind="ExternalOutput").ap()
    dbg = nc.dram_tensor("dbg", [65, 64], F32, kind="ExternalOutput").ap()

    with ctx:
        stgx = [ctx.enter_context(nc.sbuf_tensor(f"stgx{i}", [128, TPC], F32)) for i in range(2)]
        xstg = [ctx.enter_context(nc.sbuf_tensor(f"xstg{i}", [128, 512], F32)) for i in range(2)]
        xb = ctx.enter_context(nc.sbuf_tensor("xb", [128, 8, TPC], BF16))
        pl = ctx.enter_context(nc.sbuf_tensor("pl", [128, 4, 8], F32))
        gw_sb = ctx.enter_context(nc.sbuf_tensor("gw_sb", [128, 8, 8], F32))
        dstg = ctx.enter_context(nc.sbuf_tensor("dstg", [128, 8, 64], F32))
        dwb = ctx.enter_context(nc.sbuf_tensor("dwb", [128, 8, 8, 64], BF16))
        ustg = ctx.enter_context(nc.sbuf_tensor("ustg", [65, D], F32))
        ub = ctx.enter_context(nc.sbuf_tensor("ub", [65, 8, D], BF16))
        db_sb = ctx.enter_context(nc.sbuf_tensor("db_sb", [64, 8], F32))
        pool_f = ctx.enter_context(nc.sbuf_tensor("pool_f", [128, 8], F32))
        glr = ctx.enter_context(nc.sbuf_tensor("glr", [1, 16], F32))
        gate = ctx.enter_context(nc.sbuf_tensor("gate", [1, 8], F32))
        srt = ctx.enter_context(nc.sbuf_tensor("srt", [1, 8], F32))
        mrow = ctx.enter_context(nc.sbuf_tensor("mrow", [1, 8], F32))
        me_sb = ctx.enter_context(nc.sbuf_tensor("me_sb", [64, 8], F32))
        onesr = ctx.enter_context(nc.sbuf_tensor("onesr", [1, 128], F32))
        onesc = ctx.enter_context(nc.sbuf_tensor("onesc", [128, 1], F32))
        ones512 = ctx.enter_context(nc.sbuf_tensor("ones512", [1, 512], BF16))
        gaug = ctx.enter_context(nc.sbuf_tensor("gaug", [65, 8, 512], BF16))
        pool = [ctx.enter_context(nc.psum_tensor(f"pp{i}", [128, 512], F32)) for i in range(8)]
        block = ctx.enter_context(nc.Block())
        names = ["cv", "ms", "gt", "mm", "ev", "sc", "up", "st",
                 "lnf", "lnm", "lnsq", "lnv", "lnst", "lnr", "lnb", "lnbc", "lnc"]
        sems = Sems(nc, ctx, names)
        sx = DmaStream(nc, ctx, "sx")
        lds = DmaStream(nc, ctx, "lds", 1)
        dl = DmaStream(nc, ctx, "dl", 1)
        ul = DmaStream(nc, ctx, "ul", 1)
        xl = DmaStream(nc, ctx, "xl")
        out_f = ctx.enter_context(nc.sbuf_tensor("out_f", [128, 8, TPC], F32))

        @block.gpsimd
        def _(gp):
            sems.inc(gp.memset(onesr[:], 1.0), "ms")
            sems.inc(gp.memset(onesc[:], 1.0), "ms")
            sems.inc(gp.memset(ones512[:], 1.0), "ms")

        @block.sync
        def _(sync):
            for e in range(8):
                if e >= 2:
                    sync.wait_ge(sems.h["cv"], e - 1)
                sx.issue(sync, e, stgx[e % 2][:], x1T[e * 128:(e + 1) * 128, :])
            for i in range(4):
                lds.issue(sync, 0, pl[:, i, :], pooled_in[i][:])
            lds.issue(sync, 0, gw_sb[:], gwT[:].rearrange("(c p) g -> p c g", p=128))
            lds.issue(sync, 0, db_sb[:], db[:])

        @block.vector
        def _(vector):
            for e in range(8):
                sx.wait(vector, e)
                sems.inc(vector.tensor_copy(xb[:, e, :], stgx[e % 2][:]), "cv")

        @block.sync
        def _(sync):
            for ex in range(8):
                if ex >= 1:
                    sync.wait_ge(sems.h["cv"], 8 + ex)
                dl.issue(sync, 0, dstg[:], dwT[ex].rearrange("(c p) v -> p c v", p=128))

        @block.vector
        def _(vector):
            for ex in range(8):
                vector.wait_ge(dl.s[0], 16 * (ex + 1))
                sems.inc(vector.tensor_copy(dwb[:, ex, :, :], dstg[:]), "cv")

        @block.sync
        def _(sync):
            for ex in range(8):
                if ex >= 1:
                    sync.wait_ge(sems.h["cv"], 16 + ex)
                ul.issue(sync, 0, ustg[:], upa[ex])

        @block.vector
        def _(vector):
            for ex in range(8):
                vector.wait_ge(ul.s[0], 16 * (ex + 1))
                sems.inc(vector.tensor_copy(ub[:, ex, :], ustg[:]), "cv")

        @block.vector
        def _(vector):
            lds.wait_all(vector)
            sems.inc(vector.tensor_tensor(pool_f[:], pl[:, 0, :], pl[:, 1, :], ALU.add), "gt")
            sems.wait(vector, "gt")
            sems.inc(vector.tensor_tensor(pool_f[:], pool_f[:], pl[:, 2, :], ALU.add), "gt")
            sems.wait(vector, "gt")
            sems.inc(vector.tensor_tensor(pool_f[:], pool_f[:], pl[:, 3, :], ALU.add), "gt")
            sems.wait(vector, "gt")
            sems.inc(vector.tensor_scalar(pool_f[:], pool_f[:], 1.0 / S, None, ALU.mult), "gt")
            sems.wait(vector, "gt")

        @block.tensor
        def _(tensor):
            sems.wait(tensor, "gt")
            lds.wait_all(tensor)
            for cch in range(8):
                sems.inc(tensor.matmul(pool[4][0:1, 0:8], pool_f[:, cch:cch + 1], gw_sb[:, cch, :],
                                       start=(cch == 0), stop=(cch == 7)), "gt")

        @block.scalar
        def _(scalar):
            sems.wait(scalar, "gt")
            sems.inc(scalar.copy(glr[:, 0:8], pool[4][0:1, 0:8]), "gt")

        @block.vector
        def _(vector):
            sems.wait(vector, "gt")
            sems.inc(vector.tensor_reduce(glr[:, 8:9], glr[:, 0:8], mybir.AxisListType.X, ALU.max), "gt")
            sems.wait(vector, "gt")
            sems.inc(vector.tensor_scalar(glr[:, 9:10], glr[:, 8:9], -1.0, None, ALU.mult), "gt")
            sems.wait(vector, "gt")

        @block.scalar
        def _(scalar):
            sems.wait(scalar, "gt")
            sems.inc(scalar.activation(gate[:], glr[:, 0:8], AF.Exp, bias=glr[:, 9:10], scale=1.0), "gt")

        @block.vector
        def _(vector):
            sems.wait(vector, "gt")
            sems.inc(vector.tensor_reduce(glr[:, 10:11], gate[:], mybir.AxisListType.X, ALU.add), "gt")
            sems.wait(vector, "gt")
            sems.inc(vector.reciprocal(glr[:, 11:12], glr[:, 10:11]), "gt")
            sems.wait(vector, "gt")
            sems.inc(vector.tensor_scalar(gate[:], gate[:], glr[:, 11:12], None, ALU.mult), "gt")
            sems.wait(vector, "gt")
            sems.inc(vector.max(srt[:], gate[:]), "gt")
            sems.wait(vector, "gt")
            sems.inc(vector.tensor_tensor(glr[:, 12:13], srt[:, 0:1], srt[:, 1:2], ALU.add), "gt")
            sems.wait(vector, "gt")
            sems.inc(vector.tensor_scalar(glr[:, 13:14], glr[:, 12:13], 1e-8, None, ALU.add), "gt")
            sems.wait(vector, "gt")
            sems.inc(vector.reciprocal(glr[:, 14:15], glr[:, 13:14]), "gt")
            sems.wait(vector, "gt")
            sems.inc(vector.tensor_scalar(mrow[:], gate[:], srt[:, 1:2], glr[:, 14:15],
                                          ALU.is_ge, ALU.mult), "gt")
            sems.wait(vector, "gt")
            sems.inc(vector.tensor_tensor(mrow[:], mrow[:], gate[:], ALU.mult), "gt")

        @block.tensor
        def _(tensor):
            sems.wait(tensor, "gt")
            sems.wait(tensor, "ms")
            sems.inc(tensor.matmul(pool[5][0:64, 0:8], onesr[:, 0:64], mrow[:], start=True, stop=True), "gt")

        @block.scalar
        def _(scalar):
            sems.wait(scalar, "gt")
            sems.inc(scalar.copy(me_sb[:], pool[5][0:64, 0:8]), "gt")

        for t in range(4):
            for ex in range(8):
                gidx = t * 8 + ex
                pd = gidx % 2

                @block.tensor
                def _(tensor):
                    if gidx == 0:
                        sems.wait(tensor, "cv")
                        sems.wait(tensor, "gt")
                    sems.wait(tensor, "ev")
                    for cch in range(8):
                        sems.inc(tensor.matmul(pool[pd][0:64, :], dwb[:, ex, cch, :],
                                               xb[:, cch, t * 512:(t + 1) * 512],
                                               start=(cch == 0), stop=(cch == 7)), "mm")

                @block.scalar
                def _(scalar):
                    sems.wait(scalar, "mm")
                    sems.wait(scalar, "up")
                    sems.inc(scalar.activation(gaug[0:64, ex, :], pool[pd][0:64, :], AF.Gelu,
                                               bias=db_sb[:, ex:ex + 1], scale=1.0), "ev")

                @block.vector
                def _(vector):
                    sems.wait(vector, "ev")
                    sems.inc(vector.tensor_scalar(gaug[0:64, ex, :], gaug[0:64, ex, :],
                                                  me_sb[:, ex:ex + 1], None, ALU.mult), "sc")
                    sems.inc(vector.tensor_scalar(gaug[64:65, ex, :], ones512[:],
                                                  mrow[:, ex:ex + 1], None, ALU.mult), "sc")

            for j in range(8):
                pu = 2 + (t * 8 + j) % 2

                @block.sync
                def _(sync):
                    sems.wait(sync, "ev")
                    xl.issue(sync, t * 8 + j, xstg[(t * 8 + j) % 2][:],
                             x1T[j * 128:(j + 1) * 128, t * 512:(t + 1) * 512])

                @block.tensor
                def _(tensor):
                    sems.wait(tensor, "sc")
                    sems.wait(tensor, "ev")
                    for ex in range(8):
                        sems.inc(tensor.matmul(pool[pu][:, :], ub[:, ex, j * 128:(j + 1) * 128],
                                               gaug[:, ex, :], start=(ex == 0), stop=(ex == 7)), "up")

                @block.vector
                def _(vector):
                    sems.wait(vector, "up")
                    xl.wait(vector, t * 8 + j)
                    sems.inc(vector.scalar_tensor_tensor(out_f[:, j, t * 512:(t + 1) * 512],
                                                         pool[pu][:, :], 0.0,
                                                         xstg[(t * 8 + j) % 2][:],
                                                         ALU.add, ALU.add), "ev")

        @block.vector
        def _(vector):
            sems.wait(vector, "ev")
            sems.inc(vector.tensor_copy(glr[:, 15:16], out_f[0:1, 0, 0:1]), "lnf")

        def consume(vector, e, t, Abuf, Bbuf):
            sems.inc(vector.scalar_tensor_tensor(out_f[:, e, t * 512:(t + 1) * 512],
                                                 out_f[:, e, t * 512:(t + 1) * 512], 1.0,
                                                 Abuf[:], ALU.mult, ALU.mult), "sc")
            sems.inc(vector.tensor_tensor(out_f[:, e, t * 512:(t + 1) * 512],
                                          out_f[:, e, t * 512:(t + 1) * 512], Bbuf[:], ALU.add), "lnc")

        emit_ln(nc, block, sems, ctx, out_f, consume, onesc=onesc, onesr=onesr, tag="",
                psum_pool=[pool[0], pool[1], pool[2], pool[3], pool[6], pool[7]])

        @block.gpsimd
        def _(gp):
            sems.wait(gp, "lnc")
            for e in range(8):
                sems.inc(gp.dma_start(out=x2T[e * 128:(e + 1) * 128, :], in_=out_f[:, e, :]), "st", 16)
            sems.inc(gp.dma_start(out=dbg[0:1, 0:8], in_=gate[:]), "st", 16)
            sems.inc(gp.dma_start(out=dbg[1:2, 0:8], in_=srt[:]), "st", 16)
            sems.inc(gp.dma_start(out=dbg[2:3, 0:8], in_=mrow[:]), "st", 16)
            sems.inc(gp.dma_start(out=dbg[3:4, 0:16], in_=glr[:]), "st", 16)
            sems.inc(gp.dma_start(out=dbg[4:5, 0:8], in_=me_sb[0:1, :]), "st", 16)
            sems.inc(gp.dma_start(out=dbg[5:6, 0:64], in_=gaug[0:1, 0, 0:64]), "st", 16)
            sems.wait(gp, "st")


def host_prep_l0_moe(inputs, res3):
    gwT = np.ascontiguousarray(inputs["L0_ffn_gate_w"].astype(np.float32).T)
    dwT = np.ascontiguousarray(inputs["L0_ffn_down_w"].astype(np.float32).transpose(0, 2, 1))
    db = np.ascontiguousarray(inputs["L0_ffn_down_b"].astype(np.float32).T)
    upw = inputs["L0_ffn_up_w"].astype(np.float32).transpose(0, 2, 1)
    upb = inputs["L0_ffn_up_b"].astype(np.float32)[:, None, :]
    upa = np.ascontiguousarray(np.concatenate([upw, upb], axis=1))
    per_core = []
    for c in range(NC):
        b = c // 4
        d = {"x1T": res3.results[c]["x1T"], "gwT": gwT, "dwT": dwT, "db": db, "upa": upa}
        for i in range(4):
            d[f"pooled{i}"] = res3.results[b * 4 + i]["pooled"]
        per_core.append(d)
    return per_core


def host_prep_l0_post(inputs, res1, res2):
    wout = np.ascontiguousarray(inputs["L0_attn_W_out"].astype(np.float32))
    bout = np.ascontiguousarray(inputs["L0_attn_b_out"].astype(np.float32).reshape(8, 128).T)
    x = inputs["L0_attn_x"]
    per_core = []
    for c in range(NC):
        b, s0 = c // 4, (c % 4) * TPC
        xT = np.ascontiguousarray(x[b, s0:s0 + TPC, :].T.astype(np.float32))
        per_core.append({"aT": res2.results[c]["aT"], "wout": wout, "bout": bout, "xT": xT})
    return per_core



def build_l1f_qkv(nc):
    from contextlib import ExitStack
    ctx = ExitStack()
    xT = nc.dram_tensor("xT", [D, TPC], F32, kind="ExternalInput").ap()
    xcol8 = nc.dram_tensor("xcol8", [128, 8], F32, kind="ExternalInput").ap()
    wqkv = nc.dram_tensor("wqkv", [D, 1536], F32, kind="ExternalInput").ap()
    qsr = nc.dram_tensor("qsr", [128, 2], F32, kind="ExternalInput").ap()
    tabs = nc.dram_tensor("tabs", [4, 128, TPC], F32, kind="ExternalInput").ap()
    selm = nc.dram_tensor("selm", [128, 4, 8], F32, kind="ExternalInput").ap()
    selmT = nc.dram_tensor("selmT", [8, 4, 128], F32, kind="ExternalInput").ap()
    qrT = nc.dram_tensor("qrT", [512, TPC], BF16, kind="ExternalOutput").ap()
    krT = nc.dram_tensor("krT", [512, TPC], BF16, kind="ExternalOutput").ap()
    vaug = nc.dram_tensor("vaug", [TPC, 520], BF16, kind="ExternalOutput").ap()

    with ctx:
        xsb = ctx.enter_context(nc.sbuf_tensor("xsb", [128, 8, TPC + 1], BF16))
        stg = [ctx.enter_context(nc.sbuf_tensor(f"stg{i}", [128, TPC], F32)) for i in range(2)]
        wb = ctx.enter_context(nc.sbuf_tensor("wb", [128, 8, 1536], BF16))
        xc_f = ctx.enter_context(nc.sbuf_tensor("xc_f", [128, 8], F32))
        qs_sb = ctx.enter_context(nc.sbuf_tensor("qs_sb", [128, 2], F32))
        tab_b = ctx.enter_context(nc.sbuf_tensor("tab_b", [128, 4, TPC], BF16))
        sel_sb = ctx.enter_context(nc.sbuf_tensor("sel_sb", [128, 4, 8], F32))
        selT_sb = ctx.enter_context(nc.sbuf_tensor("selT_sb", [8, 4, 128], F32))
        qkb = ctx.enter_context(nc.sbuf_tensor("qkb", [128, 8, TPC], BF16))
        q2stg = [ctx.enter_context(nc.sbuf_tensor(f"q2stg{i}", [128, 512], F32)) for i in range(2)]
        rsb = ctx.enter_context(nc.sbuf_tensor("rsb", [8, 2, TPC], F32))
        rbsb = ctx.enter_context(nc.sbuf_tensor("rbsb", [128, 512], F32))
        qnb = ctx.enter_context(nc.sbuf_tensor("qnb", [128, 512], BF16))
        r1b = ctx.enter_context(nc.sbuf_tensor("r1b", [128, 512], BF16))
        r2b = ctx.enter_context(nc.sbuf_tensor("r2b", [128, 512], BF16))
        qout = ctx.enter_context(nc.sbuf_tensor("qout", [128, 4, TPC], BF16))
        kout = ctx.enter_context(nc.sbuf_tensor("kout", [128, 4, TPC], BF16))
        va = ctx.enter_context(nc.sbuf_tensor("va", [128, 16, 520], BF16))
        ps = [ctx.enter_context(nc.psum_tensor(f"ps{i}", [128, 512], F32)) for i in range(2)]
        ss_ps = ctx.enter_context(nc.psum_tensor("ss_ps", [8, 512], F32))
        rb_ps = ctx.enter_context(nc.psum_tensor("rb_ps", [128, 512], F32))
        block = ctx.enter_context(nc.Block())
        sems = Sems(nc, ctx, ["cv", "ms", "mm", "ev", "sq", "sm", "rv", "rb", "rt", "st"])
        sg = DmaStream(nc, ctx, "sg")
        lds = DmaStream(nc, ctx, "lds", 1)

        def xs(e, lo, hi):
            off = 1 if e < 4 else 0
            return xsb[:, e, lo + off:hi + off]

        @block.gpsimd
        def _(gp):
            va_h = va[:].rearrange("p t (h c) -> p t h c", c=65)
            sems.inc(gp.memset(va_h[:, :, :, 64:65], 1.0), "ms")

        @block.sync
        def _(sync):
            for i in range(8):
                if i >= 2:
                    sync.wait_ge(sems.h["cv"], i - 1)
                sg.issue(sync, i, stg[i % 2][:], xT[i * 128:(i + 1) * 128, :])
            lds.issue(sync, 0, xc_f[:], xcol8[:])
            lds.issue(sync, 0, qs_sb[:], qsr[:])
            lds.issue(sync, 0, sel_sb[:], selm[:])
            lds.issue(sync, 0, selT_sb[:], selmT[:])
            for i in range(8):
                sync.wait_ge(sems.h["cv"], (7 + i) if i < 2 else (16 + i - 1))
                sg.issue(sync, 8 + i, stg[i % 2][:, :1536], wqkv[i * 128:(i + 1) * 128, :])
            for i in range(4):
                sync.wait_ge(sems.h["cv"], (23 + i) if i < 2 else (24 + i - 1))
                sg.issue(sync, 16 + i, stg[i % 2][:], tabs[i])

        @block.vector
        def _(vector):
            for i in range(8):
                sg.wait(vector, i)
                sems.inc(vector.tensor_copy(xsb[:, i, 1:TPC + 1], stg[i % 2][:]), "cv")
            lds.wait_all(vector)
            for e in range(8):
                sems.inc(vector.tensor_copy(xsb[:, e, 0:1], xc_f[:, e:e + 1]), "cv")
            for i in range(8):
                sg.wait(vector, 8 + i)
                sems.inc(vector.tensor_copy(wb[:, i, :], stg[i % 2][:, :1536]), "cv")
            for i in range(4):
                sg.wait(vector, 16 + i)
                sems.inc(vector.tensor_copy(tab_b[:, i, :], stg[i % 2][:]), "cv")

        for j in range(8):
            for t in range(4):
                g = j * 4 + t

                @block.tensor
                def _(tensor):
                    if g == 0:
                        sems.wait(tensor, "cv")
                    sems.wait(tensor, "ev")
                    for e in range(8):
                        sems.inc(tensor.matmul(ps[g % 2][:], wb[:, e, j * 128:(j + 1) * 128],
                                               xs(e, t * 512, (t + 1) * 512),
                                               start=(e == 0), stop=(e == 7)), "mm")

                @block.scalar
                def _(scalar):
                    sems.wait(scalar, "mm")
                    sems.inc(scalar.activation(qkb[:, j, t * 512:(t + 1) * 512], ps[g % 2][:],
                                               AF.Identity, bias=0.0, scale=1.0), "ev")

        for tt in range(16):
            g = 32 + tt

            @block.tensor
            def _(tensor):
                sems.wait(tensor, "ev")
                for e in range(8):
                    sems.inc(tensor.matmul(ps[g % 2][:], xs(e, tt * 128, (tt + 1) * 128),
                                           wb[:, e, 1024:1536],
                                           start=(e == 0), stop=(e == 7)), "mm")

            @block.scalar
            def _(scalar):
                sems.wait(scalar, "mm")
                va_h = va[:, tt, :].rearrange("p (h c) -> p h c", c=65)
                sems.inc(scalar.activation(va_h[:, :, 0:64], ps[g % 2][:],
                                           AF.Identity, bias=0.0, scale=1.0), "ev")

        for qk in range(2):
            for t in range(4):
                for c in range(4):
                    @block.scalar
                    def _(scalar):
                        sems.wait(scalar, "sm")
                        sems.inc(scalar.activation(q2stg[c % 2][:], qkb[:, qk * 4 + c, t * 512:(t + 1) * 512],
                                                   AF.Square, bias=0.0, scale=1.0), "sq")

                    @block.tensor
                    def _(tensor):
                        sems.wait(tensor, "sq")
                        if c == 0:
                            sems.wait(tensor, "rv")
                        sems.inc(tensor.matmul(ss_ps[:], sel_sb[:, c, :], q2stg[c % 2][:],
                                               start=(c == 0), stop=(c == 3)), "sm")

                @block.scalar
                def _(scalar):
                    sems.wait(scalar, "sm")
                    sems.wait(scalar, "rv")
                    sems.inc(scalar.activation(rsb[:, qk, t * 512:(t + 1) * 512], ss_ps[:],
                                               AF.Sqrt, bias=0.0, scale=1.0), "rv")

                @block.vector
                def _(vector):
                    sems.wait(vector, "rv")
                    sems.inc(vector.tensor_scalar(rsb[:, qk, t * 512:(t + 1) * 512],
                                                  rsb[:, qk, t * 512:(t + 1) * 512],
                                                  1e-12, None, ALU.max), "rv")
                    sems.wait(vector, "rv")
                    sems.inc(vector.reciprocal(rsb[:, qk, t * 512:(t + 1) * 512],
                                               rsb[:, qk, t * 512:(t + 1) * 512]), "rv")

        for qk in range(2):
            tabC = 0 if qk == 0 else 2
            outb = qout if qk == 0 else kout
            for c in range(4):
                for t in range(4):
                    @block.tensor
                    def _(tensor):
                        sems.wait(tensor, "rv")
                        sems.wait(tensor, "rb")
                        sems.inc(tensor.matmul(rb_ps[:], selT_sb[:, c, :],
                                               rsb[:, qk, t * 512:(t + 1) * 512],
                                               start=True, stop=True), "rt")

                    @block.scalar
                    def _(scalar):
                        sems.wait(scalar, "rt")
                        sems.inc(scalar.copy(rbsb[:], rb_ps[:]), "rb")

                    @block.vector
                    def _(vector):
                        sems.wait(vector, "rb")
                        sems.inc(vector.scalar_tensor_tensor(
                            qnb[:], qkb[:, qk * 4 + c, t * 512:(t + 1) * 512],
                            qs_sb[:, qk:qk + 1], rbsb[:], ALU.mult, ALU.mult), "rt")
                        sems.wait(vector, "rt")
                        sems.inc(vector.tensor_tensor(r1b[:], qnb[:],
                                                      tab_b[:, tabC, t * 512:(t + 1) * 512], ALU.mult), "rt")
                        for sblk in range(4):
                            srcp = (sblk ^ 1) * 32
                            dstp = sblk * 32
                            sems.inc(vector.tensor_copy(r2b[dstp:dstp + 32, :],
                                                        qnb[srcp:srcp + 32, :]), "rt")
                        sems.wait(vector, "rt")
                        sems.inc(vector.tensor_tensor(
                            r2b[:], r2b[:], tab_b[:, tabC + 1, t * 512:(t + 1) * 512], ALU.mult), "rt")
                        sems.wait(vector, "rt")
                        sems.inc(vector.tensor_tensor(outb[:, c, t * 512:(t + 1) * 512],
                                                      r1b[:], r2b[:], ALU.add), "rt")

        @block.gpsimd
        def _(gp):
            sems.wait(gp, "rt")
            sems.wait(gp, "ev")
            sems.wait(gp, "ms")
            for c in range(4):
                sems.inc(gp.dma_start(out=qrT[c * 128:(c + 1) * 128, :], in_=qout[:, c, :]), "st", 16)
                sems.inc(gp.dma_start(out=krT[c * 128:(c + 1) * 128, :], in_=kout[:, c, :]), "st", 16)
            sems.inc(gp.dma_start(out=vaug[:].rearrange("(t p) c -> p t c", p=128), in_=va[:]), "st", 16)
            sems.wait(gp, "st")


SERIAL_TILES = False


def build_l1f_attn(nc):
    from contextlib import ExitStack
    ctx = ExitStack()
    qrT = nc.dram_tensor("qrT", [512, TPC], BF16, kind="ExternalInput").ap()
    krT = nc.dram_tensor("krT", [512, TPC], BF16, kind="ExternalInput").ap()
    vaug = nc.dram_tensor("vaug", [TPC, 520], BF16, kind="ExternalInput").ap()
    kb4 = nc.dram_tensor("kb4", [128, 4, 512], BF16, kind="ExternalInput").ap()
    vb4 = nc.dram_tensor("vb4", [128, 4, 520], BF16, kind="ExternalInput").ap()
    mask0 = nc.dram_tensor("mask0", [4, 128, 1024], F32, kind="ExternalInput").ap()
    mask1 = nc.dram_tensor("mask1", [4, 128, 1024], F32, kind="ExternalInput").ap()
    xT = nc.dram_tensor("xT", [D, TPC], F32, kind="ExternalInput").ap()
    wo = nc.dram_tensor("wo", [512, D], F32, kind="ExternalInput").ap()
    ident = nc.dram_tensor("ident", [128, 128], BF16, kind="ExternalInput").ap()
    xpaT = nc.dram_tensor("xpaT", [D, TPC], F32, kind="ExternalOutput").ap()

    with ctx:
        qrx = ctx.enter_context(nc.sbuf_tensor("qrx", [128, 4, TPC], BF16))
        krx = ctx.enter_context(nc.sbuf_tensor("krx", [128, 4, TPC + 512], BF16))
        vx = ctx.enter_context(nc.sbuf_tensor("vx", [128, 20, 520], BF16))
        m0 = ctx.enter_context(nc.sbuf_tensor("m0", [128, 4, 1024], F32))
        m1 = ctx.enter_context(nc.sbuf_tensor("m1", [128, 4, 1024], F32))
        msk = ctx.enter_context(nc.sbuf_tensor("msk", [128, 1024], F32))
        wloc32 = ctx.enter_context(nc.sbuf_tensor("wloc32", [128, 1024], F32))
        mx = ctx.enter_context(nc.sbuf_tensor("mx", [128, 2], F32))
        wloc = ctx.enter_context(nc.sbuf_tensor("wloc", [128, 1024], BF16))
        wTt = ctx.enter_context(nc.sbuf_tensor("wTt", [128, 128], BF16))
        id_sb = ctx.enter_context(nc.sbuf_tensor("id_sb", [128, 128], BF16))
        rA = ctx.enter_context(nc.sbuf_tensor("rA", [1, 128], F32))
        bA_sb = ctx.enter_context(nc.sbuf_tensor("bA_sb", [64, 128], F32))
        ones64 = ctx.enter_context(nc.sbuf_tensor("ones64", [1, 64], F32))
        vscr = ctx.enter_context(nc.sbuf_tensor("vscr", [128, 128], F32))
        oT = ctx.enter_context(nc.sbuf_tensor("oT", [128, 4, TPC], BF16))
        wstg = [ctx.enter_context(nc.sbuf_tensor(f"wstg{i}", [128, D], F32)) for i in range(2)]
        wob = ctx.enter_context(nc.sbuf_tensor("wob", [128, 4, D], BF16))
        xstg = [ctx.enter_context(nc.sbuf_tensor(f"xstg{i}", [128, 512], F32)) for i in range(2)]
        xout = ctx.enter_context(nc.sbuf_tensor("xout", [128, 8, TPC], F32))
        ps_loc = ctx.enter_context(nc.psum_tensor("ps_loc", [128, 1024], F32))
        tp_ps = ctx.enter_context(nc.psum_tensor("tp_ps", [128, 128], F32))
        id32 = ctx.enter_context(nc.sbuf_tensor("id32", [128, 128], F32))
        o_ps = ctx.enter_context(nc.psum_tensor("o_ps", [65, 128], F32))
        b_ps = ctx.enter_context(nc.psum_tensor("b_ps", [64, 128], F32))
        ps = [ctx.enter_context(nc.psum_tensor(f"ps{i}", [128, 512], F32)) for i in range(2)]
        block = ctx.enter_context(nc.Block())
        sems = Sems(nc, ctx, ["ld", "cv", "ms", "mm", "mk", "ex", "tp", "tv", "av",
                              "rc", "tb", "bc", "no", "pm", "pe", "xev"])
        wl = DmaStream(nc, ctx, "wl")
        xl = DmaStream(nc, ctx, "xl")
        stp = DmaStream(nc, ctx, "stp", 1)

        @block.gpsimd
        def _(gp):
            sems.inc(gp.memset(ones64[:], 1.0), "ms")

        @block.sync
        def _(sync):
            for c in range(4):
                sems.inc(sync.dma_start(out=qrx[:, c, :], in_=qrT[c * 128:(c + 1) * 128, :]), "ld", 16)
                sems.inc(sync.dma_start(out=krx[:, c, 512:], in_=krT[c * 128:(c + 1) * 128, :]), "ld", 16)
            sems.inc(sync.dma_start(out=krx[:, :, 0:512], in_=kb4[:]), "ld", 16)
            sems.inc(sync.dma_start(out=vx[:, 0:4, :], in_=vb4[:]), "ld", 16)
            va_src = vaug[:].rearrange("(kt p) c -> p kt c", p=128)
            for i in range(2):
                sems.inc(sync.dma_start(out=vx[:, 4 + i * 8:4 + (i + 1) * 8, :],
                                        in_=va_src[:, i * 8:(i + 1) * 8, :]), "ld", 16)
            for v in range(4):
                sems.inc(sync.dma_start(out=m0[:, v, :], in_=mask0[v]), "ld", 16)
                sems.inc(sync.dma_start(out=m1[:, v, :], in_=mask1[v]), "ld", 16)
            sems.inc(sync.dma_start(out=id_sb[:], in_=ident[:]), "ld", 16)
            for c in range(4):
                if c >= 2:
                    sync.wait_ge(sems.h["cv"], c)
                wl.issue(sync, c, wstg[c % 2][:], wo[c * 128:(c + 1) * 128, :])

        @block.vector
        def _(vector):
            vector.wait_ge(sems.h["ld"], 16 * 21)
            sems.inc(vector.tensor_copy(id32[:], id_sb[:]), "cv")
            for c in range(4):
                wl.wait(vector, c)
                sems.inc(vector.tensor_copy(wob[:, c, :], wstg[c % 2][:]), "cv")

        for h in range(8):
            hb = (h % 2) * 64
            ch = h // 2
            for w in range(4):
                mbuf = m1 if w == 0 else m0
                for qt in range(4):
                    q0 = w * 512 + qt * 128

                    @block.tensor
                    def _(tensor):
                        if h == 0 and w == 0 and qt == 0:
                            tensor.wait_ge(sems.h["ld"], 16 * 21)
                        if SERIAL_TILES:
                            sems.wait(tensor, "no")
                        sems.wait(tensor, "pm")
                        sems.inc(tensor.matmul(ps_loc[:, 0:512], qrx[hb:hb + 64, ch, q0:q0 + 128],
                                               krx[hb:hb + 64, ch, w * 512:w * 512 + 512],
                                               start=True, stop=True,
                                               tile_position=(hb, 0)), "mm")
                        sems.inc(tensor.matmul(ps_loc[:, 512:1024], qrx[hb:hb + 64, ch, q0:q0 + 128],
                                               krx[hb:hb + 64, ch, w * 512 + 512:w * 512 + 1024],
                                               start=True, stop=True,
                                               tile_position=(hb, 0)), "mm")

                    @block.vector
                    def _(vector):
                        sems.wait(vector, "mm")
                        sems.wait(vector, "pe")
                        sems.inc(vector.scalar_tensor_tensor(msk[:], ps_loc[:], 1.0,
                                                             mbuf[:, qt, :], ALU.mult, ALU.add), "pm")
                        sems.wait(vector, "pm")
                        sems.inc(vector.tensor_reduce(mx[:, 0:1], msk[:],
                                                      mybir.AxisListType.X, ALU.max), "pm")
                        sems.wait(vector, "pm")
                        sems.inc(vector.tensor_scalar(mx[:, 1:2], mx[:, 0:1], -1.0, None, ALU.mult), "pm")
                        sems.wait(vector, "pm")
                        sems.inc(vector.tensor_copy(vscr[:, 0:1], mx[:, 1:2]), "pm")

                    @block.scalar
                    def _(scalar):
                        sems.wait(scalar, "pm")
                        sems.wait(scalar, "tp")
                        sems.inc(scalar.activation(wloc32[:], msk[:], AF.Exp,
                                                   bias=mx[:, 1:2], scale=1.0), "pe")

                    for kt in range(8):
                        @block.tensor
                        def _(tensor):
                            sems.wait(tensor, "pe")
                            sems.wait(tensor, "tv")
                            if kt == 0:
                                sems.wait(tensor, "no")
                                if h == 0 and w == 0 and qt == 0:
                                    tensor.wait_ge(sems.h["cv"], 1)
                            sems.inc(tensor.transpose(tp_ps[:], wloc32[:, kt * 128:(kt + 1) * 128],
                                                      id32[:]), "tp")

                        @block.scalar
                        def _(scalar):
                            sems.wait(scalar, "tp")
                            sems.wait(scalar, "av")
                            sems.inc(scalar.copy(wTt[:], tp_ps[:]), "tv")

                        @block.tensor
                        def _(tensor):
                            sems.wait(tensor, "tv")
                            sems.inc(tensor.matmul(o_ps[:], vx[:, w * 4 + kt, h * 65:h * 65 + 65],
                                                   wTt[:], start=(kt == 0), stop=(kt == 7)), "av")

                    @block.vector
                    def _(vector):
                        sems.wait(vector, "av")
                        sems.inc(vector.reciprocal(rA[:], o_ps[64:65, :]), "rc")
                        sems.wait(vector, "rc")
                        sems.inc(vector.tensor_copy(vscr[0:1, :128], rA[:]), "rc")

                    @block.tensor
                    def _(tensor):
                        sems.wait(tensor, "rc")
                        sems.wait(tensor, "ms")
                        sems.wait(tensor, "bc")
                        sems.inc(tensor.matmul(b_ps[:], ones64[:], rA[:], start=True, stop=True), "tb")

                    @block.scalar
                    def _(scalar):
                        sems.wait(scalar, "tb")
                        sems.wait(scalar, "no")
                        sems.inc(scalar.copy(bA_sb[:], b_ps[:]), "bc")

                    @block.vector
                    def _(vector):
                        sems.wait(vector, "bc")
                        sems.inc(vector.scalar_tensor_tensor(
                            oT[hb:hb + 64, ch, q0:q0 + 128], o_ps[0:64, :], 1.0,
                            bA_sb[:], ALU.mult, ALU.mult), "no")

        for j in range(8):
            for t in range(4):
                g = j * 4 + t

                @block.sync
                def _(sync):
                    sems.wait(sync, "xev", max(0, g - 1))
                    xl.issue(sync, g, xstg[g % 2][:],
                             xT[j * 128:(j + 1) * 128, t * 512:(t + 1) * 512])

                @block.tensor
                def _(tensor):
                    if g == 0:
                        sems.wait(tensor, "no")
                        sems.wait(tensor, "cv")
                    sems.wait(tensor, "xev", max(0, g - 1))
                    for c in range(4):
                        sems.inc(tensor.matmul(ps[g % 2][:], wob[:, c, j * 128:(j + 1) * 128],
                                               oT[:, c, t * 512:(t + 1) * 512],
                                               start=(c == 0), stop=(c == 3)), "mm")

                @block.vector
                def _(vector):
                    sems.wait(vector, "mm")
                    xl.wait(vector, g)
                    sems.inc(vector.scalar_tensor_tensor(xout[:, j, t * 512:(t + 1) * 512],
                                                         ps[g % 2][:], 0.0, xstg[g % 2][:],
                                                         ALU.add, ALU.add), "xev")

        @block.gpsimd
        def _(gp):
            sems.wait(gp, "xev")
            for e in range(8):
                stp.issue(gp, 0, xpaT[e * 128:(e + 1) * 128, :], xout[:, e, :])
            stp.wait_all(gp)


def build_l1f_mlp(nc):
    from contextlib import ExitStack
    ctx = ExitStack()
    HP = 2816
    xT = nc.dram_tensor("xT", [D, TPC], F32, kind="ExternalInput").ap()
    xcol8 = nc.dram_tensor("xcol8", [128, 8], F32, kind="ExternalInput").ap()
    lgb = nc.dram_tensor("lgb", [128, 16], F32, kind="ExternalInput").ap()
    w1a = nc.dram_tensor("w1a", [D, HP], F32, kind="ExternalInput").ap()
    w1g = nc.dram_tensor("w1g", [D, HP], F32, kind="ExternalInput").ap()
    b1ag = nc.dram_tensor("b1ag", [128, 44], F32, kind="ExternalInput").ap()
    w2p = nc.dram_tensor("w2p", [HP, D], F32, kind="ExternalInput").ap()
    b2 = nc.dram_tensor("b2", [128, 8], F32, kind="ExternalInput").ap()
    xoutT = nc.dram_tensor("xoutT", [D, TPC], F32, kind="ExternalOutput").ap()

    NJ = 22
    with ctx:
        xsb = ctx.enter_context(nc.sbuf_tensor("xsb", [128, 8, TPC + 1], BF16))
        stg = [ctx.enter_context(nc.sbuf_tensor(f"stg{i}", [128, TPC], F32)) for i in range(2)]
        xc_f = ctx.enter_context(nc.sbuf_tensor("xc_f", [128, 8], F32))
        lgb_sb = ctx.enter_context(nc.sbuf_tensor("lgb_sb", [128, 16], F32))
        b1_sb = ctx.enter_context(nc.sbuf_tensor("b1_sb", [128, 44], F32))
        b2_sb = ctx.enter_context(nc.sbuf_tensor("b2_sb", [128, 8], F32))
        hn = ctx.enter_context(nc.sbuf_tensor("hn", [128, 8, 512], BF16))
        w1stg = [ctx.enter_context(nc.sbuf_tensor(f"w1stg{i}", [128, 8, 128], F32)) for i in range(2)]
        w1b = [ctx.enter_context(nc.sbuf_tensor(f"w1b{i}", [128, 8, 128], BF16)) for i in range(2)]
        aab = ctx.enter_context(nc.sbuf_tensor("aab", [128, NJ, 512], BF16))
        ggb = ctx.enter_context(nc.sbuf_tensor("ggb", [128, NJ, 512], BF16))
        w2stg = [ctx.enter_context(nc.sbuf_tensor(f"w2stg{i}", [128, D], F32)) for i in range(2)]
        w2b = [ctx.enter_context(nc.sbuf_tensor(f"w2b{i}", [128, D], BF16)) for i in range(2)]
        xstg = [ctx.enter_context(nc.sbuf_tensor(f"xstg{i}", [128, 512], F32)) for i in range(2)]
        onesc_b = ctx.enter_context(nc.sbuf_tensor("onesc_b", [128, 1], BF16))
        onesr = ctx.enter_context(nc.sbuf_tensor("onesr", [1, 128], F32))
        xo = ctx.enter_context(nc.sbuf_tensor("xo", [128, 8, 512], F32))
        pool = [ctx.enter_context(nc.psum_tensor(f"pp{i}", [128, 512], F32)) for i in range(8)]
        block = ctx.enter_context(nc.Block())
        names = ["cv", "ms", "w1c", "mm", "ev", "gg", "w2c",
                 "up", "uev", "st",
                 "lnf", "lnm", "lnsq", "lnv", "lnst", "lnr", "lnb", "lnbc", "lnc"]
        sems = Sems(nc, ctx, names)
        sx = DmaStream(nc, ctx, "sx")
        lds = DmaStream(nc, ctx, "lds", 1)
        w1s = DmaStream(nc, ctx, "w1s")
        w2s = DmaStream(nc, ctx, "w2s")
        xl = DmaStream(nc, ctx, "xl")

        def xs(e, lo, hi):
            off = 1 if e < 4 else 0
            return xsb[:, e, lo + off:hi + off]

        @block.gpsimd
        def _(gp):
            sems.inc(gp.memset(onesc_b[:], 1.0), "ms")
            sems.inc(gp.memset(onesr[:], 1.0), "ms")

        @block.sync
        def _(sync):
            for i in range(8):
                if i >= 2:
                    sync.wait_ge(sems.h["cv"], i - 1)
                sx.issue(sync, i, stg[i % 2][:], xT[i * 128:(i + 1) * 128, :])
            lds.issue(sync, 0, xc_f[:], xcol8[:])
            lds.issue(sync, 0, lgb_sb[:], lgb[:])
            lds.issue(sync, 0, b1_sb[:], b1ag[:])
            lds.issue(sync, 0, b2_sb[:], b2[:])

        @block.vector
        def _(vector):
            for i in range(8):
                sx.wait(vector, i)
                sems.inc(vector.tensor_copy(xsb[:, i, 1:TPC + 1], stg[i % 2][:]), "cv")
            lds.wait_all(vector)
            for e in range(8):
                sems.inc(vector.tensor_copy(xsb[:, e, 0:1], xc_f[:, e:e + 1]), "cv")

        @block.vector
        def _(vector):
            sems.wait(vector, "cv")
            sems.inc(vector.tensor_copy(xo[:, 0, 0:1], xsb[:, 0, 0:1]), "lnf")

        ln_bufs = dict(
            y2stg=[ctx.enter_context(nc.sbuf_tensor(f"y2s{i}", [128, 512], BF16)) for i in range(2)],
            rowA=ctx.enter_context(nc.sbuf_tensor("rowA_", [1, 512], F32)),
            rowB=ctx.enter_context(nc.sbuf_tensor("rowB_", [1, 512], F32)),
            rtmp=ctx.enter_context(nc.sbuf_tensor("rtmp_", [1, 512], F32)),
            rowS=ctx.enter_context(nc.sbuf_tensor("rowS_", [1, 512], F32)),
            Ab=[ctx.enter_context(nc.sbuf_tensor(f"Ab_{i}", [128, 512], F32)) for i in range(2)],
            Bb=[ctx.enter_context(nc.sbuf_tensor(f"Bb_{i}", [128, 512], F32)) for i in range(2)],
        )
        for t in range(4):
            def consume(vector, e, tt_, Abuf, Bbuf, t=t):
                if e == 0:
                    sems.wait(vector, "mm")
                sems.inc(vector.scalar_tensor_tensor(hn[:, e, :], xs(e, t * 512, (t + 1) * 512),
                                                     1.0, Abuf[:], ALU.mult, ALU.mult), "gg")
                sems.wait(vector, "gg")
                sems.inc(vector.tensor_tensor(hn[:, e, :], hn[:, e, :], Bbuf[:], ALU.add), "gg")
                sems.wait(vector, "gg")
                sems.inc(vector.tensor_scalar(hn[:, e, :], hn[:, e, :], lgb_sb[:, e:e + 1],
                                              lgb_sb[:, 8 + e:8 + e + 1], ALU.mult, ALU.add), "lnc")

            emit_ln(nc, block, sems, ctx, None, consume,
                    nchunk=8, ntc=1, tw=512, onesc=onesc_b, onesr=onesr, tag="",
                    psum_pool=[pool[0], pool[1], pool[2], pool[3], pool[4], pool[5]],
                    src_cb=lambda e, tt_, t=t: xs(e, t * 512, (t + 1) * 512), y2_dtype=BF16,
                    uniq=f"t{t}", pre_wait=("uev", "ms"), bufs=ln_bufs)

            for part in range(2):
                wsrc = w1a if part == 0 else w1g
                outb = aab if part == 0 else ggb
                for j in range(NJ):
                    gi = (part * NJ + j)
                    pj = gi % 2

                    gidx_w1 = t * 44 + part * NJ + j

                    @block.sync
                    def _(sync):
                        sems.wait(sync, "w1c", max(0, sems.n["w1c"] - 1))
                        w1s.issue(sync, gidx_w1, w1stg[gi % 2][:],
                                  wsrc[:, j * 128:(j + 1) * 128].rearrange("(e p) j -> p e j", p=128))

                    @block.vector
                    def _(vector):
                        w1s.wait(vector, gidx_w1)
                        sems.wait(vector, "mm", max(0, (sems.n["mm"]) - 8))
                        sems.inc(vector.tensor_copy(w1b[gi % 2][:], w1stg[gi % 2][:]), "w1c")

                    @block.tensor
                    def _(tensor):
                        sems.wait(tensor, "w1c")
                        sems.wait(tensor, "lnc")
                        sems.wait(tensor, "ev", max(0, sems.n["ev"] - 1))
                        for e in range(8):
                            sems.inc(tensor.matmul(pool[6 + pj][:], w1b[gi % 2][:, e, :], hn[:, e, :],
                                                   start=(e == 0), stop=(e == 7)), "mm")

                    @block.scalar
                    def _(scalar):
                        sems.wait(scalar, "mm")
                        fn = AF.Identity if part == 0 else AF.Gelu
                        sems.inc(scalar.activation(outb[:, j, :], pool[6 + pj][:], fn,
                                                   bias=b1_sb[:, part * NJ + j:part * NJ + j + 1],
                                                   scale=1.0), "ev")

            @block.vector
            def _(vector):
                sems.wait(vector, "ev")
                for j in range(NJ):
                    sems.inc(vector.tensor_tensor(aab[:, j, :], aab[:, j, :], ggb[:, j, :],
                                                  ALU.mult), "gg")

            for c in range(NJ):
                gidx_w2 = t * NJ + c

                @block.sync
                def _(sync):
                    sems.wait(sync, "w2c", max(0, sems.n["w2c"] - 1))
                    w2s.issue(sync, gidx_w2, w2stg[c % 2][:], w2p[c * 128:(c + 1) * 128, :])

                @block.vector
                def _(vector):
                    w2s.wait(vector, gidx_w2)
                    sems.wait(vector, "up", max(0, sems.n["up"] - 8))
                    sems.inc(vector.tensor_copy(w2b[c % 2][:], w2stg[c % 2][:]), "w2c")

                @block.tensor
                def _(tensor):
                    sems.wait(tensor, "w2c")
                    if c == 0:
                        sems.wait(tensor, "gg")
                        sems.wait(tensor, "uev")
                        sems.wait(tensor, "lnv")
                        sems.wait(tensor, "lnbc")
                    for j in range(8):
                        sems.inc(tensor.matmul(pool[j][:], w2b[c % 2][:, j * 128:(j + 1) * 128],
                                               aab[:, c, :], start=(c == 0), stop=(c == NJ - 1)), "up")

            for j in range(8):
                gidx_x = t * 8 + j

                @block.sync
                def _(sync):
                    sems.wait(sync, "uev", max(0, sems.n["uev"] - 1))
                    xl.issue(sync, gidx_x, xstg[j % 2][:],
                             xT[j * 128:(j + 1) * 128, t * 512:(t + 1) * 512])

                @block.vector
                def _(vector):
                    sems.wait(vector, "up")
                    xl.wait(vector, gidx_x)
                    if j == 0:
                        sems.wait(vector, "st")
                    sems.inc(vector.scalar_tensor_tensor(xo[:, j, :], pool[j][:],
                                                         b2_sb[:, j:j + 1], xstg[j % 2][:],
                                                         ALU.add, ALU.add), "uev")

            @block.gpsimd
            def _(gp):
                sems.wait(gp, "uev")
                for j in range(8):
                    sems.inc(gp.dma_start(out=xoutT[j * 128:(j + 1) * 128, t * 512:(t + 1) * 512],
                                          in_=xo[:, j, :]), "st", 16)
                gp.wait_ge(sems.h["st"], sems.n["st"])

        @block.gpsimd
        def _(gp):
            gp.wait_ge(sems.h["st"], sems.n["st"])


def build_final(nc):
    from contextlib import ExitStack
    ctx = ExitStack()
    x3T = nc.dram_tensor("x3T", [D, TPC], F32, kind="ExternalInput").ap()
    f1T = nc.dram_tensor("f1T", [D, TPC], F32, kind="ExternalInput").ap()
    x4T = nc.dram_tensor("x4T", [D, TPC], F32, kind="ExternalOutput").ap()
    with ctx:
        yf = ctx.enter_context(nc.sbuf_tensor("yf", [128, 8, TPC], F32))
        fstg = [ctx.enter_context(nc.sbuf_tensor(f"fstg{i}", [128, TPC], F32)) for i in range(2)]
        onesc = ctx.enter_context(nc.sbuf_tensor("onesc", [128, 1], F32))
        onesr = ctx.enter_context(nc.sbuf_tensor("onesr", [1, 128], F32))
        block = ctx.enter_context(nc.Block())
        names = ["ms", "ad", "st",
                 "lnf", "lnm", "lnsq", "lnv", "lnst", "lnr", "lnb", "lnbc", "lnc"]
        sems = Sems(nc, ctx, names)
        ld = DmaStream(nc, ctx, "ld", 1)
        fl = DmaStream(nc, ctx, "fl")

        @block.gpsimd
        def _(gp):
            sems.inc(gp.memset(onesc[:], 1.0), "ms")
            sems.inc(gp.memset(onesr[:], 1.0), "ms")

        @block.sync
        def _(sync):
            for e in range(8):
                ld.issue(sync, 0, yf[:, e, :], x3T[e * 128:(e + 1) * 128, :])
            for e in range(8):
                if e >= 2:
                    sync.wait_ge(sems.h["ad"], e - 1)
                fl.issue(sync, e, fstg[e % 2][:], f1T[e * 128:(e + 1) * 128, :])

        @block.vector
        def _(vector):
            ld.wait_all(vector)
            for e in range(8):
                fl.wait(vector, e)
                sems.inc(vector.tensor_tensor(yf[:, e, :], yf[:, e, :], fstg[e % 2][:], ALU.add), "ad")
            sems.wait(vector, "ad")
            sems.inc(vector.tensor_copy(fstg[0][:, 0:1], yf[:, 0, 0:1]), "lnf")

        def consume(vector, e, t, Abuf, Bbuf):
            sems.inc(vector.scalar_tensor_tensor(yf[:, e, t * 512:(t + 1) * 512],
                                                 yf[:, e, t * 512:(t + 1) * 512], 1.0,
                                                 Abuf[:], ALU.mult, ALU.mult), "ad")
            sems.inc(vector.tensor_tensor(yf[:, e, t * 512:(t + 1) * 512],
                                          yf[:, e, t * 512:(t + 1) * 512], Bbuf[:], ALU.add), "lnc")

        emit_ln(nc, block, sems, ctx, yf, consume, onesc=onesc, onesr=onesr)

        @block.gpsimd
        def _(gp):
            sems.wait(gp, "lnc")
            for e in range(8):
                sems.inc(gp.dma_start(out=x4T[e * 128:(e + 1) * 128, :], in_=yf[:, e, :]), "st", 16)
            sems.wait(gp, "st")


def rotary_tables(s0):
    DIM_HEAD, WINDOW, QK = 64, 512, 8.0
    half = np.arange(0, DIM_HEAD, 2, dtype=np.float32)
    inv_freq = 1.0 / 10000.0 ** (half / DIM_HEAD)
    t = np.arange(s0, s0 + TPC, dtype=np.float32)
    fr = t[:, None] * inv_freq[None, :]
    freqs = np.concatenate([fr, fr], axis=-1)
    base = (half + 0.4 * DIM_HEAD) / (1.4 * DIM_HEAD)
    power = (t - S // 2) / float(WINDOW // 2)
    scale = (base[None, :] ** power[:, None]).astype(np.float32)
    scale = np.concatenate([scale, scale], axis=-1)
    cos, sin = np.cos(freqs), np.sin(freqs)
    sgn = np.concatenate([-np.ones(32, np.float32), np.ones(32, np.float32)])
    Cq = (cos * scale * QK).T
    Sq = (sin * scale * QK * sgn[:, None].T).T
    Ck = (cos / scale).T
    Sk = (sin / scale * sgn[:, None].T).T
    out = []
    for tab in (Cq, Sq, Ck, Sk):
        out.append(np.ascontiguousarray(np.concatenate([tab, tab], axis=0).astype(np.float32)))
    return out


def local_masks(first):
    m = np.zeros((4, 128, 1024), np.float32)
    for qt in range(4):
        i = np.arange(128)[:, None] + qt * 128 + 512
        j = np.arange(1024)[None, :]
        bad = (j > i) | (j <= i - 512)
        if first:
            bad = bad | (j < 512)
        m[qt][bad] = -1e30
    return m


def host_prep_l1f_qkv(inputs, li, xT_by_core):
    wqkv = np.ascontiguousarray(inputs["L1_ffn_wqkv"][li].astype(np.float32))
    qs = inputs["L1_ffn_q_scale"][li].astype(np.float32)
    ks = inputs["L1_ffn_k_scale"][li].astype(np.float32)
    qsr = np.ascontiguousarray(np.stack([np.tile(qs, 2), np.tile(ks, 2)], axis=1))
    selmT = np.zeros((8, 4, 128), np.float32)
    for c in range(4):
        selmT[2 * c, c, 0:64] = 1.0
        selmT[2 * c + 1, c, 64:128] = 1.0
    selm = np.ascontiguousarray(selmT.transpose(2, 1, 0))
    per_core = []
    for c in range(NC):
        tabs = np.stack(rotary_tables((c % 4) * TPC), axis=0)
        if c % 4 == 0:
            xcol = np.zeros((128, 8), np.float32)
        else:
            xcol = np.ascontiguousarray(xT_by_core[c - 1][:, -1].reshape(8, 128).T)
        per_core.append({"xT": xT_by_core[c], "xcol8": xcol, "wqkv": wqkv,
                         "qsr": qsr, "tabs": tabs, "selm": selm, "selmT": selmT})
    return per_core


def host_prep_l1f_attn(inputs, li, res_qkv, xT_by_core):
    wo = np.ascontiguousarray(inputs["L1_ffn_wo"][li].astype(np.float32))
    ident = np.eye(128, dtype=BF)
    m0 = local_masks(False)
    m1n = local_masks(False)
    m1f = local_masks(True)
    per_core = []
    for c in range(NC):
        if c % 4 == 0:
            kb4 = np.zeros((128, 4, 512), BF)
            vb4 = np.zeros((128, 4, 520), BF)
            m1 = m1f
        else:
            krN = res_qkv.results[c - 1]["krT"]
            kb4 = np.ascontiguousarray(krN[:, -512:].reshape(4, 128, 512).transpose(1, 0, 2))
            vaN = res_qkv.results[c - 1]["vaug"]
            vb4 = np.ascontiguousarray(vaN[-512:, :].reshape(4, 128, 520).transpose(1, 0, 2))
            m1 = m1n
        per_core.append({"qrT": res_qkv.results[c]["qrT"], "krT": res_qkv.results[c]["krT"],
                         "vaug": res_qkv.results[c]["vaug"], "kb4": kb4, "vb4": vb4,
                         "mask0": m0, "mask1": m1, "xT": xT_by_core[c], "wo": wo, "ident": ident})
    return per_core


def host_prep_l1f_mlp(inputs, li, xpa_by_core):
    HP = 2816
    HID = 2730
    w1 = inputs["L1_ffn_w1"][li].astype(np.float32)
    b1 = inputs["L1_ffn_b1"][li].astype(np.float32)
    w2 = inputs["L1_ffn_w2"][li].astype(np.float32)
    b2 = inputs["L1_ffn_b2"][li].astype(np.float32)
    lg = inputs["L1_ffn_ln_g"][li].astype(np.float32)
    lb = inputs["L1_ffn_ln_b"][li].astype(np.float32)
    w1a = np.zeros((D, HP), np.float32); w1a[:, :HID] = w1[:, :HID]
    w1g = np.zeros((D, HP), np.float32); w1g[:, :HID] = w1[:, HID:]
    b1a = np.zeros(HP, np.float32); b1a[:HID] = b1[:HID]
    b1g = np.zeros(HP, np.float32); b1g[:HID] = b1[HID:]
    b1ag = np.ascontiguousarray(np.concatenate(
        [b1a.reshape(22, 128).T, b1g.reshape(22, 128).T], axis=1))
    w2p = np.zeros((HP, D), np.float32); w2p[:HID] = w2
    lgb = np.ascontiguousarray(np.concatenate(
        [lg.reshape(8, 128).T, lb.reshape(8, 128).T], axis=1))
    b2c = np.ascontiguousarray(b2.reshape(8, 128).T)
    per_core = []
    for c in range(NC):
        if c % 4 == 0:
            xcol = np.zeros((128, 8), np.float32)
        else:
            xcol = np.ascontiguousarray(xpa_by_core[c - 1][:, -1].reshape(8, 128).T)
        per_core.append({"xT": xpa_by_core[c], "xcol8": xcol, "lgb": lgb,
                         "w1a": w1a, "w1g": w1g, "b1ag": b1ag, "w2p": w2p, "b2": b2c})
    return per_core


def build_l1_post(nc):
    build_l0_post(nc, with_pooled=False)


def host_prep_l1_post(inputs, res4, res6):
    wout = np.ascontiguousarray(inputs["L1_attn_Wo"].astype(np.float32).T)
    bout = np.ascontiguousarray(inputs["L1_attn_bo"].astype(np.float32).reshape(8, 128).T)
    per_core = []
    for c in range(NC):
        per_core.append({"aT": res6.results[c]["aT"], "wout": wout, "bout": bout,
                         "xT": res4.results[c]["x2T"]})
    return per_core


def host_prep_l0_qkv(inputs):
    x = inputs["L0_attn_x"]
    wqkv = np.ascontiguousarray(inputs["L0_attn_W_qkv"].astype(np.float32))
    bqkv = inputs["L0_attn_b_qkv"].astype(np.float32)
    bq16 = np.ascontiguousarray(bqkv[:2048].reshape(16, 128).T)
    bv_row = np.ascontiguousarray(bqkv[2048:].reshape(1, 1024))
    per_core = []
    for c in range(NC):
        b, s0 = c // 4, (c % 4) * TPC
        xT = np.ascontiguousarray(x[b, s0:s0 + TPC, :].T.astype(np.float32))
        per_core.append({"xT": xT, "wqkv": wqkv, "bqkv": bq16, "bv_row": bv_row})
    return per_core


def kernel(**inputs):
    r1 = run_stage(build_l0_qkv, host_prep_l0_qkv(inputs))
    r2 = run_stage(build_l0_attn, host_prep_l0_attn(inputs, r1))
    r3 = run_stage(build_l0_post, host_prep_l0_post(inputs, r1, r2))
    r4 = run_stage(build_l0_moe, host_prep_l0_moe(inputs, r3))
    r5 = run_stage(build_l1_qkv, host_prep_l1_qkv(inputs, r4))
    r6 = run_stage(build_l1_attn, host_prep_l1_attn(inputs, r5))
    r7 = run_stage(build_l1_post, host_prep_l1_post(inputs, r4, r6))
    xT = [r7.results[c]["x1T"] for c in range(NC)]
    for li in range(2):
        rq = run_stage(build_l1f_qkv, host_prep_l1f_qkv(inputs, li, xT))
        ra = run_stage(build_l1f_attn, host_prep_l1f_attn(inputs, li, rq, xT))
        xpa = [ra.results[c]["xpaT"] for c in range(NC)]
        rm = run_stage(build_l1f_mlp, host_prep_l1f_mlp(inputs, li, xpa))
        xT = [rm.results[c]["xoutT"] for c in range(NC)]
    x3 = [r7.results[c]["x1T"] for c in range(NC)]
    r9 = run_stage(build_final, [{"x3T": x3[c], "f1T": xT[c]} for c in range(NC)])
    out = np.empty((B, S, D), np.float32)
    for c in range(NC):
        b, s0 = c // 4, (c % 4) * TPC
        out[b, s0:s0 + TPC, :] = r9.results[c]["x4T"].T
    return out


# version 02
# speedup vs baseline: 1.5956x; 1.5956x over previous
import numpy as np
import ml_dtypes
import concourse.bass as bass
import concourse.mybir as mybir
from concourse.bass_utils import run_bass_kernel_spmd

F32 = mybir.dt.float32
BF16 = mybir.dt.bfloat16
AF = mybir.ActivationFunctionType
ALU = mybir.AluOpType

NC = 8
B = 2
S = 8192
D = 1024
TPC = 2048
H0 = 16
DH = 64

BF = ml_dtypes.bfloat16



class Sems:

    def __init__(self, nc, ctx, names):
        self.h = {}
        self.n = {}
        for nm in names:
            self.h[nm] = ctx.enter_context(nc.semaphore(nm))
            self.n[nm] = 0

    def inc(self, instr, nm, k=1):
        instr.then_inc(self.h[nm], k)
        self.n[nm] += k

    def wait(self, eng, nm, val=None):
        eng.wait_ge(self.h[nm], self.n[nm] if val is None else val)

    def wait_all(self, eng):
        for nm in self.h:
            eng.wait_ge(self.h[nm], self.n[nm])


PARANOID = False


class DmaStream:

    def __init__(self, nc, ctx, name, nbuf=2):
        self.s = [ctx.enter_context(nc.semaphore(f"{name}{i}")) for i in range(nbuf)]
        self.cnt = [0] * nbuf
        self.nbuf = nbuf

    def issue(self, qeng, idx, out, in_):
        p = idx % self.nbuf
        qeng.dma_start(out=out, in_=in_).then_inc(self.s[p], 16)
        self.cnt[p] += 16

    def wait(self, eng, idx):
        p = idx % self.nbuf
        eng.wait_ge(self.s[p], 16 * (idx // self.nbuf + 1))

    def wait_all(self, eng):
        for p in range(self.nbuf):
            if self.cnt[p]:
                eng.wait_ge(self.s[p], self.cnt[p])


def pwait(sems, eng):
    if PARANOID:
        sems.wait_all(eng)


def run_stage(build, per_core_inputs, trace=False):
    nc = bass.Bass("TRN2", target_bir_lowering=False)
    build(nc)
    res = run_bass_kernel_spmd(nc, per_core_inputs, core_ids=list(range(NC)), trace=trace)
    return res



def build_l0_qkv(nc, q_scale=1.0):
    from contextlib import ExitStack
    ctx = ExitStack()
    xT = nc.dram_tensor("xT", [D, TPC], F32, kind="ExternalInput").ap()
    wqkv = nc.dram_tensor("wqkv", [D, 3 * D], F32, kind="ExternalInput").ap()
    bqkv = nc.dram_tensor("bqkv", [128, 16], F32, kind="ExternalInput").ap()
    bv_row = nc.dram_tensor("bv_row", [1, D], F32, kind="ExternalInput").ap()
    qT = nc.dram_tensor("qT", [D, TPC], BF16, kind="ExternalOutput").ap()
    kT = nc.dram_tensor("kT", [D, TPC], BF16, kind="ExternalOutput").ap()
    vaug = nc.dram_tensor("vaug", [TPC, 16 * 65], BF16, kind="ExternalOutput").ap()

    with ctx:
        xb = ctx.enter_context(nc.sbuf_tensor("xb", [128, 8, TPC], BF16))
        wb = ctx.enter_context(nc.sbuf_tensor("wb", [128, 8, 3 * D], BF16))
        stg = [ctx.enter_context(nc.sbuf_tensor(f"stg{i}", [128, 3 * D], F32)) for i in range(2)]
        bq_sb = ctx.enter_context(nc.sbuf_tensor("bq_sb", [128, 16], F32))
        bvr_f = ctx.enter_context(nc.sbuf_tensor("bvr_f", [1, D], F32))
        bvr_b = ctx.enter_context(nc.sbuf_tensor("bvr_b", [1, D], BF16))
        ones1 = ctx.enter_context(nc.sbuf_tensor("ones1", [1, 128], BF16))
        qk_st = [ctx.enter_context(nc.sbuf_tensor(f"qk_st{i}", [128, TPC], BF16)) for i in range(2)]
        va = ctx.enter_context(nc.sbuf_tensor("va", [128, 16, 1040], BF16))
        ps = [ctx.enter_context(nc.psum_tensor(f"ps{i}", [128, 512], F32)) for i in range(2)]
        block = ctx.enter_context(nc.Block())
        sems = Sems(nc, ctx, ["cv", "ms", "mm", "ev", "vst"])
        ld = DmaStream(nc, ctx, "ld")
        lds = DmaStream(nc, ctx, "lds", 1)
        stp = DmaStream(nc, ctx, "stp")

        NCHUNK = 16

        @block.sync
        def _(sync):
            for i in range(NCHUNK):
                if i >= 2:
                    sync.wait_ge(sems.h["cv"], i - 1)
                if i % 2 == 0:
                    e = i // 2
                    ld.issue(sync, i, stg[i % 2][:, :TPC], xT[e * 128:(e + 1) * 128, :])
                else:
                    e = i // 2
                    ld.issue(sync, i, stg[i % 2][:, :], wqkv[e * 128:(e + 1) * 128, :])
            lds.issue(sync, 0, bq_sb[:], bqkv[:])
            lds.issue(sync, 0, bvr_f[:], bv_row[:])

        @block.vector
        def _(vector):
            for i in range(NCHUNK):
                ld.wait(vector, i)
                e = i // 2
                if i % 2 == 0:
                    sems.inc(vector.tensor_copy(xb[:, e, :], stg[i % 2][:, :TPC]), "cv")
                else:
                    sems.inc(vector.tensor_copy(wb[:, e, :], stg[i % 2][:, :]), "cv")
            lds.wait_all(vector)
            sems.inc(vector.tensor_copy(bvr_b[:], bvr_f[:]), "cv")
            if q_scale != 1.0:
                sems.inc(vector.tensor_scalar(bq_sb[:, 0:8], bq_sb[:, 0:8], q_scale, None, ALU.mult), "cv")

        @block.gpsimd
        def _(gp):
            sems.inc(gp.memset(ones1[:], 1.0), "ms")
            va_h = va[:].rearrange("p t (h c) -> p t h c", c=65)
            sems.inc(gp.memset(va_h[:, :, :, 64:65], 1.0), "ms")

        @block.tensor
        def _(tensor):
            tensor.wait_ge(sems.h["cv"], sems.n["cv"])
            tensor.wait_ge(sems.h["ms"], 1)
            nmm = 0
            for j in range(16):
                for t in range(4):
                    pj = (j * 4 + t) % 2
                    if PARANOID:
                        tensor.wait_ge(sems.h["ev"], 4 * j + t)
                    if j * 4 + t >= 2:
                        tensor.wait_ge(sems.h["ev"], j * 4 + t - 1)
                    for e in range(8):
                        sems.inc(tensor.matmul(ps[pj][:], wb[:, e, j * 128:(j + 1) * 128],
                                               xb[:, e, t * 512:(t + 1) * 512],
                                               start=(e == 0), stop=(e == 7)), "mm")
            for tt in range(16):
                for hf in range(2):
                    i = 64 + tt * 2 + hf
                    pj = i % 2
                    tensor.wait_ge(sems.h["ev"], i - 1)
                    for e in range(8):
                        sems.inc(tensor.matmul(ps[pj][:], xb[:, e, tt * 128:(tt + 1) * 128],
                                               wb[:, e, 2048 + hf * 512: 2048 + (hf + 1) * 512],
                                               start=(e == 0), stop=False), "mm")
                    sems.inc(tensor.matmul(ps[pj][:, :], ones1[:, :], bvr_b[:, hf * 512:(hf + 1) * 512],
                                           start=False, stop=True), "mm")

        @block.scalar
        def _(scalar):
            for j in range(16):
                for t in range(4):
                    pj = (j * 4 + t) % 2
                    if PARANOID:
                        scalar.wait_ge(sems.h["mm"], sems.n["mm"])
                        scalar.wait_ge(sems.h["st"], sems.n["st"] and 0)
                    scalar.wait_ge(sems.h["mm"], 8 * (j * 4 + t + 1))
                    dst = qk_st[j % 2][:, t * 512:(t + 1) * 512]
                    if t == 0 and j >= 2:
                        stp.wait(scalar, j - 2)
                    sems.inc(scalar.activation(dst, ps[pj][:], AF.Identity,
                                               bias=bq_sb[:, j:j + 1],
                                               scale=(q_scale if j < 8 else 1.0)), "ev")
            for tt in range(16):
                for hf in range(2):
                    i = 64 + tt * 2 + hf
                    scalar.wait_ge(sems.h["mm"], 8 * 64 + 9 * (tt * 2 + hf + 1))
                    va_h = va[:, tt, :].rearrange("p (h c) -> p h c", c=65)
                    sems.inc(scalar.activation(va_h[:, hf * 8:(hf + 1) * 8, 0:64], ps[i % 2][:],
                                               AF.Identity, bias=0.0, scale=1.0), "ev")

        @block.gpsimd
        def _(gp):
            for j in range(16):
                gp.wait_ge(sems.h["ev"], 4 * (j + 1))
                dram = qT if j < 8 else kT
                r = (j % 8) * 128
                stp.issue(gp, j, dram[r:r + 128, :], qk_st[j % 2][:, :])
            gp.wait_ge(sems.h["ev"], 64 + 32)
            sems.inc(gp.dma_start(out=vaug[:].rearrange("(t p) c -> p t c", p=128), in_=va[:]), "vst", 16)
            stp.wait_all(gp)
            gp.wait_ge(sems.h["vst"], 16)



def build_l1_qkv(nc):
    build_l0_qkv(nc, q_scale=1.0 / 8.0)


def host_prep_l1_qkv(inputs, res4):
    wq = inputs["L1_attn_Wq"].astype(np.float32).T
    wk = inputs["L1_attn_Wk"].astype(np.float32).T
    wv = inputs["L1_attn_Wv"].astype(np.float32).T
    wcat = np.ascontiguousarray(np.concatenate([wq, wk, wv], axis=1))
    bq = inputs["L1_attn_bq"].astype(np.float32)
    bk = inputs["L1_attn_bk"].astype(np.float32)
    bv = inputs["L1_attn_bv"].astype(np.float32)
    b16 = np.ascontiguousarray(np.concatenate([bq, bk]).reshape(16, 128).T)
    bv_row = np.ascontiguousarray(bv.reshape(1, 1024))
    per_core = []
    for c in range(NC):
        per_core.append({"xT": res4.results[c]["x2T"], "wqkv": wcat, "bqkv": b16, "bv_row": bv_row})
    return per_core


def build_l0_attn(nc):
    from contextlib import ExitStack
    ctx = ExitStack()
    qTd = nc.dram_tensor("qT", [D, TPC], BF16, kind="ExternalInput").ap()
    kTf = nc.dram_tensor("kTf", [D, S], BF16, kind="ExternalInput").ap()
    vaugf = nc.dram_tensor("vaugf", [S, 16 * 65], BF16, kind="ExternalInput").ap()
    aT = nc.dram_tensor("aT", [D, TPC], BF16, kind="ExternalOutput").ap()

    NKT = S // 128
    with ctx:
        qb = ctx.enter_context(nc.sbuf_tensor("qb", [128, 8, TPC], BF16))
        kt_sb = [ctx.enter_context(nc.sbuf_tensor(f"kt{i}", [128, 128], BF16)) for i in range(2)]
        vt_sb = [ctx.enter_context(nc.sbuf_tensor(f"vt{i}", [128, 130], BF16)) for i in range(2)]
        wA = ctx.enter_context(nc.sbuf_tensor("wA", [128, 512], BF16))
        wB = ctx.enter_context(nc.sbuf_tensor("wB", [128, 512], BF16))
        rA = ctx.enter_context(nc.sbuf_tensor("rA", [1, 512], F32))
        rB = ctx.enter_context(nc.sbuf_tensor("rB", [1, 512], F32))
        bA_sb = ctx.enter_context(nc.sbuf_tensor("bA_sb", [64, 512], F32))
        bB_sb = ctx.enter_context(nc.sbuf_tensor("bB_sb", [64, 512], F32))
        ones64 = ctx.enter_context(nc.sbuf_tensor("ones64", [1, 64], F32))
        at_st = [ctx.enter_context(nc.sbuf_tensor(f"at_st{i}", [128, TPC], BF16)) for i in range(2)]
        psA = ctx.enter_context(nc.psum_tensor("psA", [128, 512], F32))
        psB = ctx.enter_context(nc.psum_tensor("psB", [128, 512], F32))
        oA = ctx.enter_context(nc.psum_tensor("oA", [65, 512], F32))
        oB = ctx.enter_context(nc.psum_tensor("oB", [65, 512], F32))
        bA_ps = ctx.enter_context(nc.psum_tensor("bA_ps", [64, 512], F32))
        bB_ps = ctx.enter_context(nc.psum_tensor("bB_ps", [64, 512], F32))
        block = ctx.enter_context(nc.Block())
        sems = Sems(nc, ctx, ["ld", "tm", "ev", "rc", "tb", "bc", "no", "ms"])
        kv = DmaStream(nc, ctx, "kv")
        stp = DmaStream(nc, ctx, "stp")

        def g_of(hp, qc, kt):
            return (hp * 4 + qc) * NKT + kt

        @block.gpsimd
        def _(gp):
            sems.inc(gp.memset(ones64[:], 1.0), "ms")

        @block.sync
        def _(sync):
            for e in range(8):
                sems.inc(sync.dma_start(out=qb[:, e, :], in_=qTd[e * 128:(e + 1) * 128, :]), "ld", 16)
            for hp in range(8):
                for qc in range(4):
                    for kt in range(NKT):
                        g = g_of(hp, qc, kt)
                        if g >= 2:
                            sync.wait_ge(sems.h["tm"], 4 * (g - 1))
                        p = g % 2
                        sync.dma_start(out=kt_sb[p][:],
                                       in_=kTf[hp * 128:(hp + 1) * 128,
                                               kt * 128:(kt + 1) * 128]).then_inc(kv.s[p], 16)
                        sync.dma_start(out=vt_sb[p][:],
                                       in_=vaugf[kt * 128:(kt + 1) * 128,
                                                 130 * hp:130 * (hp + 1)]).then_inc(kv.s[p], 16)
                        kv.cnt[p] += 32

        @block.tensor
        def _(tensor):
            tensor.wait_ge(sems.h["ld"], 8 * 16)
            for hp in range(8):
                for qc in range(4):
                    u = hp * 4 + qc
                    for kt in range(NKT):
                        g = g_of(hp, qc, kt)
                        tensor.wait_ge(kv.s[g % 2], 32 * (g // 2 + 1))
                        if g >= 1:
                            tensor.wait_ge(sems.h["ev"], 2 * g)
                        sems.inc(tensor.matmul(psA[:], kt_sb[g % 2][0:64, :],
                                               qb[0:64, hp, qc * 512:(qc + 1) * 512],
                                               start=True, stop=True, tile_position=(0, 0)), "tm")
                        sems.inc(tensor.matmul(psB[:], kt_sb[g % 2][64:128, :],
                                               qb[64:128, hp, qc * 512:(qc + 1) * 512],
                                               start=True, stop=True, tile_position=(64, 0)), "tm")
                        tensor.wait_ge(sems.h["ev"], 2 * (g + 1))
                        if kt == 0 and u >= 1:
                            tensor.wait_ge(sems.h["no"], 2 * u)
                        sems.inc(tensor.matmul(oA[:], vt_sb[g % 2][:, 0:65], wA[:],
                                               start=(kt == 0), stop=(kt == NKT - 1)), "tm")
                        sems.inc(tensor.matmul(oB[:], vt_sb[g % 2][:, 65:130], wB[:],
                                               start=(kt == 0), stop=(kt == NKT - 1)), "tm")
                    tensor.wait_ge(sems.h["rc"], 2 * (u + 1))
                    tensor.wait_ge(sems.h["ms"], 1)
                    if u >= 1:
                        tensor.wait_ge(sems.h["bc"], 2 * u)
                    sems.inc(tensor.matmul(bA_ps[:], ones64[:], rA[:], start=True, stop=True), "tb")
                    sems.inc(tensor.matmul(bB_ps[:], ones64[:], rB[:], start=True, stop=True), "tb")

        @block.scalar
        def _(scalar):
            for hp in range(8):
                for qc in range(4):
                    u = hp * 4 + qc
                    for kt in range(NKT):
                        g = g_of(hp, qc, kt)
                        scalar.wait_ge(sems.h["tm"], 4 * g + 2)
                        sems.inc(scalar.activation(wA[:], psA[:], AF.Exp, bias=0.0, scale=1.0), "ev")
                        sems.inc(scalar.activation(wB[:], psB[:], AF.Exp, bias=0.0, scale=1.0), "ev")
                    scalar.wait_ge(sems.h["tb"], 2 * (u + 1))
                    if u >= 1:
                        scalar.wait_ge(sems.h["no"], 2 * u)
                    sems.inc(scalar.copy(bA_sb[:], bA_ps[:]), "bc")
                    sems.inc(scalar.copy(bB_sb[:], bB_ps[:]), "bc")

        @block.vector
        def _(vector):
            for hp in range(8):
                for qc in range(4):
                    u = hp * 4 + qc
                    vector.wait_ge(sems.h["tm"], 4 * NKT * (u + 1))
                    sems.inc(vector.reciprocal(rA[:], oA[64:65, :]), "rc")
                    sems.inc(vector.reciprocal(rB[:], oB[64:65, :]), "rc")
                    vector.wait_ge(sems.h["bc"], 2 * (u + 1))
                    if qc == 0 and hp >= 2:
                        stp.wait(vector, hp - 2)
                    sems.inc(vector.scalar_tensor_tensor(
                        at_st[hp % 2][0:64, qc * 512:(qc + 1) * 512], oA[0:64, :], 1.0 / 32.0,
                        bA_sb[:], ALU.mult, ALU.mult), "no")
                    sems.inc(vector.scalar_tensor_tensor(
                        at_st[hp % 2][64:128, qc * 512:(qc + 1) * 512], oB[0:64, :], 1.0 / 32.0,
                        bB_sb[:], ALU.mult, ALU.mult), "no")

        @block.gpsimd
        def _(gp):
            for hp in range(8):
                gp.wait_ge(sems.h["no"], 2 * (hp * 4 + 4))
                stp.issue(gp, hp, aT[hp * 128:(hp + 1) * 128, :], at_st[hp % 2][:])
            stp.wait_all(gp)


def host_prep_l0_attn(inputs, stage1_res):
    per_core = []
    kT_by_batch = []
    va_by_batch = []
    for b in range(B):
        kT_by_batch.append(np.concatenate(
            [stage1_res.results[b * 4 + i]["kT"] for i in range(4)], axis=1))
        va_by_batch.append(np.concatenate(
            [stage1_res.results[b * 4 + i]["vaug"] for i in range(4)], axis=0))
    for c in range(NC):
        b = c // 4
        per_core.append({"qT": stage1_res.results[c]["qT"],
                         "kTf": kT_by_batch[b], "vaugf": va_by_batch[b]})
    return per_core



def build_l1_attn(nc):
    from contextlib import ExitStack
    ctx = ExitStack()
    qTd = nc.dram_tensor("qT", [D, TPC], BF16, kind="ExternalInput").ap()
    kTf = nc.dram_tensor("kTf", [D, S], BF16, kind="ExternalInput").ap()
    vaugf = nc.dram_tensor("vaugf", [S, 16 * 65], BF16, kind="ExternalInput").ap()
    ident = nc.dram_tensor("ident", [128, 128], BF16, kind="ExternalInput").ap()
    aT = nc.dram_tensor("aT", [D, TPC], BF16, kind="ExternalOutput").ap()

    NEG = -3.0e38
    NU = 256
    with ctx:
        qb = ctx.enter_context(nc.sbuf_tensor("qb", [128, 8, TPC], BF16))
        Kres = ctx.enter_context(nc.sbuf_tensor("Kres", [128, S], BF16))
        Vres = ctx.enter_context(nc.sbuf_tensor("Vres", [128, 64, 130], BF16))
        scA = ctx.enter_context(nc.sbuf_tensor("scA", [128, S], BF16))
        scB = ctx.enter_context(nc.sbuf_tensor("scB", [128, S], BF16))
        tmpA = ctx.enter_context(nc.sbuf_tensor("tmpA", [128, S], BF16))
        tmpB = ctx.enter_context(nc.sbuf_tensor("tmpB", [128, S], BF16))
        wm = [ctx.enter_context(nc.sbuf_tensor(f"wm{j}", [128, S], BF16)) for j in range(2)]
        t8 = ctx.enter_context(nc.sbuf_tensor("t8", [128, 8], BF16))
        nt = ctx.enter_context(nc.sbuf_tensor("nt", [128, 2], F32))
        vscr = ctx.enter_context(nc.sbuf_tensor("vscr", [1, 128], F32))
        wTb = [ctx.enter_context(nc.sbuf_tensor(f"wTb{j}", [128, 512], BF16)) for j in range(2)]
        id_sb = ctx.enter_context(nc.sbuf_tensor("id_sb", [128, 128], BF16))
        rA = ctx.enter_context(nc.sbuf_tensor("rA", [1, 128], F32))
        bA_sb = ctx.enter_context(nc.sbuf_tensor("bA_sb", [64, 128], F32))
        ones64 = ctx.enter_context(nc.sbuf_tensor("ones64", [1, 64], F32))
        at_st = [ctx.enter_context(nc.sbuf_tensor(f"at_st{j}", [128, TPC], BF16)) for j in range(2)]
        psA = [ctx.enter_context(nc.psum_tensor(f"psA{j}", [128, 512], F32)) for j in range(2)]
        psB = [ctx.enter_context(nc.psum_tensor(f"psB{j}", [128, 512], F32)) for j in range(2)]
        tp_ps = [ctx.enter_context(nc.psum_tensor(f"tp_ps{j}", [128, 512], BF16)) for j in range(2)]
        o_ps = ctx.enter_context(nc.psum_tensor("o_ps", [65, 256], F32))
        b_ps = ctx.enter_context(nc.psum_tensor("b_ps", [64, 128], F32))
        block = ctx.enter_context(nc.Block())
        sems = Sems(nc, ctx, ["ld", "kv", "ms", "mm", "ev", "xt", "ex", "mk",
                              "tp", "tv", "av", "rc", "tb", "bc", "no"])
        stp = DmaStream(nc, ctx, "stp")
        gkb = [0]

        @block.gpsimd
        def _(gp):
            sems.inc(gp.memset(ones64[:], 1.0), "ms")

        @block.sync
        def _(sync):
            for e in range(8):
                sems.inc(sync.dma_start(out=qb[:, e, :], in_=qTd[e * 128:(e + 1) * 128, :]), "ld", 16)
            sems.inc(sync.dma_start(out=id_sb[:], in_=ident[:]), "ld", 16)

        def emit_kv_load(hp):
            @block.sync
            def _(sync):
                if hp > 0:
                    sync.wait_ge(sems.h["no"], 32 * hp)
                sems.inc(sync.dma_start(out=Kres[:], in_=kTf[hp * 128:(hp + 1) * 128, :]), "kv", 16)
                va_src = vaugf[:, 130 * hp:130 * (hp + 1)].rearrange("(kt p) c -> p kt c", p=128)
                for j2 in range(8):
                    sems.inc(sync.dma_start(out=Vres[:, j2 * 8:(j2 + 1) * 8, :],
                                            in_=va_src[:, j2 * 8:(j2 + 1) * 8, :]), "kv", 16)

        def emit_scores(p):
            hp, qt = p // 16, p % 16
            @block.tensor
            def _(tensor):
                if p == 0:
                    tensor.wait_ge(sems.h["ld"], 16 * 9)
                if qt == 0:
                    tensor.wait_ge(sems.h["kv"], 144 * (hp + 1))
                for kc in range(16):
                    tensor.wait_ge(sems.h["ev"], 32 * p + (2 * (kc - 1) if kc >= 2 else 0))
                    sems.inc(tensor.matmul(psA[kc % 2][:], qb[0:64, hp, qt * 128:(qt + 1) * 128],
                                           Kres[0:64, kc * 512:(kc + 1) * 512],
                                           start=True, stop=True, tile_position=(0, 0)), "mm")
                    sems.inc(tensor.matmul(psB[kc % 2][:], qb[64:128, hp, qt * 128:(qt + 1) * 128],
                                           Kres[64:128, kc * 512:(kc + 1) * 512],
                                           start=True, stop=True, tile_position=(64, 0)), "mm")

        def emit_evac(p):
            @block.scalar
            def _(scalar):
                for kc in range(16):
                    scalar.wait_ge(sems.h["mm"], 32 * p + 2 * (kc + 1))
                    if kc == 0 and p > 0:
                        scalar.wait_ge(sems.h["ex"], 2 * p)
                    sems.inc(scalar.activation(scA[:, kc * 512:(kc + 1) * 512], psA[kc % 2][:],
                                               AF.Identity, bias=0.0, scale=1.0), "ev")
                    sems.inc(scalar.activation(scB[:, kc * 512:(kc + 1) * 512], psB[kc % 2][:],
                                               AF.Identity, bias=0.0, scale=1.0), "ev")

        def emit_sel(i):
            p = i // 2
            sc = scA if i % 2 == 0 else scB
            @block.vector
            def _(vector):
                vector.wait_ge(sems.h["ev"], 32 * (p + 1))
                sems.inc(vector.max(t8[:], sc[:]), "xt")
                vector.wait_ge(sems.h["xt"], sems.n["xt"])
                if i > 0:
                    vector.wait_ge(sems.h["ex"], i)
                sems.inc(vector.match_replace(tmpA[:], t8[:], sc[:], NEG), "xt")
                vector.wait_ge(sems.h["xt"], sems.n["xt"])
                sems.inc(vector.max(t8[:], tmpA[:]), "xt")
                vector.wait_ge(sems.h["xt"], sems.n["xt"])
                sems.inc(vector.match_replace(tmpB[:], t8[:], tmpA[:], NEG), "xt")
                vector.wait_ge(sems.h["xt"], sems.n["xt"])
                sems.inc(vector.max(t8[:], tmpB[:]), "xt")
                vector.wait_ge(sems.h["xt"], sems.n["xt"])
                sems.inc(vector.match_replace(tmpA[:], t8[:], tmpB[:], NEG), "xt")
                vector.wait_ge(sems.h["xt"], sems.n["xt"])
                sems.inc(vector.max(t8[:], tmpA[:]), "xt")
                vector.wait_ge(sems.h["xt"], sems.n["xt"])
                sems.inc(vector.tensor_scalar(nt[:, i % 2:i % 2 + 1], t8[:, 7:8],
                                              -1.0, None, ALU.mult), "xt")

        def emit_exp(i):
            sc = scA if i % 2 == 0 else scB
            @block.scalar
            def _(scalar):
                scalar.wait_ge(sems.h["xt"], 8 * (i + 1))
                if i > 0:
                    scalar.wait_ge(sems.h["mk"], i)
                sems.inc(scalar.activation(tmpA[:], sc[:], AF.Exp,
                                           bias=nt[:, i % 2:i % 2 + 1], scale=1.0), "ex")

        def emit_mask(i):
            @block.vector
            def _(vector):
                vector.wait_ge(sems.h["ex"], i + 1)
                if i >= 2:
                    vector.wait_ge(sems.h["tp"], 64 * (i - 1))
                sems.inc(vector.scalar_tensor_tensor(wm[i % 2][:], tmpA[:], 1.0, tmpA[:],
                                                     ALU.is_ge, ALU.mult), "mk")

        def emit_av(u):
            head = u % 2
            vlo, vhi = (0, 65) if head == 0 else (65, 130)
            par = u % 2

            def emit_T(kb, g):
                @block.tensor
                def _(tensor):
                    if kb == 0:
                        tensor.wait_ge(sems.h["mk"], u + 1)
                    if g >= 2:
                        tensor.wait_ge(sems.h["tv"], g - 1)
                    for j2 in range(4):
                        sems.inc(tensor.transpose(tp_ps[g % 2][:, j2 * 128:(j2 + 1) * 128],
                                                  wm[par][:, (kb * 4 + j2) * 128:(kb * 4 + j2 + 1) * 128],
                                                  id_sb[:]), "tp")

            def emit_copy(kb, g):
                @block.scalar
                def _(scalar):
                    scalar.wait_ge(sems.h["tp"], 4 * (g + 1))
                    if g >= 2:
                        scalar.wait_ge(sems.h["av"], 4 * (g - 1))
                    sems.inc(scalar.copy(wTb[g % 2][:], tp_ps[g % 2][:]), "tv")

            def emit_AV(kb, g):
                @block.tensor
                def _(tensor):
                    tensor.wait_ge(sems.h["tv"], g + 1)
                    if kb == 0 and u >= 2:
                        tensor.wait_ge(sems.h["no"], u - 1)
                    for j2 in range(4):
                        sems.inc(tensor.matmul(o_ps[:, par * 128:(par + 1) * 128],
                                               Vres[:, kb * 4 + j2, vlo:vhi],
                                               wTb[g % 2][:, j2 * 128:(j2 + 1) * 128],
                                               start=(kb == 0 and j2 == 0),
                                               stop=(kb == 15 and j2 == 3)), "av")

            gs = []
            for kb in range(16):
                g = gkb[0]; gkb[0] += 1; gs.append(g)
                emit_T(kb, g)
                emit_copy(kb, g)
                if kb >= 1:
                    emit_AV(kb - 1, gs[kb - 1])
            emit_AV(15, gs[15])

        def emit_norm(u):
            head = u % 2
            par = u % 2
            hp, qt = u // 32, (u // 2) % 16
            prow = slice(0, 64) if head == 0 else slice(64, 128)

            @block.vector
            def _(vector):
                vector.wait_ge(sems.h["av"], 64 * (u + 1))
                if u >= 1:
                    vector.wait_ge(sems.h["tb"], u)
                sems.inc(vector.reciprocal(rA[:], o_ps[64:65, par * 128:(par + 1) * 128]), "rc")
                vector.wait_ge(sems.h["rc"], sems.n["rc"])
                sems.inc(vector.tensor_copy(vscr[0:1, :], rA[:]), "rc")

            @block.tensor
            def _(tensor):
                tensor.wait_ge(sems.h["rc"], 2 * (u + 1))
                if u == 0:
                    tensor.wait_ge(sems.h["ms"], 1)
                if u >= 1:
                    tensor.wait_ge(sems.h["bc"], u)
                sems.inc(tensor.matmul(b_ps[:], ones64[:], rA[:], start=True, stop=True), "tb")

            @block.scalar
            def _(scalar):
                scalar.wait_ge(sems.h["tb"], u + 1)
                if u >= 1:
                    scalar.wait_ge(sems.h["no"], u)
                sems.inc(scalar.copy(bA_sb[:], b_ps[:]), "bc")

            @block.vector
            def _(vector):
                vector.wait_ge(sems.h["bc"], u + 1)
                if u % 32 == 0 and hp >= 2:
                    stp.wait(vector, hp - 2)
                sems.inc(vector.scalar_tensor_tensor(
                    at_st[hp % 2][prow, qt * 128:(qt + 1) * 128],
                    o_ps[0:64, par * 128:(par + 1) * 128], 1.0,
                    bA_sb[:], ALU.mult, ALU.mult), "no")

            if u % 32 == 31:
                @block.gpsimd
                def _(gp):
                    gp.wait_ge(sems.h["no"], 32 * (hp + 1))
                    stp.issue(gp, hp, aT[hp * 128:(hp + 1) * 128, :], at_st[hp % 2][:])

        emit_kv_load(0)
        emit_scores(0)
        emit_evac(0)

        for i in range(NU):
            p, head = i // 2, i % 2
            boundary = (i % 32 == 0 and i > 0)
            if boundary:
                emit_av(i - 1)
                emit_norm(i - 1)
                emit_kv_load(i // 32)
                emit_scores(p)
            emit_sel(i)
            if head == 0 and p >= 1:
                emit_evac(p)
            if i >= 1 and not boundary:
                emit_av(i - 1)
                emit_norm(i - 1)
            if head == 1 and (p + 1) < 128 and (p + 1) % 16 != 0:
                emit_scores(p + 1)
            emit_exp(i)
            emit_mask(i)

        emit_av(NU - 1)
        emit_norm(NU - 1)

        @block.gpsimd
        def _(gp):
            stp.wait_all(gp)


def host_prep_l1_attn(inputs, res5):
    per_core = []
    kT_by_batch, va_by_batch = [], []
    for b in range(B):
        kT_by_batch.append(np.concatenate(
            [res5.results[b * 4 + i]["kT"] for i in range(4)], axis=1))
        va_by_batch.append(np.concatenate(
            [res5.results[b * 4 + i]["vaug"] for i in range(4)], axis=0))
    ident = np.eye(128, dtype=BF)
    for c in range(NC):
        b = c // 4
        per_core.append({"qT": res5.results[c]["qT"], "kTf": kT_by_batch[b],
                         "vaugf": va_by_batch[b], "ident": ident})
    return per_core



def emit_ln(nc, block, sems, ctx, yf, x_out_cb, eps=1e-5, nchunk=8, ntc=4, tw=512,
            onesc=None, onesr=None, tag="", psum_pool=None, src_cb=None, y2_dtype=F32,
            uniq="", pre_wait=(), bufs=None):
    D_ = 128 * nchunk
    if psum_pool is None:
        sum_ps = [ctx.enter_context(nc.psum_tensor(f"sum_ps{tag}{i}", [1, 512], F32)) for i in range(2)]
        sum2_ps = [ctx.enter_context(nc.psum_tensor(f"sum2_ps{tag}{i}", [1, 512], F32)) for i in range(2)]
        ab_full = ctx.enter_context(nc.psum_tensor(f"ab_ps{tag}", [128, 2, 512], F32))
        abA, abB = ab_full[:, 0, :], ab_full[:, 1, :]
    else:
        sum_ps = [psum_pool[0][0:1, :], psum_pool[1][0:1, :]]
        sum2_ps = [psum_pool[2][0:1, :], psum_pool[3][0:1, :]]
        abA, abB = psum_pool[4][:, :], psum_pool[5][:, :]
    if bufs is None:
        y2stg = [ctx.enter_context(nc.sbuf_tensor(f"y2stg{tag}{uniq}{i}", [128, 512], y2_dtype)) for i in range(2)]
        rowA = ctx.enter_context(nc.sbuf_tensor(f"rowA{tag}{uniq}", [1, 512], F32))
        rowB = ctx.enter_context(nc.sbuf_tensor(f"rowB{tag}{uniq}", [1, 512], F32))
        rtmp = ctx.enter_context(nc.sbuf_tensor(f"rtmp{tag}{uniq}", [1, 512], F32))
        rowS = ctx.enter_context(nc.sbuf_tensor(f"rowS{tag}{uniq}", [1, 512], F32))
        Ab = [ctx.enter_context(nc.sbuf_tensor(f"Ab{tag}{uniq}{i}", [128, 512], F32)) for i in range(2)]
        Bb = [ctx.enter_context(nc.sbuf_tensor(f"Bb{tag}{uniq}{i}", [128, 512], F32)) for i in range(2)]
        bufs = dict(y2stg=y2stg, rowA=rowA, rowB=rowB, rtmp=rtmp, rowS=rowS, Ab=Ab, Bb=Bb)
    else:
        y2stg, rowA, rowB, rtmp, rowS, Ab, Bb = (bufs["y2stg"], bufs["rowA"], bufs["rowB"],
                                                 bufs["rtmp"], bufs["rowS"], bufs["Ab"], bufs["Bb"])

    if src_cb is None:
        src_cb = lambda e, t: yf[:, e, t * tw:(t + 1) * tw]
    onesc_b = None
    for t in range(ntc):
        pt = t % 2
        @block.tensor
        def _(tensor):
            sems.wait(tensor, f"lnf{tag}")
            sems.wait(tensor, f"lnv{tag}")
            for pw in pre_wait:
                sems.wait(tensor, pw)
            for e in range(nchunk):
                sems.inc(tensor.matmul(sum_ps[pt][:, :tw], onesc[:], src_cb(e, t),
                                       start=(e == 0), stop=(e == nchunk - 1)), f"lnm{tag}")

        for e in range(nchunk):
            pe = e % 2
            @block.scalar
            def _(scalar):
                sems.wait(scalar, f"lnm{tag}")
                sems.inc(scalar.activation(y2stg[pe][:, :tw], src_cb(e, t),
                                           AF.Square, bias=0.0, scale=1.0), f"lnsq{tag}")

            @block.tensor
            def _(tensor):
                sems.wait(tensor, f"lnsq{tag}")
                sems.inc(tensor.matmul(sum2_ps[pt][:, :tw], onesc[:], y2stg[pe][:, :tw],
                                       start=(e == 0), stop=(e == nchunk - 1)), f"lnm{tag}")

        @block.vector
        def _(vector):
            sems.wait(vector, f"lnm{tag}")
            sems.inc(vector.tensor_scalar(rowA[:, :tw], sum_ps[pt][:, :tw], 1.0 / D_, None, ALU.mult), f"lnv{tag}")
            sems.inc(vector.tensor_scalar(rtmp[:, :tw], sum2_ps[pt][:, :tw], 1.0 / D_, eps, ALU.mult, ALU.add), f"lnv{tag}")
            sems.inc(vector.scalar_tensor_tensor(rowB[:, :tw], rowA[:, :tw], 1.0, rowA[:, :tw],
                                                 ALU.mult, ALU.mult), f"lnv{tag}")
            sems.inc(vector.tensor_tensor(rtmp[:, :tw], rtmp[:, :tw], rowB[:, :tw], ALU.subtract), f"lnv{tag}")

        @block.scalar
        def _(scalar):
            sems.wait(scalar, f"lnv{tag}")
            sems.inc(scalar.activation(rowS[:, :tw], rtmp[:, :tw], AF.Sqrt, bias=0.0, scale=1.0), f"lnst{tag}")

        @block.vector
        def _(vector):
            sems.wait(vector, f"lnst{tag}")
            sems.inc(vector.reciprocal(rtmp[:, :tw], rowS[:, :tw]), f"lnr{tag}")
            sems.inc(vector.scalar_tensor_tensor(rowB[:, :tw], rowA[:, :tw], -1.0, rtmp[:, :tw],
                                                 ALU.mult, ALU.mult), f"lnr{tag}")

        @block.tensor
        def _(tensor):
            sems.wait(tensor, f"lnr{tag}")
            sems.wait(tensor, f"lnbc{tag}")
            sems.inc(tensor.matmul(abA[:, :tw], onesr[:], rtmp[:, :tw], start=True, stop=True), f"lnb{tag}")
            sems.inc(tensor.matmul(abB[:, :tw], onesr[:], rowB[:, :tw], start=True, stop=True), f"lnb{tag}")

        @block.scalar
        def _(scalar):
            sems.wait(scalar, f"lnb{tag}")
            sems.wait(scalar, f"lnc{tag}")
            sems.inc(scalar.copy(Ab[pt][:, :tw], abA[:, :tw]), f"lnbc{tag}")
            sems.inc(scalar.copy(Bb[pt][:, :tw], abB[:, :tw]), f"lnbc{tag}")

        @block.vector
        def _(vector):
            sems.wait(vector, f"lnbc{tag}")
            for e in range(nchunk):
                x_out_cb(vector, e, t, Ab[pt], Bb[pt])


def build_l0_post(nc, with_pooled=True):
    from contextlib import ExitStack
    ctx = ExitStack()
    aT = nc.dram_tensor("aT", [D, TPC], BF16, kind="ExternalInput").ap()
    wout = nc.dram_tensor("wout", [D, D], F32, kind="ExternalInput").ap()
    bout = nc.dram_tensor("bout", [128, 8], F32, kind="ExternalInput").ap()
    xT = nc.dram_tensor("xT", [D, TPC], F32, kind="ExternalInput").ap()
    x1T = nc.dram_tensor("x1T", [D, TPC], F32, kind="ExternalOutput").ap()
    if with_pooled:
        pooled = nc.dram_tensor("pooled", [128, 8], F32, kind="ExternalOutput").ap()

    with ctx:
        ab = ctx.enter_context(nc.sbuf_tensor("ab", [128, 8, TPC], BF16))
        wb = ctx.enter_context(nc.sbuf_tensor("wb", [128, 8, D], BF16))
        stg = [ctx.enter_context(nc.sbuf_tensor(f"stg{i}", [128, D], F32)) for i in range(2)]
        xstg = [ctx.enter_context(nc.sbuf_tensor(f"xstg{i}", [128, 512], F32)) for i in range(2)]
        bo_sb = ctx.enter_context(nc.sbuf_tensor("bo_sb", [128, 8], F32))
        yf = ctx.enter_context(nc.sbuf_tensor("yf", [128, 8, TPC], F32))
        x1b = yf
        onesc = ctx.enter_context(nc.sbuf_tensor("onesc", [128, 1], F32))
        onesr = ctx.enter_context(nc.sbuf_tensor("onesr", [1, 128], F32))
        ptmp = ctx.enter_context(nc.sbuf_tensor("ptmp", [128, 8], F32))
        pacc = ctx.enter_context(nc.sbuf_tensor("pacc", [128, 8], F32))
        ps = [ctx.enter_context(nc.psum_tensor(f"ps{i}", [128, 512], F32)) for i in range(2)]
        block = ctx.enter_context(nc.Block())
        names = ["cv", "ms", "mm", "ev", "pr", "st"]
        names += ["lnf", "lnm", "lnsq", "lnv", "lnst", "lnr", "lnb", "lnbc", "lnc"]
        sems = Sems(nc, ctx, names)
        ld = DmaStream(nc, ctx, "ld", 1)
        wl = DmaStream(nc, ctx, "wl")
        xl = DmaStream(nc, ctx, "xl")

        @block.gpsimd
        def _(gp):
            sems.inc(gp.memset(onesc[:], 1.0), "ms")
            sems.inc(gp.memset(onesr[:], 1.0), "ms")

        @block.sync
        def _(sync):
            for e in range(8):
                ld.issue(sync, 0, ab[:, e, :], aT[e * 128:(e + 1) * 128, :])
            ld.issue(sync, 0, bo_sb[:], bout[:])
            for e in range(8):
                if e >= 2:
                    sync.wait_ge(sems.h["cv"], e - 1)
                wl.issue(sync, e, stg[e % 2][:], wout[e * 128:(e + 1) * 128, :])

        @block.vector
        def _(vector):
            for e in range(8):
                wl.wait(vector, e)
                sems.inc(vector.tensor_copy(wb[:, e, :], stg[e % 2][:]), "cv")

        for j in range(8):
            for t in range(4):
                g = j * 4 + t
                pj = g % 2

                @block.sync
                def _(sync):
                    if g >= 2:
                        sync.wait_ge(sems.h["ev"], g - 1)
                    xl.issue(sync, g, xstg[pj][:], xT[j * 128:(j + 1) * 128,
                                                     t * 512:(t + 1) * 512])

                @block.tensor
                def _(tensor):
                    if g == 0:
                        tensor.wait_ge(sems.h["cv"], 8)
                        ld.wait_all(tensor)
                    if g >= 2:
                        tensor.wait_ge(sems.h["ev"], g - 1)
                    for e in range(8):
                        sems.inc(tensor.matmul(ps[pj][:], wb[:, e, j * 128:(j + 1) * 128],
                                               ab[:, e, t * 512:(t + 1) * 512],
                                               start=(e == 0), stop=(e == 7)), "mm")

                @block.vector
                def _(vector):
                    sems.wait(vector, "mm")
                    xl.wait(vector, g)
                    sems.inc(vector.scalar_tensor_tensor(yf[:, j, t * 512:(t + 1) * 512], ps[pj][:],
                                                         bo_sb[:, j:j + 1], xstg[pj][:],
                                                         ALU.add, ALU.add), "ev")
        @block.vector
        def _(vector):
            sems.wait(vector, "ev")
            sems.inc(vector.tensor_copy(ptmp[:, 0:1], yf[:, 0, 0:1]), "lnf")

        def consume(vector, e, t, Abuf, Bbuf):
            sems.inc(vector.scalar_tensor_tensor(x1b[:, e, t * 512:(t + 1) * 512],
                                                 yf[:, e, t * 512:(t + 1) * 512], 1.0,
                                                 Abuf[:], ALU.mult, ALU.mult), "pr")
            sems.inc(vector.tensor_tensor(x1b[:, e, t * 512:(t + 1) * 512],
                                          x1b[:, e, t * 512:(t + 1) * 512], Bbuf[:], ALU.add), "lnc")

        emit_ln(nc, block, sems, ctx, yf, consume, onesc=onesc, onesr=onesr, tag="")

        if with_pooled:
            @block.vector
            def _(vector):
                sems.wait(vector, "lnc")
                for e in range(8):
                    for t in range(4):
                        sems.inc(vector.tensor_reduce(ptmp[:, t:t + 1],
                                                      x1b[:, e, t * 512:(t + 1) * 512],
                                                      mybir.AxisListType.X, ALU.add), "pr")
                    sems.wait(vector, "pr")
                    sems.inc(vector.tensor_reduce(pacc[:, e:e + 1], ptmp[:, 0:4],
                                                  mybir.AxisListType.X, ALU.add), "pr")
                sems.wait(vector, "pr")

        @block.gpsimd
        def _(gp):
            sems.wait(gp, "pr")
            sems.wait(gp, "lnc")
            for e in range(8):
                sems.inc(gp.dma_start(out=x1T[e * 128:(e + 1) * 128, :], in_=x1b[:, e, :]), "st", 16)
            if with_pooled:
                sems.inc(gp.dma_start(out=pooled[:], in_=pacc[:]), "st", 16)
            sems.wait(gp, "st")



def build_l0_moe(nc):
    from contextlib import ExitStack
    ctx = ExitStack()
    x1T = nc.dram_tensor("x1T", [D, TPC], F32, kind="ExternalInput").ap()
    pooled_in = [nc.dram_tensor(f"pooled{i}", [128, 8], F32, kind="ExternalInput").ap() for i in range(4)]
    gwT = nc.dram_tensor("gwT", [D, 8], F32, kind="ExternalInput").ap()
    dwT = nc.dram_tensor("dwT", [8, D, 64], F32, kind="ExternalInput").ap()
    db = nc.dram_tensor("db", [64, 8], F32, kind="ExternalInput").ap()
    upa = nc.dram_tensor("upa", [8, 65, D], F32, kind="ExternalInput").ap()
    x2T = nc.dram_tensor("x2T", [D, TPC], F32, kind="ExternalOutput").ap()
    dbg = nc.dram_tensor("dbg", [65, 64], F32, kind="ExternalOutput").ap()

    with ctx:
        stgx = [ctx.enter_context(nc.sbuf_tensor(f"stgx{i}", [128, TPC], F32)) for i in range(2)]
        xstg = [ctx.enter_context(nc.sbuf_tensor(f"xstg{i}", [128, 512], F32)) for i in range(2)]
        xb = ctx.enter_context(nc.sbuf_tensor("xb", [128, 8, TPC], BF16))
        pl = ctx.enter_context(nc.sbuf_tensor("pl", [128, 4, 8], F32))
        gw_sb = ctx.enter_context(nc.sbuf_tensor("gw_sb", [128, 8, 8], F32))
        dstg = ctx.enter_context(nc.sbuf_tensor("dstg", [128, 8, 64], F32))
        dwb = ctx.enter_context(nc.sbuf_tensor("dwb", [128, 8, 8, 64], BF16))
        ustg = ctx.enter_context(nc.sbuf_tensor("ustg", [65, D], F32))
        ub = ctx.enter_context(nc.sbuf_tensor("ub", [65, 8, D], BF16))
        db_sb = ctx.enter_context(nc.sbuf_tensor("db_sb", [64, 8], F32))
        pool_f = ctx.enter_context(nc.sbuf_tensor("pool_f", [128, 8], F32))
        glr = ctx.enter_context(nc.sbuf_tensor("glr", [1, 16], F32))
        gate = ctx.enter_context(nc.sbuf_tensor("gate", [1, 8], F32))
        srt = ctx.enter_context(nc.sbuf_tensor("srt", [1, 8], F32))
        mrow = ctx.enter_context(nc.sbuf_tensor("mrow", [1, 8], F32))
        me_sb = ctx.enter_context(nc.sbuf_tensor("me_sb", [64, 8], F32))
        onesr = ctx.enter_context(nc.sbuf_tensor("onesr", [1, 128], F32))
        onesc = ctx.enter_context(nc.sbuf_tensor("onesc", [128, 1], F32))
        ones512 = ctx.enter_context(nc.sbuf_tensor("ones512", [1, 512], BF16))
        gaug = ctx.enter_context(nc.sbuf_tensor("gaug", [65, 8, 512], BF16))
        pool = [ctx.enter_context(nc.psum_tensor(f"pp{i}", [128, 512], F32)) for i in range(8)]
        block = ctx.enter_context(nc.Block())
        names = ["cv", "ms", "gt", "mm", "ev", "sc", "up", "st",
                 "lnf", "lnm", "lnsq", "lnv", "lnst", "lnr", "lnb", "lnbc", "lnc"]
        sems = Sems(nc, ctx, names)
        sx = DmaStream(nc, ctx, "sx")
        lds = DmaStream(nc, ctx, "lds", 1)
        dl = DmaStream(nc, ctx, "dl", 1)
        ul = DmaStream(nc, ctx, "ul", 1)
        xl = DmaStream(nc, ctx, "xl")
        out_f = ctx.enter_context(nc.sbuf_tensor("out_f", [128, 8, TPC], F32))

        @block.gpsimd
        def _(gp):
            sems.inc(gp.memset(onesr[:], 1.0), "ms")
            sems.inc(gp.memset(onesc[:], 1.0), "ms")
            sems.inc(gp.memset(ones512[:], 1.0), "ms")

        @block.sync
        def _(sync):
            for e in range(8):
                if e >= 2:
                    sync.wait_ge(sems.h["cv"], e - 1)
                sx.issue(sync, e, stgx[e % 2][:], x1T[e * 128:(e + 1) * 128, :])
            for i in range(4):
                lds.issue(sync, 0, pl[:, i, :], pooled_in[i][:])
            lds.issue(sync, 0, gw_sb[:], gwT[:].rearrange("(c p) g -> p c g", p=128))
            lds.issue(sync, 0, db_sb[:], db[:])

        @block.vector
        def _(vector):
            for e in range(8):
                sx.wait(vector, e)
                sems.inc(vector.tensor_copy(xb[:, e, :], stgx[e % 2][:]), "cv")

        @block.sync
        def _(sync):
            for ex in range(8):
                if ex >= 1:
                    sync.wait_ge(sems.h["cv"], 8 + ex)
                dl.issue(sync, 0, dstg[:], dwT[ex].rearrange("(c p) v -> p c v", p=128))

        @block.vector
        def _(vector):
            for ex in range(8):
                vector.wait_ge(dl.s[0], 16 * (ex + 1))
                sems.inc(vector.tensor_copy(dwb[:, ex, :, :], dstg[:]), "cv")

        @block.sync
        def _(sync):
            for ex in range(8):
                if ex >= 1:
                    sync.wait_ge(sems.h["cv"], 16 + ex)
                ul.issue(sync, 0, ustg[:], upa[ex])

        @block.vector
        def _(vector):
            for ex in range(8):
                vector.wait_ge(ul.s[0], 16 * (ex + 1))
                sems.inc(vector.tensor_copy(ub[:, ex, :], ustg[:]), "cv")

        @block.vector
        def _(vector):
            lds.wait_all(vector)
            sems.inc(vector.tensor_tensor(pool_f[:], pl[:, 0, :], pl[:, 1, :], ALU.add), "gt")
            sems.wait(vector, "gt")
            sems.inc(vector.tensor_tensor(pool_f[:], pool_f[:], pl[:, 2, :], ALU.add), "gt")
            sems.wait(vector, "gt")
            sems.inc(vector.tensor_tensor(pool_f[:], pool_f[:], pl[:, 3, :], ALU.add), "gt")
            sems.wait(vector, "gt")
            sems.inc(vector.tensor_scalar(pool_f[:], pool_f[:], 1.0 / S, None, ALU.mult), "gt")
            sems.wait(vector, "gt")

        @block.tensor
        def _(tensor):
            sems.wait(tensor, "gt")
            lds.wait_all(tensor)
            for cch in range(8):
                sems.inc(tensor.matmul(pool[4][0:1, 0:8], pool_f[:, cch:cch + 1], gw_sb[:, cch, :],
                                       start=(cch == 0), stop=(cch == 7)), "gt")

        @block.scalar
        def _(scalar):
            sems.wait(scalar, "gt")
            sems.inc(scalar.copy(glr[:, 0:8], pool[4][0:1, 0:8]), "gt")

        @block.vector
        def _(vector):
            sems.wait(vector, "gt")
            sems.inc(vector.tensor_reduce(glr[:, 8:9], glr[:, 0:8], mybir.AxisListType.X, ALU.max), "gt")
            sems.wait(vector, "gt")
            sems.inc(vector.tensor_scalar(glr[:, 9:10], glr[:, 8:9], -1.0, None, ALU.mult), "gt")
            sems.wait(vector, "gt")

        @block.scalar
        def _(scalar):
            sems.wait(scalar, "gt")
            sems.inc(scalar.activation(gate[:], glr[:, 0:8], AF.Exp, bias=glr[:, 9:10], scale=1.0), "gt")

        @block.vector
        def _(vector):
            sems.wait(vector, "gt")
            sems.inc(vector.tensor_reduce(glr[:, 10:11], gate[:], mybir.AxisListType.X, ALU.add), "gt")
            sems.wait(vector, "gt")
            sems.inc(vector.reciprocal(glr[:, 11:12], glr[:, 10:11]), "gt")
            sems.wait(vector, "gt")
            sems.inc(vector.tensor_scalar(gate[:], gate[:], glr[:, 11:12], None, ALU.mult), "gt")
            sems.wait(vector, "gt")
            sems.inc(vector.max(srt[:], gate[:]), "gt")
            sems.wait(vector, "gt")
            sems.inc(vector.tensor_tensor(glr[:, 12:13], srt[:, 0:1], srt[:, 1:2], ALU.add), "gt")
            sems.wait(vector, "gt")
            sems.inc(vector.tensor_scalar(glr[:, 13:14], glr[:, 12:13], 1e-8, None, ALU.add), "gt")
            sems.wait(vector, "gt")
            sems.inc(vector.reciprocal(glr[:, 14:15], glr[:, 13:14]), "gt")
            sems.wait(vector, "gt")
            sems.inc(vector.tensor_scalar(mrow[:], gate[:], srt[:, 1:2], glr[:, 14:15],
                                          ALU.is_ge, ALU.mult), "gt")
            sems.wait(vector, "gt")
            sems.inc(vector.tensor_tensor(mrow[:], mrow[:], gate[:], ALU.mult), "gt")

        @block.tensor
        def _(tensor):
            sems.wait(tensor, "gt")
            sems.wait(tensor, "ms")
            sems.inc(tensor.matmul(pool[5][0:64, 0:8], onesr[:, 0:64], mrow[:], start=True, stop=True), "gt")

        @block.scalar
        def _(scalar):
            sems.wait(scalar, "gt")
            sems.inc(scalar.copy(me_sb[:], pool[5][0:64, 0:8]), "gt")

        for t in range(4):
            for ex in range(8):
                gidx = t * 8 + ex
                pd = gidx % 2

                @block.tensor
                def _(tensor):
                    if gidx == 0:
                        sems.wait(tensor, "cv")
                        sems.wait(tensor, "gt")
                    sems.wait(tensor, "ev")
                    for cch in range(8):
                        sems.inc(tensor.matmul(pool[pd][0:64, :], dwb[:, ex, cch, :],
                                               xb[:, cch, t * 512:(t + 1) * 512],
                                               start=(cch == 0), stop=(cch == 7)), "mm")

                @block.scalar
                def _(scalar):
                    sems.wait(scalar, "mm")
                    sems.wait(scalar, "up")
                    sems.inc(scalar.activation(gaug[0:64, ex, :], pool[pd][0:64, :], AF.Gelu,
                                               bias=db_sb[:, ex:ex + 1], scale=1.0), "ev")

                @block.vector
                def _(vector):
                    sems.wait(vector, "ev")
                    sems.inc(vector.tensor_scalar(gaug[0:64, ex, :], gaug[0:64, ex, :],
                                                  me_sb[:, ex:ex + 1], None, ALU.mult), "sc")
                    sems.inc(vector.tensor_scalar(gaug[64:65, ex, :], ones512[:],
                                                  mrow[:, ex:ex + 1], None, ALU.mult), "sc")

            for j in range(8):
                pu = 2 + (t * 8 + j) % 2

                @block.sync
                def _(sync):
                    sems.wait(sync, "ev")
                    xl.issue(sync, t * 8 + j, xstg[(t * 8 + j) % 2][:],
                             x1T[j * 128:(j + 1) * 128, t * 512:(t + 1) * 512])

                @block.tensor
                def _(tensor):
                    sems.wait(tensor, "sc")
                    sems.wait(tensor, "ev")
                    for ex in range(8):
                        sems.inc(tensor.matmul(pool[pu][:, :], ub[:, ex, j * 128:(j + 1) * 128],
                                               gaug[:, ex, :], start=(ex == 0), stop=(ex == 7)), "up")

                @block.vector
                def _(vector):
                    sems.wait(vector, "up")
                    xl.wait(vector, t * 8 + j)
                    sems.inc(vector.scalar_tensor_tensor(out_f[:, j, t * 512:(t + 1) * 512],
                                                         pool[pu][:, :], 0.0,
                                                         xstg[(t * 8 + j) % 2][:],
                                                         ALU.add, ALU.add), "ev")

        @block.vector
        def _(vector):
            sems.wait(vector, "ev")
            sems.inc(vector.tensor_copy(glr[:, 15:16], out_f[0:1, 0, 0:1]), "lnf")

        def consume(vector, e, t, Abuf, Bbuf):
            sems.inc(vector.scalar_tensor_tensor(out_f[:, e, t * 512:(t + 1) * 512],
                                                 out_f[:, e, t * 512:(t + 1) * 512], 1.0,
                                                 Abuf[:], ALU.mult, ALU.mult), "sc")
            sems.inc(vector.tensor_tensor(out_f[:, e, t * 512:(t + 1) * 512],
                                          out_f[:, e, t * 512:(t + 1) * 512], Bbuf[:], ALU.add), "lnc")

        emit_ln(nc, block, sems, ctx, out_f, consume, onesc=onesc, onesr=onesr, tag="",
                psum_pool=[pool[0], pool[1], pool[2], pool[3], pool[6], pool[7]])

        @block.gpsimd
        def _(gp):
            sems.wait(gp, "lnc")
            for e in range(8):
                sems.inc(gp.dma_start(out=x2T[e * 128:(e + 1) * 128, :], in_=out_f[:, e, :]), "st", 16)
            sems.inc(gp.dma_start(out=dbg[0:1, 0:8], in_=gate[:]), "st", 16)
            sems.inc(gp.dma_start(out=dbg[1:2, 0:8], in_=srt[:]), "st", 16)
            sems.inc(gp.dma_start(out=dbg[2:3, 0:8], in_=mrow[:]), "st", 16)
            sems.inc(gp.dma_start(out=dbg[3:4, 0:16], in_=glr[:]), "st", 16)
            sems.inc(gp.dma_start(out=dbg[4:5, 0:8], in_=me_sb[0:1, :]), "st", 16)
            sems.inc(gp.dma_start(out=dbg[5:6, 0:64], in_=gaug[0:1, 0, 0:64]), "st", 16)
            sems.wait(gp, "st")


def host_prep_l0_moe(inputs, res3):
    gwT = np.ascontiguousarray(inputs["L0_ffn_gate_w"].astype(np.float32).T)
    dwT = np.ascontiguousarray(inputs["L0_ffn_down_w"].astype(np.float32).transpose(0, 2, 1))
    db = np.ascontiguousarray(inputs["L0_ffn_down_b"].astype(np.float32).T)
    upw = inputs["L0_ffn_up_w"].astype(np.float32).transpose(0, 2, 1)
    upb = inputs["L0_ffn_up_b"].astype(np.float32)[:, None, :]
    upa = np.ascontiguousarray(np.concatenate([upw, upb], axis=1))
    per_core = []
    for c in range(NC):
        b = c // 4
        d = {"x1T": res3.results[c]["x1T"], "gwT": gwT, "dwT": dwT, "db": db, "upa": upa}
        for i in range(4):
            d[f"pooled{i}"] = res3.results[b * 4 + i]["pooled"]
        per_core.append(d)
    return per_core


def host_prep_l0_post(inputs, res1, res2):
    wout = np.ascontiguousarray(inputs["L0_attn_W_out"].astype(np.float32))
    bout = np.ascontiguousarray(inputs["L0_attn_b_out"].astype(np.float32).reshape(8, 128).T)
    x = inputs["L0_attn_x"]
    per_core = []
    for c in range(NC):
        b, s0 = c // 4, (c % 4) * TPC
        xT = np.ascontiguousarray(x[b, s0:s0 + TPC, :].T.astype(np.float32))
        per_core.append({"aT": res2.results[c]["aT"], "wout": wout, "bout": bout, "xT": xT})
    return per_core



def build_l1f_qkv(nc):
    from contextlib import ExitStack
    ctx = ExitStack()
    xT = nc.dram_tensor("xT", [D, TPC], F32, kind="ExternalInput").ap()
    xcol8 = nc.dram_tensor("xcol8", [128, 8], F32, kind="ExternalInput").ap()
    wqkv = nc.dram_tensor("wqkv", [D, 1536], F32, kind="ExternalInput").ap()
    qsr = nc.dram_tensor("qsr", [128, 2], F32, kind="ExternalInput").ap()
    tabs = nc.dram_tensor("tabs", [4, 128, TPC], F32, kind="ExternalInput").ap()
    selm = nc.dram_tensor("selm", [128, 4, 8], F32, kind="ExternalInput").ap()
    selmT = nc.dram_tensor("selmT", [8, 4, 128], F32, kind="ExternalInput").ap()
    qrT = nc.dram_tensor("qrT", [512, TPC], BF16, kind="ExternalOutput").ap()
    krT = nc.dram_tensor("krT", [512, TPC], BF16, kind="ExternalOutput").ap()
    vaug = nc.dram_tensor("vaug", [TPC, 520], BF16, kind="ExternalOutput").ap()

    with ctx:
        xsb = ctx.enter_context(nc.sbuf_tensor("xsb", [128, 8, TPC + 1], BF16))
        stg = [ctx.enter_context(nc.sbuf_tensor(f"stg{i}", [128, TPC], F32)) for i in range(2)]
        wb = ctx.enter_context(nc.sbuf_tensor("wb", [128, 8, 1536], BF16))
        xc_f = ctx.enter_context(nc.sbuf_tensor("xc_f", [128, 8], F32))
        qs_sb = ctx.enter_context(nc.sbuf_tensor("qs_sb", [128, 2], F32))
        tab_b = ctx.enter_context(nc.sbuf_tensor("tab_b", [128, 4, TPC], BF16))
        sel_sb = ctx.enter_context(nc.sbuf_tensor("sel_sb", [128, 4, 8], F32))
        selT_sb = ctx.enter_context(nc.sbuf_tensor("selT_sb", [8, 4, 128], F32))
        qkb = ctx.enter_context(nc.sbuf_tensor("qkb", [128, 8, TPC], BF16))
        q2stg = [ctx.enter_context(nc.sbuf_tensor(f"q2stg{i}", [128, 512], F32)) for i in range(2)]
        rsb = ctx.enter_context(nc.sbuf_tensor("rsb", [8, 2, TPC], F32))
        rbsb = ctx.enter_context(nc.sbuf_tensor("rbsb", [128, 512], F32))
        qnb = ctx.enter_context(nc.sbuf_tensor("qnb", [128, 512], BF16))
        r1b = ctx.enter_context(nc.sbuf_tensor("r1b", [128, 512], BF16))
        r2b = ctx.enter_context(nc.sbuf_tensor("r2b", [128, 512], BF16))
        qout = ctx.enter_context(nc.sbuf_tensor("qout", [128, 4, TPC], BF16))
        kout = ctx.enter_context(nc.sbuf_tensor("kout", [128, 4, TPC], BF16))
        va = ctx.enter_context(nc.sbuf_tensor("va", [128, 16, 520], BF16))
        ps = [ctx.enter_context(nc.psum_tensor(f"ps{i}", [128, 512], F32)) for i in range(2)]
        ss_ps = ctx.enter_context(nc.psum_tensor("ss_ps", [8, 512], F32))
        rb_ps = ctx.enter_context(nc.psum_tensor("rb_ps", [128, 512], F32))
        block = ctx.enter_context(nc.Block())
        sems = Sems(nc, ctx, ["cv", "ms", "mm", "ev", "sq", "sm", "rv", "rb", "rt", "st"])
        sg = DmaStream(nc, ctx, "sg")
        lds = DmaStream(nc, ctx, "lds", 1)

        def xs(e, lo, hi):
            off = 1 if e < 4 else 0
            return xsb[:, e, lo + off:hi + off]

        @block.gpsimd
        def _(gp):
            va_h = va[:].rearrange("p t (h c) -> p t h c", c=65)
            sems.inc(gp.memset(va_h[:, :, :, 64:65], 1.0), "ms")

        @block.sync
        def _(sync):
            for i in range(8):
                if i >= 2:
                    sync.wait_ge(sems.h["cv"], i - 1)
                sg.issue(sync, i, stg[i % 2][:], xT[i * 128:(i + 1) * 128, :])
            lds.issue(sync, 0, xc_f[:], xcol8[:])
            lds.issue(sync, 0, qs_sb[:], qsr[:])
            lds.issue(sync, 0, sel_sb[:], selm[:])
            lds.issue(sync, 0, selT_sb[:], selmT[:])
            for i in range(8):
                sync.wait_ge(sems.h["cv"], (7 + i) if i < 2 else (16 + i - 1))
                sg.issue(sync, 8 + i, stg[i % 2][:, :1536], wqkv[i * 128:(i + 1) * 128, :])
            for i in range(4):
                sync.wait_ge(sems.h["cv"], (23 + i) if i < 2 else (24 + i - 1))
                sg.issue(sync, 16 + i, stg[i % 2][:], tabs[i])

        @block.vector
        def _(vector):
            for i in range(8):
                sg.wait(vector, i)
                sems.inc(vector.tensor_copy(xsb[:, i, 1:TPC + 1], stg[i % 2][:]), "cv")
            lds.wait_all(vector)
            for e in range(8):
                sems.inc(vector.tensor_copy(xsb[:, e, 0:1], xc_f[:, e:e + 1]), "cv")
            for i in range(8):
                sg.wait(vector, 8 + i)
                sems.inc(vector.tensor_copy(wb[:, i, :], stg[i % 2][:, :1536]), "cv")
            for i in range(4):
                sg.wait(vector, 16 + i)
                sems.inc(vector.tensor_copy(tab_b[:, i, :], stg[i % 2][:]), "cv")

        for j in range(8):
            for t in range(4):
                g = j * 4 + t

                @block.tensor
                def _(tensor):
                    if g == 0:
                        sems.wait(tensor, "cv")
                    sems.wait(tensor, "ev")
                    for e in range(8):
                        sems.inc(tensor.matmul(ps[g % 2][:], wb[:, e, j * 128:(j + 1) * 128],
                                               xs(e, t * 512, (t + 1) * 512),
                                               start=(e == 0), stop=(e == 7)), "mm")

                @block.scalar
                def _(scalar):
                    sems.wait(scalar, "mm")
                    sems.inc(scalar.activation(qkb[:, j, t * 512:(t + 1) * 512], ps[g % 2][:],
                                               AF.Identity, bias=0.0, scale=1.0), "ev")

        for tt in range(16):
            g = 32 + tt

            @block.tensor
            def _(tensor):
                sems.wait(tensor, "ev")
                for e in range(8):
                    sems.inc(tensor.matmul(ps[g % 2][:], xs(e, tt * 128, (tt + 1) * 128),
                                           wb[:, e, 1024:1536],
                                           start=(e == 0), stop=(e == 7)), "mm")

            @block.scalar
            def _(scalar):
                sems.wait(scalar, "mm")
                va_h = va[:, tt, :].rearrange("p (h c) -> p h c", c=65)
                sems.inc(scalar.activation(va_h[:, :, 0:64], ps[g % 2][:],
                                           AF.Identity, bias=0.0, scale=1.0), "ev")

        for qk in range(2):
            for t in range(4):
                for c in range(4):
                    @block.scalar
                    def _(scalar):
                        sems.wait(scalar, "sm")
                        sems.inc(scalar.activation(q2stg[c % 2][:], qkb[:, qk * 4 + c, t * 512:(t + 1) * 512],
                                                   AF.Square, bias=0.0, scale=1.0), "sq")

                    @block.tensor
                    def _(tensor):
                        sems.wait(tensor, "sq")
                        if c == 0:
                            sems.wait(tensor, "rv")
                        sems.inc(tensor.matmul(ss_ps[:], sel_sb[:, c, :], q2stg[c % 2][:],
                                               start=(c == 0), stop=(c == 3)), "sm")

                @block.scalar
                def _(scalar):
                    sems.wait(scalar, "sm")
                    sems.wait(scalar, "rv")
                    sems.inc(scalar.activation(rsb[:, qk, t * 512:(t + 1) * 512], ss_ps[:],
                                               AF.Sqrt, bias=0.0, scale=1.0), "rv")

                @block.vector
                def _(vector):
                    sems.wait(vector, "rv")
                    sems.inc(vector.tensor_scalar(rsb[:, qk, t * 512:(t + 1) * 512],
                                                  rsb[:, qk, t * 512:(t + 1) * 512],
                                                  1e-12, None, ALU.max), "rv")
                    sems.wait(vector, "rv")
                    sems.inc(vector.reciprocal(rsb[:, qk, t * 512:(t + 1) * 512],
                                               rsb[:, qk, t * 512:(t + 1) * 512]), "rv")

        for qk in range(2):
            tabC = 0 if qk == 0 else 2
            outb = qout if qk == 0 else kout
            for c in range(4):
                for t in range(4):
                    @block.tensor
                    def _(tensor):
                        sems.wait(tensor, "rv")
                        sems.wait(tensor, "rb")
                        sems.inc(tensor.matmul(rb_ps[:], selT_sb[:, c, :],
                                               rsb[:, qk, t * 512:(t + 1) * 512],
                                               start=True, stop=True), "rt")

                    @block.scalar
                    def _(scalar):
                        sems.wait(scalar, "rt")
                        sems.inc(scalar.copy(rbsb[:], rb_ps[:]), "rb")

                    @block.vector
                    def _(vector):
                        sems.wait(vector, "rb")
                        sems.inc(vector.scalar_tensor_tensor(
                            qnb[:], qkb[:, qk * 4 + c, t * 512:(t + 1) * 512],
                            qs_sb[:, qk:qk + 1], rbsb[:], ALU.mult, ALU.mult), "rt")
                        sems.wait(vector, "rt")
                        sems.inc(vector.tensor_tensor(r1b[:], qnb[:],
                                                      tab_b[:, tabC, t * 512:(t + 1) * 512], ALU.mult), "rt")
                        for sblk in range(4):
                            srcp = (sblk ^ 1) * 32
                            dstp = sblk * 32
                            sems.inc(vector.tensor_copy(r2b[dstp:dstp + 32, :],
                                                        qnb[srcp:srcp + 32, :]), "rt")
                        sems.wait(vector, "rt")
                        sems.inc(vector.tensor_tensor(
                            r2b[:], r2b[:], tab_b[:, tabC + 1, t * 512:(t + 1) * 512], ALU.mult), "rt")
                        sems.wait(vector, "rt")
                        sems.inc(vector.tensor_tensor(outb[:, c, t * 512:(t + 1) * 512],
                                                      r1b[:], r2b[:], ALU.add), "rt")

        @block.gpsimd
        def _(gp):
            sems.wait(gp, "rt")
            sems.wait(gp, "ev")
            sems.wait(gp, "ms")
            for c in range(4):
                sems.inc(gp.dma_start(out=qrT[c * 128:(c + 1) * 128, :], in_=qout[:, c, :]), "st", 16)
                sems.inc(gp.dma_start(out=krT[c * 128:(c + 1) * 128, :], in_=kout[:, c, :]), "st", 16)
            sems.inc(gp.dma_start(out=vaug[:].rearrange("(t p) c -> p t c", p=128), in_=va[:]), "st", 16)
            sems.wait(gp, "st")


SERIAL_TILES = False


def build_l1f_attn(nc):
    from contextlib import ExitStack
    ctx = ExitStack()
    qrT = nc.dram_tensor("qrT", [512, TPC], BF16, kind="ExternalInput").ap()
    krT = nc.dram_tensor("krT", [512, TPC], BF16, kind="ExternalInput").ap()
    vaug = nc.dram_tensor("vaug", [TPC, 520], BF16, kind="ExternalInput").ap()
    kb4 = nc.dram_tensor("kb4", [128, 4, 512], BF16, kind="ExternalInput").ap()
    vb4 = nc.dram_tensor("vb4", [128, 4, 520], BF16, kind="ExternalInput").ap()
    mask0 = nc.dram_tensor("mask0", [4, 128, 1024], F32, kind="ExternalInput").ap()
    mask1 = nc.dram_tensor("mask1", [4, 128, 1024], F32, kind="ExternalInput").ap()
    xT = nc.dram_tensor("xT", [D, TPC], F32, kind="ExternalInput").ap()
    wo = nc.dram_tensor("wo", [512, D], F32, kind="ExternalInput").ap()
    ident = nc.dram_tensor("ident", [128, 128], BF16, kind="ExternalInput").ap()
    xpaT = nc.dram_tensor("xpaT", [D, TPC], F32, kind="ExternalOutput").ap()

    with ctx:
        qrx = ctx.enter_context(nc.sbuf_tensor("qrx", [128, 4, TPC], BF16))
        krx = ctx.enter_context(nc.sbuf_tensor("krx", [128, 4, TPC + 512], BF16))
        vx = ctx.enter_context(nc.sbuf_tensor("vx", [128, 20, 520], BF16))
        m0 = ctx.enter_context(nc.sbuf_tensor("m0", [128, 4, 1024], F32))
        m1 = ctx.enter_context(nc.sbuf_tensor("m1", [128, 4, 1024], F32))
        msk = ctx.enter_context(nc.sbuf_tensor("msk", [128, 1024], F32))
        wloc32 = ctx.enter_context(nc.sbuf_tensor("wloc32", [128, 1024], F32))
        mx = ctx.enter_context(nc.sbuf_tensor("mx", [128, 2], F32))
        wloc = ctx.enter_context(nc.sbuf_tensor("wloc", [128, 1024], BF16))
        wTt = ctx.enter_context(nc.sbuf_tensor("wTt", [128, 128], BF16))
        id_sb = ctx.enter_context(nc.sbuf_tensor("id_sb", [128, 128], BF16))
        rA = ctx.enter_context(nc.sbuf_tensor("rA", [1, 128], F32))
        bA_sb = ctx.enter_context(nc.sbuf_tensor("bA_sb", [64, 128], F32))
        ones64 = ctx.enter_context(nc.sbuf_tensor("ones64", [1, 64], F32))
        vscr = ctx.enter_context(nc.sbuf_tensor("vscr", [128, 128], F32))
        oT = ctx.enter_context(nc.sbuf_tensor("oT", [128, 4, TPC], BF16))
        wstg = [ctx.enter_context(nc.sbuf_tensor(f"wstg{i}", [128, D], F32)) for i in range(2)]
        wob = ctx.enter_context(nc.sbuf_tensor("wob", [128, 4, D], BF16))
        xstg = [ctx.enter_context(nc.sbuf_tensor(f"xstg{i}", [128, 512], F32)) for i in range(2)]
        xout = ctx.enter_context(nc.sbuf_tensor("xout", [128, 8, TPC], F32))
        ps_loc = ctx.enter_context(nc.psum_tensor("ps_loc", [128, 1024], F32))
        tp_ps = ctx.enter_context(nc.psum_tensor("tp_ps", [128, 128], F32))
        id32 = ctx.enter_context(nc.sbuf_tensor("id32", [128, 128], F32))
        o_ps = ctx.enter_context(nc.psum_tensor("o_ps", [65, 128], F32))
        b_ps = ctx.enter_context(nc.psum_tensor("b_ps", [64, 128], F32))
        ps = [ctx.enter_context(nc.psum_tensor(f"ps{i}", [128, 512], F32)) for i in range(2)]
        block = ctx.enter_context(nc.Block())
        sems = Sems(nc, ctx, ["ld", "cv", "ms", "mm", "mk", "ex", "tp", "tv", "av",
                              "rc", "tb", "bc", "no", "pm", "pe", "xev"])
        wl = DmaStream(nc, ctx, "wl")
        xl = DmaStream(nc, ctx, "xl")
        stp = DmaStream(nc, ctx, "stp", 1)

        @block.gpsimd
        def _(gp):
            sems.inc(gp.memset(ones64[:], 1.0), "ms")

        @block.sync
        def _(sync):
            for c in range(4):
                sems.inc(sync.dma_start(out=qrx[:, c, :], in_=qrT[c * 128:(c + 1) * 128, :]), "ld", 16)
                sems.inc(sync.dma_start(out=krx[:, c, 512:], in_=krT[c * 128:(c + 1) * 128, :]), "ld", 16)
            sems.inc(sync.dma_start(out=krx[:, :, 0:512], in_=kb4[:]), "ld", 16)
            sems.inc(sync.dma_start(out=vx[:, 0:4, :], in_=vb4[:]), "ld", 16)
            va_src = vaug[:].rearrange("(kt p) c -> p kt c", p=128)
            for i in range(2):
                sems.inc(sync.dma_start(out=vx[:, 4 + i * 8:4 + (i + 1) * 8, :],
                                        in_=va_src[:, i * 8:(i + 1) * 8, :]), "ld", 16)
            for v in range(4):
                sems.inc(sync.dma_start(out=m0[:, v, :], in_=mask0[v]), "ld", 16)
                sems.inc(sync.dma_start(out=m1[:, v, :], in_=mask1[v]), "ld", 16)
            sems.inc(sync.dma_start(out=id_sb[:], in_=ident[:]), "ld", 16)
            for c in range(4):
                if c >= 2:
                    sync.wait_ge(sems.h["cv"], c)
                wl.issue(sync, c, wstg[c % 2][:], wo[c * 128:(c + 1) * 128, :])

        @block.vector
        def _(vector):
            vector.wait_ge(sems.h["ld"], 16 * 21)
            sems.inc(vector.tensor_copy(id32[:], id_sb[:]), "cv")
            for c in range(4):
                wl.wait(vector, c)
                sems.inc(vector.tensor_copy(wob[:, c, :], wstg[c % 2][:]), "cv")

        for h in range(8):
            hb = (h % 2) * 64
            ch = h // 2
            for w in range(4):
                mbuf = m1 if w == 0 else m0
                for qt in range(4):
                    q0 = w * 512 + qt * 128

                    @block.tensor
                    def _(tensor):
                        if h == 0 and w == 0 and qt == 0:
                            tensor.wait_ge(sems.h["ld"], 16 * 21)
                        if SERIAL_TILES:
                            sems.wait(tensor, "no")
                        sems.wait(tensor, "pm")
                        sems.inc(tensor.matmul(ps_loc[:, 0:512], qrx[hb:hb + 64, ch, q0:q0 + 128],
                                               krx[hb:hb + 64, ch, w * 512:w * 512 + 512],
                                               start=True, stop=True,
                                               tile_position=(hb, 0)), "mm")
                        sems.inc(tensor.matmul(ps_loc[:, 512:1024], qrx[hb:hb + 64, ch, q0:q0 + 128],
                                               krx[hb:hb + 64, ch, w * 512 + 512:w * 512 + 1024],
                                               start=True, stop=True,
                                               tile_position=(hb, 0)), "mm")

                    @block.vector
                    def _(vector):
                        sems.wait(vector, "mm")
                        sems.wait(vector, "pe")
                        sems.inc(vector.scalar_tensor_tensor(msk[:], ps_loc[:], 1.0,
                                                             mbuf[:, qt, :], ALU.mult, ALU.add), "pm")
                        sems.wait(vector, "pm")
                        sems.inc(vector.tensor_reduce(mx[:, 0:1], msk[:],
                                                      mybir.AxisListType.X, ALU.max), "pm")
                        sems.wait(vector, "pm")
                        sems.inc(vector.tensor_scalar(mx[:, 1:2], mx[:, 0:1], -1.0, None, ALU.mult), "pm")
                        sems.wait(vector, "pm")
                        sems.inc(vector.tensor_copy(vscr[:, 0:1], mx[:, 1:2]), "pm")

                    @block.scalar
                    def _(scalar):
                        sems.wait(scalar, "pm")
                        sems.wait(scalar, "tp")
                        sems.inc(scalar.activation(wloc32[:], msk[:], AF.Exp,
                                                   bias=mx[:, 1:2], scale=1.0), "pe")

                    for kt in range(8):
                        @block.tensor
                        def _(tensor):
                            sems.wait(tensor, "pe")
                            sems.wait(tensor, "tv")
                            if kt == 0:
                                sems.wait(tensor, "no")
                                if h == 0 and w == 0 and qt == 0:
                                    tensor.wait_ge(sems.h["cv"], 1)
                            sems.inc(tensor.transpose(tp_ps[:], wloc32[:, kt * 128:(kt + 1) * 128],
                                                      id32[:]), "tp")

                        @block.scalar
                        def _(scalar):
                            sems.wait(scalar, "tp")
                            sems.wait(scalar, "av")
                            sems.inc(scalar.copy(wTt[:], tp_ps[:]), "tv")

                        @block.tensor
                        def _(tensor):
                            sems.wait(tensor, "tv")
                            sems.inc(tensor.matmul(o_ps[:], vx[:, w * 4 + kt, h * 65:h * 65 + 65],
                                                   wTt[:], start=(kt == 0), stop=(kt == 7)), "av")

                    @block.vector
                    def _(vector):
                        sems.wait(vector, "av")
                        sems.inc(vector.reciprocal(rA[:], o_ps[64:65, :]), "rc")
                        sems.wait(vector, "rc")
                        sems.inc(vector.tensor_copy(vscr[0:1, :128], rA[:]), "rc")

                    @block.tensor
                    def _(tensor):
                        sems.wait(tensor, "rc")
                        sems.wait(tensor, "ms")
                        sems.wait(tensor, "bc")
                        sems.inc(tensor.matmul(b_ps[:], ones64[:], rA[:], start=True, stop=True), "tb")

                    @block.scalar
                    def _(scalar):
                        sems.wait(scalar, "tb")
                        sems.wait(scalar, "no")
                        sems.inc(scalar.copy(bA_sb[:], b_ps[:]), "bc")

                    @block.vector
                    def _(vector):
                        sems.wait(vector, "bc")
                        sems.inc(vector.scalar_tensor_tensor(
                            oT[hb:hb + 64, ch, q0:q0 + 128], o_ps[0:64, :], 1.0,
                            bA_sb[:], ALU.mult, ALU.mult), "no")

        for j in range(8):
            for t in range(4):
                g = j * 4 + t

                @block.sync
                def _(sync):
                    sems.wait(sync, "xev", max(0, g - 1))
                    xl.issue(sync, g, xstg[g % 2][:],
                             xT[j * 128:(j + 1) * 128, t * 512:(t + 1) * 512])

                @block.tensor
                def _(tensor):
                    if g == 0:
                        sems.wait(tensor, "no")
                        sems.wait(tensor, "cv")
                    sems.wait(tensor, "xev", max(0, g - 1))
                    for c in range(4):
                        sems.inc(tensor.matmul(ps[g % 2][:], wob[:, c, j * 128:(j + 1) * 128],
                                               oT[:, c, t * 512:(t + 1) * 512],
                                               start=(c == 0), stop=(c == 3)), "mm")

                @block.vector
                def _(vector):
                    sems.wait(vector, "mm")
                    xl.wait(vector, g)
                    sems.inc(vector.scalar_tensor_tensor(xout[:, j, t * 512:(t + 1) * 512],
                                                         ps[g % 2][:], 0.0, xstg[g % 2][:],
                                                         ALU.add, ALU.add), "xev")

        @block.gpsimd
        def _(gp):
            sems.wait(gp, "xev")
            for e in range(8):
                stp.issue(gp, 0, xpaT[e * 128:(e + 1) * 128, :], xout[:, e, :])
            stp.wait_all(gp)


def build_l1f_mlp(nc):
    from contextlib import ExitStack
    ctx = ExitStack()
    HP = 2816
    xT = nc.dram_tensor("xT", [D, TPC], F32, kind="ExternalInput").ap()
    xcol8 = nc.dram_tensor("xcol8", [128, 8], F32, kind="ExternalInput").ap()
    lgb = nc.dram_tensor("lgb", [128, 16], F32, kind="ExternalInput").ap()
    w1a = nc.dram_tensor("w1a", [D, HP], F32, kind="ExternalInput").ap()
    w1g = nc.dram_tensor("w1g", [D, HP], F32, kind="ExternalInput").ap()
    b1ag = nc.dram_tensor("b1ag", [128, 44], F32, kind="ExternalInput").ap()
    w2p = nc.dram_tensor("w2p", [HP, D], F32, kind="ExternalInput").ap()
    b2 = nc.dram_tensor("b2", [128, 8], F32, kind="ExternalInput").ap()
    xoutT = nc.dram_tensor("xoutT", [D, TPC], F32, kind="ExternalOutput").ap()

    NJ = 22
    with ctx:
        xsb = ctx.enter_context(nc.sbuf_tensor("xsb", [128, 8, TPC + 1], BF16))
        stg = [ctx.enter_context(nc.sbuf_tensor(f"stg{i}", [128, TPC], F32)) for i in range(2)]
        xc_f = ctx.enter_context(nc.sbuf_tensor("xc_f", [128, 8], F32))
        lgb_sb = ctx.enter_context(nc.sbuf_tensor("lgb_sb", [128, 16], F32))
        b1_sb = ctx.enter_context(nc.sbuf_tensor("b1_sb", [128, 44], F32))
        b2_sb = ctx.enter_context(nc.sbuf_tensor("b2_sb", [128, 8], F32))
        hn = ctx.enter_context(nc.sbuf_tensor("hn", [128, 8, 512], BF16))
        w1stg = [ctx.enter_context(nc.sbuf_tensor(f"w1stg{i}", [128, 8, 128], F32)) for i in range(2)]
        w1b = [ctx.enter_context(nc.sbuf_tensor(f"w1b{i}", [128, 8, 128], BF16)) for i in range(2)]
        aab = ctx.enter_context(nc.sbuf_tensor("aab", [128, NJ, 512], BF16))
        ggb = ctx.enter_context(nc.sbuf_tensor("ggb", [128, NJ, 512], BF16))
        w2stg = [ctx.enter_context(nc.sbuf_tensor(f"w2stg{i}", [128, D], F32)) for i in range(2)]
        w2b = [ctx.enter_context(nc.sbuf_tensor(f"w2b{i}", [128, D], BF16)) for i in range(2)]
        xstg = [ctx.enter_context(nc.sbuf_tensor(f"xstg{i}", [128, 512], F32)) for i in range(2)]
        onesc_b = ctx.enter_context(nc.sbuf_tensor("onesc_b", [128, 1], BF16))
        onesr = ctx.enter_context(nc.sbuf_tensor("onesr", [1, 128], F32))
        xo = ctx.enter_context(nc.sbuf_tensor("xo", [128, 8, 512], F32))
        pool = [ctx.enter_context(nc.psum_tensor(f"pp{i}", [128, 512], F32)) for i in range(8)]
        block = ctx.enter_context(nc.Block())
        names = ["cv", "ms", "w1c", "mm", "ev", "gg", "w2c",
                 "up", "uev", "st",
                 "lnf", "lnm", "lnsq", "lnv", "lnst", "lnr", "lnb", "lnbc", "lnc"]
        sems = Sems(nc, ctx, names)
        sx = DmaStream(nc, ctx, "sx")
        lds = DmaStream(nc, ctx, "lds", 1)
        w1s = DmaStream(nc, ctx, "w1s")
        w2s = DmaStream(nc, ctx, "w2s")
        xl = DmaStream(nc, ctx, "xl")

        def xs(e, lo, hi):
            off = 1 if e < 4 else 0
            return xsb[:, e, lo + off:hi + off]

        @block.gpsimd
        def _(gp):
            sems.inc(gp.memset(onesc_b[:], 1.0), "ms")
            sems.inc(gp.memset(onesr[:], 1.0), "ms")

        @block.sync
        def _(sync):
            for i in range(8):
                if i >= 2:
                    sync.wait_ge(sems.h["cv"], i - 1)
                sx.issue(sync, i, stg[i % 2][:], xT[i * 128:(i + 1) * 128, :])
            lds.issue(sync, 0, xc_f[:], xcol8[:])
            lds.issue(sync, 0, lgb_sb[:], lgb[:])
            lds.issue(sync, 0, b1_sb[:], b1ag[:])
            lds.issue(sync, 0, b2_sb[:], b2[:])

        @block.vector
        def _(vector):
            for i in range(8):
                sx.wait(vector, i)
                sems.inc(vector.tensor_copy(xsb[:, i, 1:TPC + 1], stg[i % 2][:]), "cv")
            lds.wait_all(vector)
            for e in range(8):
                sems.inc(vector.tensor_copy(xsb[:, e, 0:1], xc_f[:, e:e + 1]), "cv")

        @block.vector
        def _(vector):
            sems.wait(vector, "cv")
            sems.inc(vector.tensor_copy(xo[:, 0, 0:1], xsb[:, 0, 0:1]), "lnf")

        ln_bufs = dict(
            y2stg=[ctx.enter_context(nc.sbuf_tensor(f"y2s{i}", [128, 512], BF16)) for i in range(2)],
            rowA=ctx.enter_context(nc.sbuf_tensor("rowA_", [1, 512], F32)),
            rowB=ctx.enter_context(nc.sbuf_tensor("rowB_", [1, 512], F32)),
            rtmp=ctx.enter_context(nc.sbuf_tensor("rtmp_", [1, 512], F32)),
            rowS=ctx.enter_context(nc.sbuf_tensor("rowS_", [1, 512], F32)),
            Ab=[ctx.enter_context(nc.sbuf_tensor(f"Ab_{i}", [128, 512], F32)) for i in range(2)],
            Bb=[ctx.enter_context(nc.sbuf_tensor(f"Bb_{i}", [128, 512], F32)) for i in range(2)],
        )
        for t in range(4):
            def consume(vector, e, tt_, Abuf, Bbuf, t=t):
                if e == 0:
                    sems.wait(vector, "mm")
                sems.inc(vector.scalar_tensor_tensor(hn[:, e, :], xs(e, t * 512, (t + 1) * 512),
                                                     1.0, Abuf[:], ALU.mult, ALU.mult), "gg")
                sems.wait(vector, "gg")
                sems.inc(vector.tensor_tensor(hn[:, e, :], hn[:, e, :], Bbuf[:], ALU.add), "gg")
                sems.wait(vector, "gg")
                sems.inc(vector.tensor_scalar(hn[:, e, :], hn[:, e, :], lgb_sb[:, e:e + 1],
                                              lgb_sb[:, 8 + e:8 + e + 1], ALU.mult, ALU.add), "lnc")

            emit_ln(nc, block, sems, ctx, None, consume,
                    nchunk=8, ntc=1, tw=512, onesc=onesc_b, onesr=onesr, tag="",
                    psum_pool=[pool[0], pool[1], pool[2], pool[3], pool[4], pool[5]],
                    src_cb=lambda e, tt_, t=t: xs(e, t * 512, (t + 1) * 512), y2_dtype=BF16,
                    uniq=f"t{t}", pre_wait=("uev", "ms"), bufs=ln_bufs)

            for part in range(2):
                wsrc = w1a if part == 0 else w1g
                outb = aab if part == 0 else ggb
                for j in range(NJ):
                    gi = (part * NJ + j)
                    pj = gi % 2

                    gidx_w1 = t * 44 + part * NJ + j

                    @block.sync
                    def _(sync):
                        sems.wait(sync, "w1c", max(0, sems.n["w1c"] - 1))
                        w1s.issue(sync, gidx_w1, w1stg[gi % 2][:],
                                  wsrc[:, j * 128:(j + 1) * 128].rearrange("(e p) j -> p e j", p=128))

                    @block.vector
                    def _(vector):
                        w1s.wait(vector, gidx_w1)
                        sems.wait(vector, "mm", max(0, (sems.n["mm"]) - 8))
                        sems.inc(vector.tensor_copy(w1b[gi % 2][:], w1stg[gi % 2][:]), "w1c")

                    @block.tensor
                    def _(tensor):
                        sems.wait(tensor, "w1c")
                        sems.wait(tensor, "lnc")
                        sems.wait(tensor, "ev", max(0, sems.n["ev"] - 1))
                        for e in range(8):
                            sems.inc(tensor.matmul(pool[6 + pj][:], w1b[gi % 2][:, e, :], hn[:, e, :],
                                                   start=(e == 0), stop=(e == 7)), "mm")

                    @block.scalar
                    def _(scalar):
                        sems.wait(scalar, "mm")
                        fn = AF.Identity if part == 0 else AF.Gelu
                        sems.inc(scalar.activation(outb[:, j, :], pool[6 + pj][:], fn,
                                                   bias=b1_sb[:, part * NJ + j:part * NJ + j + 1],
                                                   scale=1.0), "ev")

            @block.vector
            def _(vector):
                sems.wait(vector, "ev")
                for j in range(NJ):
                    sems.inc(vector.tensor_tensor(aab[:, j, :], aab[:, j, :], ggb[:, j, :],
                                                  ALU.mult), "gg")

            for c in range(NJ):
                gidx_w2 = t * NJ + c

                @block.sync
                def _(sync):
                    sems.wait(sync, "w2c", max(0, sems.n["w2c"] - 1))
                    w2s.issue(sync, gidx_w2, w2stg[c % 2][:], w2p[c * 128:(c + 1) * 128, :])

                @block.vector
                def _(vector):
                    w2s.wait(vector, gidx_w2)
                    sems.wait(vector, "up", max(0, sems.n["up"] - 8))
                    sems.inc(vector.tensor_copy(w2b[c % 2][:], w2stg[c % 2][:]), "w2c")

                @block.tensor
                def _(tensor):
                    sems.wait(tensor, "w2c")
                    if c == 0:
                        sems.wait(tensor, "gg")
                        sems.wait(tensor, "uev")
                        sems.wait(tensor, "lnv")
                        sems.wait(tensor, "lnbc")
                    for j in range(8):
                        sems.inc(tensor.matmul(pool[j][:], w2b[c % 2][:, j * 128:(j + 1) * 128],
                                               aab[:, c, :], start=(c == 0), stop=(c == NJ - 1)), "up")

            for j in range(8):
                gidx_x = t * 8 + j

                @block.sync
                def _(sync):
                    sems.wait(sync, "uev", max(0, sems.n["uev"] - 1))
                    xl.issue(sync, gidx_x, xstg[j % 2][:],
                             xT[j * 128:(j + 1) * 128, t * 512:(t + 1) * 512])

                @block.vector
                def _(vector):
                    sems.wait(vector, "up")
                    xl.wait(vector, gidx_x)
                    if j == 0:
                        sems.wait(vector, "st")
                    sems.inc(vector.scalar_tensor_tensor(xo[:, j, :], pool[j][:],
                                                         b2_sb[:, j:j + 1], xstg[j % 2][:],
                                                         ALU.add, ALU.add), "uev")

            @block.gpsimd
            def _(gp):
                sems.wait(gp, "uev")
                for j in range(8):
                    sems.inc(gp.dma_start(out=xoutT[j * 128:(j + 1) * 128, t * 512:(t + 1) * 512],
                                          in_=xo[:, j, :]), "st", 16)
                gp.wait_ge(sems.h["st"], sems.n["st"])

        @block.gpsimd
        def _(gp):
            gp.wait_ge(sems.h["st"], sems.n["st"])


def build_final(nc):
    from contextlib import ExitStack
    ctx = ExitStack()
    x3T = nc.dram_tensor("x3T", [D, TPC], F32, kind="ExternalInput").ap()
    f1T = nc.dram_tensor("f1T", [D, TPC], F32, kind="ExternalInput").ap()
    x4T = nc.dram_tensor("x4T", [D, TPC], F32, kind="ExternalOutput").ap()
    with ctx:
        yf = ctx.enter_context(nc.sbuf_tensor("yf", [128, 8, TPC], F32))
        fstg = [ctx.enter_context(nc.sbuf_tensor(f"fstg{i}", [128, TPC], F32)) for i in range(2)]
        onesc = ctx.enter_context(nc.sbuf_tensor("onesc", [128, 1], F32))
        onesr = ctx.enter_context(nc.sbuf_tensor("onesr", [1, 128], F32))
        block = ctx.enter_context(nc.Block())
        names = ["ms", "ad", "st",
                 "lnf", "lnm", "lnsq", "lnv", "lnst", "lnr", "lnb", "lnbc", "lnc"]
        sems = Sems(nc, ctx, names)
        ld = DmaStream(nc, ctx, "ld", 1)
        fl = DmaStream(nc, ctx, "fl")

        @block.gpsimd
        def _(gp):
            sems.inc(gp.memset(onesc[:], 1.0), "ms")
            sems.inc(gp.memset(onesr[:], 1.0), "ms")

        @block.sync
        def _(sync):
            for e in range(8):
                ld.issue(sync, 0, yf[:, e, :], x3T[e * 128:(e + 1) * 128, :])
            for e in range(8):
                if e >= 2:
                    sync.wait_ge(sems.h["ad"], e - 1)
                fl.issue(sync, e, fstg[e % 2][:], f1T[e * 128:(e + 1) * 128, :])

        @block.vector
        def _(vector):
            ld.wait_all(vector)
            for e in range(8):
                fl.wait(vector, e)
                sems.inc(vector.tensor_tensor(yf[:, e, :], yf[:, e, :], fstg[e % 2][:], ALU.add), "ad")
            sems.wait(vector, "ad")
            sems.inc(vector.tensor_copy(fstg[0][:, 0:1], yf[:, 0, 0:1]), "lnf")

        def consume(vector, e, t, Abuf, Bbuf):
            sems.inc(vector.scalar_tensor_tensor(yf[:, e, t * 512:(t + 1) * 512],
                                                 yf[:, e, t * 512:(t + 1) * 512], 1.0,
                                                 Abuf[:], ALU.mult, ALU.mult), "ad")
            sems.inc(vector.tensor_tensor(yf[:, e, t * 512:(t + 1) * 512],
                                          yf[:, e, t * 512:(t + 1) * 512], Bbuf[:], ALU.add), "lnc")

        emit_ln(nc, block, sems, ctx, yf, consume, onesc=onesc, onesr=onesr)

        @block.gpsimd
        def _(gp):
            sems.wait(gp, "lnc")
            for e in range(8):
                sems.inc(gp.dma_start(out=x4T[e * 128:(e + 1) * 128, :], in_=yf[:, e, :]), "st", 16)
            sems.wait(gp, "st")


def rotary_tables(s0):
    DIM_HEAD, WINDOW, QK = 64, 512, 8.0
    half = np.arange(0, DIM_HEAD, 2, dtype=np.float32)
    inv_freq = 1.0 / 10000.0 ** (half / DIM_HEAD)
    t = np.arange(s0, s0 + TPC, dtype=np.float32)
    fr = t[:, None] * inv_freq[None, :]
    freqs = np.concatenate([fr, fr], axis=-1)
    base = (half + 0.4 * DIM_HEAD) / (1.4 * DIM_HEAD)
    power = (t - S // 2) / float(WINDOW // 2)
    scale = (base[None, :] ** power[:, None]).astype(np.float32)
    scale = np.concatenate([scale, scale], axis=-1)
    cos, sin = np.cos(freqs), np.sin(freqs)
    sgn = np.concatenate([-np.ones(32, np.float32), np.ones(32, np.float32)])
    Cq = (cos * scale * QK).T
    Sq = (sin * scale * QK * sgn[:, None].T).T
    Ck = (cos / scale).T
    Sk = (sin / scale * sgn[:, None].T).T
    out = []
    for tab in (Cq, Sq, Ck, Sk):
        out.append(np.ascontiguousarray(np.concatenate([tab, tab], axis=0).astype(np.float32)))
    return out


def local_masks(first):
    m = np.zeros((4, 128, 1024), np.float32)
    for qt in range(4):
        i = np.arange(128)[:, None] + qt * 128 + 512
        j = np.arange(1024)[None, :]
        bad = (j > i) | (j <= i - 512)
        if first:
            bad = bad | (j < 512)
        m[qt][bad] = -1e30
    return m


def host_prep_l1f_qkv(inputs, li, xT_by_core):
    wqkv = np.ascontiguousarray(inputs["L1_ffn_wqkv"][li].astype(np.float32))
    qs = inputs["L1_ffn_q_scale"][li].astype(np.float32)
    ks = inputs["L1_ffn_k_scale"][li].astype(np.float32)
    qsr = np.ascontiguousarray(np.stack([np.tile(qs, 2), np.tile(ks, 2)], axis=1))
    selmT = np.zeros((8, 4, 128), np.float32)
    for c in range(4):
        selmT[2 * c, c, 0:64] = 1.0
        selmT[2 * c + 1, c, 64:128] = 1.0
    selm = np.ascontiguousarray(selmT.transpose(2, 1, 0))
    per_core = []
    for c in range(NC):
        tabs = np.stack(rotary_tables((c % 4) * TPC), axis=0)
        if c % 4 == 0:
            xcol = np.zeros((128, 8), np.float32)
        else:
            xcol = np.ascontiguousarray(xT_by_core[c - 1][:, -1].reshape(8, 128).T)
        per_core.append({"xT": xT_by_core[c], "xcol8": xcol, "wqkv": wqkv,
                         "qsr": qsr, "tabs": tabs, "selm": selm, "selmT": selmT})
    return per_core


def host_prep_l1f_attn(inputs, li, res_qkv, xT_by_core):
    wo = np.ascontiguousarray(inputs["L1_ffn_wo"][li].astype(np.float32))
    ident = np.eye(128, dtype=BF)
    m0 = local_masks(False)
    m1n = local_masks(False)
    m1f = local_masks(True)
    per_core = []
    for c in range(NC):
        if c % 4 == 0:
            kb4 = np.zeros((128, 4, 512), BF)
            vb4 = np.zeros((128, 4, 520), BF)
            m1 = m1f
        else:
            krN = res_qkv.results[c - 1]["krT"]
            kb4 = np.ascontiguousarray(krN[:, -512:].reshape(4, 128, 512).transpose(1, 0, 2))
            vaN = res_qkv.results[c - 1]["vaug"]
            vb4 = np.ascontiguousarray(vaN[-512:, :].reshape(4, 128, 520).transpose(1, 0, 2))
            m1 = m1n
        per_core.append({"qrT": res_qkv.results[c]["qrT"], "krT": res_qkv.results[c]["krT"],
                         "vaug": res_qkv.results[c]["vaug"], "kb4": kb4, "vb4": vb4,
                         "mask0": m0, "mask1": m1, "xT": xT_by_core[c], "wo": wo, "ident": ident})
    return per_core


def host_prep_l1f_mlp(inputs, li, xpa_by_core):
    HP = 2816
    HID = 2730
    w1 = inputs["L1_ffn_w1"][li].astype(np.float32)
    b1 = inputs["L1_ffn_b1"][li].astype(np.float32)
    w2 = inputs["L1_ffn_w2"][li].astype(np.float32)
    b2 = inputs["L1_ffn_b2"][li].astype(np.float32)
    lg = inputs["L1_ffn_ln_g"][li].astype(np.float32)
    lb = inputs["L1_ffn_ln_b"][li].astype(np.float32)
    w1a = np.zeros((D, HP), np.float32); w1a[:, :HID] = w1[:, :HID]
    w1g = np.zeros((D, HP), np.float32); w1g[:, :HID] = w1[:, HID:]
    b1a = np.zeros(HP, np.float32); b1a[:HID] = b1[:HID]
    b1g = np.zeros(HP, np.float32); b1g[:HID] = b1[HID:]
    b1ag = np.ascontiguousarray(np.concatenate(
        [b1a.reshape(22, 128).T, b1g.reshape(22, 128).T], axis=1))
    w2p = np.zeros((HP, D), np.float32); w2p[:HID] = w2
    lgb = np.ascontiguousarray(np.concatenate(
        [lg.reshape(8, 128).T, lb.reshape(8, 128).T], axis=1))
    b2c = np.ascontiguousarray(b2.reshape(8, 128).T)
    per_core = []
    for c in range(NC):
        if c % 4 == 0:
            xcol = np.zeros((128, 8), np.float32)
        else:
            xcol = np.ascontiguousarray(xpa_by_core[c - 1][:, -1].reshape(8, 128).T)
        per_core.append({"xT": xpa_by_core[c], "xcol8": xcol, "lgb": lgb,
                         "w1a": w1a, "w1g": w1g, "b1ag": b1ag, "w2p": w2p, "b2": b2c})
    return per_core


def build_l1_post(nc):
    build_l0_post(nc, with_pooled=False)


def host_prep_l1_post(inputs, res4, res6):
    wout = np.ascontiguousarray(inputs["L1_attn_Wo"].astype(np.float32).T)
    bout = np.ascontiguousarray(inputs["L1_attn_bo"].astype(np.float32).reshape(8, 128).T)
    per_core = []
    for c in range(NC):
        per_core.append({"aT": res6.results[c]["aT"], "wout": wout, "bout": bout,
                         "xT": res4.results[c]["x2T"]})
    return per_core


def host_prep_l0_qkv(inputs):
    x = inputs["L0_attn_x"]
    wqkv = np.ascontiguousarray(inputs["L0_attn_W_qkv"].astype(np.float32))
    bqkv = inputs["L0_attn_b_qkv"].astype(np.float32)
    bq16 = np.ascontiguousarray(bqkv[:2048].reshape(16, 128).T)
    bv_row = np.ascontiguousarray(bqkv[2048:].reshape(1, 1024))
    per_core = []
    for c in range(NC):
        b, s0 = c // 4, (c % 4) * TPC
        xT = np.ascontiguousarray(x[b, s0:s0 + TPC, :].T.astype(np.float32))
        per_core.append({"xT": xT, "wqkv": wqkv, "bqkv": bq16, "bv_row": bv_row})
    return per_core


def kernel(**inputs):
    r1 = run_stage(build_l0_qkv, host_prep_l0_qkv(inputs))
    r2 = run_stage(build_l0_attn, host_prep_l0_attn(inputs, r1))
    r3 = run_stage(build_l0_post, host_prep_l0_post(inputs, r1, r2))
    r4 = run_stage(build_l0_moe, host_prep_l0_moe(inputs, r3))
    r5 = run_stage(build_l1_qkv, host_prep_l1_qkv(inputs, r4))
    r6 = run_stage(build_l1_attn, host_prep_l1_attn(inputs, r5))
    r7 = run_stage(build_l1_post, host_prep_l1_post(inputs, r4, r6))
    xT = [r7.results[c]["x1T"] for c in range(NC)]
    for li in range(2):
        rq = run_stage(build_l1f_qkv, host_prep_l1f_qkv(inputs, li, xT))
        ra = run_stage(build_l1f_attn, host_prep_l1f_attn(inputs, li, rq, xT))
        xpa = [ra.results[c]["xpaT"] for c in range(NC)]
        rm = run_stage(build_l1f_mlp, host_prep_l1f_mlp(inputs, li, xpa))
        xT = [rm.results[c]["xoutT"] for c in range(NC)]
    x3 = [r7.results[c]["x1T"] for c in range(NC)]
    r9 = run_stage(build_final, [{"x3T": x3[c], "f1T": xT[c]} for c in range(NC)])
    out = np.empty((B, S, D), np.float32)
    for c in range(NC):
        b, s0 = c // 4, (c % 4) * TPC
        out[b, s0:s0 + TPC, :] = r9.results[c]["x4T"].T
    return out


# version 03 (final)
# speedup vs baseline: 1.9619x; 1.2296x over previous
import numpy as np
import ml_dtypes
import concourse.bass as bass
import concourse.mybir as mybir
from concourse.bass_utils import run_bass_kernel_spmd

F32 = mybir.dt.float32
BF16 = mybir.dt.bfloat16
AF = mybir.ActivationFunctionType
ALU = mybir.AluOpType

NC = 8
B = 2
S = 8192
D = 1024
TPC = 2048
H0 = 16
DH = 64

BF = ml_dtypes.bfloat16



class Sems:

    def __init__(self, nc, ctx, names):
        self.h = {}
        self.n = {}
        for nm in names:
            self.h[nm] = ctx.enter_context(nc.semaphore(nm))
            self.n[nm] = 0

    def inc(self, instr, nm, k=1):
        instr.then_inc(self.h[nm], k)
        self.n[nm] += k

    def wait(self, eng, nm, val=None):
        eng.wait_ge(self.h[nm], self.n[nm] if val is None else val)

    def wait_all(self, eng):
        for nm in self.h:
            eng.wait_ge(self.h[nm], self.n[nm])


PARANOID = False


class DmaStream:

    def __init__(self, nc, ctx, name, nbuf=2):
        self.s = [ctx.enter_context(nc.semaphore(f"{name}{i}")) for i in range(nbuf)]
        self.cnt = [0] * nbuf
        self.nbuf = nbuf

    def issue(self, qeng, idx, out, in_):
        p = idx % self.nbuf
        qeng.dma_start(out=out, in_=in_).then_inc(self.s[p], 16)
        self.cnt[p] += 16

    def wait(self, eng, idx):
        p = idx % self.nbuf
        eng.wait_ge(self.s[p], 16 * (idx // self.nbuf + 1))

    def wait_all(self, eng):
        for p in range(self.nbuf):
            if self.cnt[p]:
                eng.wait_ge(self.s[p], self.cnt[p])


def pwait(sems, eng):
    if PARANOID:
        sems.wait_all(eng)


def run_stage(build, per_core_inputs, trace=False):
    nc = bass.Bass("TRN2", target_bir_lowering=False)
    build(nc)
    res = run_bass_kernel_spmd(nc, per_core_inputs, core_ids=list(range(NC)), trace=trace)
    return res



def build_l0_qkv(nc, q_scale=1.0):
    from contextlib import ExitStack
    ctx = ExitStack()
    xT = nc.dram_tensor("xT", [D, TPC], F32, kind="ExternalInput").ap()
    wqkv = nc.dram_tensor("wqkv", [D, 3 * D], F32, kind="ExternalInput").ap()
    bqkv = nc.dram_tensor("bqkv", [128, 16], F32, kind="ExternalInput").ap()
    bv_row = nc.dram_tensor("bv_row", [1, D], F32, kind="ExternalInput").ap()
    qT = nc.dram_tensor("qT", [D, TPC], BF16, kind="ExternalOutput").ap()
    kT = nc.dram_tensor("kT", [D, TPC], BF16, kind="ExternalOutput").ap()
    vaug = nc.dram_tensor("vaug", [TPC, 16 * 65], BF16, kind="ExternalOutput").ap()

    with ctx:
        xb = ctx.enter_context(nc.sbuf_tensor("xb", [128, 8, TPC], BF16))
        wb = ctx.enter_context(nc.sbuf_tensor("wb", [128, 8, 3 * D], BF16))
        stg = [ctx.enter_context(nc.sbuf_tensor(f"stg{i}", [128, 3 * D], F32)) for i in range(2)]
        bq_sb = ctx.enter_context(nc.sbuf_tensor("bq_sb", [128, 16], F32))
        bvr_f = ctx.enter_context(nc.sbuf_tensor("bvr_f", [1, D], F32))
        bvr_b = ctx.enter_context(nc.sbuf_tensor("bvr_b", [1, D], BF16))
        ones1 = ctx.enter_context(nc.sbuf_tensor("ones1", [1, 128], BF16))
        qk_st = [ctx.enter_context(nc.sbuf_tensor(f"qk_st{i}", [128, TPC], BF16)) for i in range(2)]
        va = ctx.enter_context(nc.sbuf_tensor("va", [128, 16, 1040], BF16))
        ps = [ctx.enter_context(nc.psum_tensor(f"ps{i}", [128, 512], F32)) for i in range(2)]
        block = ctx.enter_context(nc.Block())
        sems = Sems(nc, ctx, ["cv", "ms", "mm", "ev", "vst"])
        ld = DmaStream(nc, ctx, "ld")
        lds = DmaStream(nc, ctx, "lds", 1)
        stp = DmaStream(nc, ctx, "stp")

        NCHUNK = 16

        @block.sync
        def _(sync):
            for i in range(NCHUNK):
                if i >= 2:
                    sync.wait_ge(sems.h["cv"], i - 1)
                if i % 2 == 0:
                    e = i // 2
                    ld.issue(sync, i, stg[i % 2][:, :TPC], xT[e * 128:(e + 1) * 128, :])
                else:
                    e = i // 2
                    ld.issue(sync, i, stg[i % 2][:, :], wqkv[e * 128:(e + 1) * 128, :])
            lds.issue(sync, 0, bq_sb[:], bqkv[:])
            lds.issue(sync, 0, bvr_f[:], bv_row[:])

        @block.vector
        def _(vector):
            for i in range(NCHUNK):
                ld.wait(vector, i)
                e = i // 2
                if i % 2 == 0:
                    sems.inc(vector.tensor_copy(xb[:, e, :], stg[i % 2][:, :TPC]), "cv")
                else:
                    sems.inc(vector.tensor_copy(wb[:, e, :], stg[i % 2][:, :]), "cv")
            lds.wait_all(vector)
            sems.inc(vector.tensor_copy(bvr_b[:], bvr_f[:]), "cv")
            if q_scale != 1.0:
                sems.inc(vector.tensor_scalar(bq_sb[:, 0:8], bq_sb[:, 0:8], q_scale, None, ALU.mult), "cv")

        @block.gpsimd
        def _(gp):
            sems.inc(gp.memset(ones1[:], 1.0), "ms")
            va_h = va[:].rearrange("p t (h c) -> p t h c", c=65)
            sems.inc(gp.memset(va_h[:, :, :, 64:65], 1.0), "ms")

        @block.tensor
        def _(tensor):
            tensor.wait_ge(sems.h["cv"], sems.n["cv"])
            tensor.wait_ge(sems.h["ms"], 1)
            nmm = 0
            for j in range(16):
                for t in range(4):
                    pj = (j * 4 + t) % 2
                    if PARANOID:
                        tensor.wait_ge(sems.h["ev"], 4 * j + t)
                    if j * 4 + t >= 2:
                        tensor.wait_ge(sems.h["ev"], j * 4 + t - 1)
                    for e in range(8):
                        sems.inc(tensor.matmul(ps[pj][:], wb[:, e, j * 128:(j + 1) * 128],
                                               xb[:, e, t * 512:(t + 1) * 512],
                                               start=(e == 0), stop=(e == 7)), "mm")
            for tt in range(16):
                for hf in range(2):
                    i = 64 + tt * 2 + hf
                    pj = i % 2
                    tensor.wait_ge(sems.h["ev"], i - 1)
                    for e in range(8):
                        sems.inc(tensor.matmul(ps[pj][:], xb[:, e, tt * 128:(tt + 1) * 128],
                                               wb[:, e, 2048 + hf * 512: 2048 + (hf + 1) * 512],
                                               start=(e == 0), stop=False), "mm")
                    sems.inc(tensor.matmul(ps[pj][:, :], ones1[:, :], bvr_b[:, hf * 512:(hf + 1) * 512],
                                           start=False, stop=True), "mm")

        @block.scalar
        def _(scalar):
            for j in range(16):
                for t in range(4):
                    pj = (j * 4 + t) % 2
                    if PARANOID:
                        scalar.wait_ge(sems.h["mm"], sems.n["mm"])
                        scalar.wait_ge(sems.h["st"], sems.n["st"] and 0)
                    scalar.wait_ge(sems.h["mm"], 8 * (j * 4 + t + 1))
                    dst = qk_st[j % 2][:, t * 512:(t + 1) * 512]
                    if t == 0 and j >= 2:
                        stp.wait(scalar, j - 2)
                    sems.inc(scalar.activation(dst, ps[pj][:], AF.Identity,
                                               bias=bq_sb[:, j:j + 1],
                                               scale=(q_scale if j < 8 else 1.0)), "ev")
            for tt in range(16):
                for hf in range(2):
                    i = 64 + tt * 2 + hf
                    scalar.wait_ge(sems.h["mm"], 8 * 64 + 9 * (tt * 2 + hf + 1))
                    va_h = va[:, tt, :].rearrange("p (h c) -> p h c", c=65)
                    sems.inc(scalar.activation(va_h[:, hf * 8:(hf + 1) * 8, 0:64], ps[i % 2][:],
                                               AF.Identity, bias=0.0, scale=1.0), "ev")

        @block.gpsimd
        def _(gp):
            for j in range(16):
                gp.wait_ge(sems.h["ev"], 4 * (j + 1))
                dram = qT if j < 8 else kT
                r = (j % 8) * 128
                stp.issue(gp, j, dram[r:r + 128, :], qk_st[j % 2][:, :])
            gp.wait_ge(sems.h["ev"], 64 + 32)
            sems.inc(gp.dma_start(out=vaug[:].rearrange("(t p) c -> p t c", p=128), in_=va[:]), "vst", 16)
            stp.wait_all(gp)
            gp.wait_ge(sems.h["vst"], 16)



def build_l1_qkv(nc):
    build_l0_qkv(nc, q_scale=1.0 / 8.0)


def host_prep_l1_qkv(inputs, res4):
    wq = inputs["L1_attn_Wq"].astype(np.float32).T
    wk = inputs["L1_attn_Wk"].astype(np.float32).T
    wv = inputs["L1_attn_Wv"].astype(np.float32).T
    wcat = np.ascontiguousarray(np.concatenate([wq, wk, wv], axis=1))
    bq = inputs["L1_attn_bq"].astype(np.float32)
    bk = inputs["L1_attn_bk"].astype(np.float32)
    bv = inputs["L1_attn_bv"].astype(np.float32)
    b16 = np.ascontiguousarray(np.concatenate([bq, bk]).reshape(16, 128).T)
    bv_row = np.ascontiguousarray(bv.reshape(1, 1024))
    per_core = []
    for c in range(NC):
        per_core.append({"xT": res4.results[c]["x2T"], "wqkv": wcat, "bqkv": b16, "bv_row": bv_row})
    return per_core


def build_l0_attn(nc):
    from contextlib import ExitStack
    ctx = ExitStack()
    qTd = nc.dram_tensor("qT", [D, TPC], BF16, kind="ExternalInput").ap()
    kTf = nc.dram_tensor("kTf", [D, S], BF16, kind="ExternalInput").ap()
    vaugf = nc.dram_tensor("vaugf", [S, 16 * 65], BF16, kind="ExternalInput").ap()
    aT = nc.dram_tensor("aT", [D, TPC], BF16, kind="ExternalOutput").ap()

    NKT = S // 128
    with ctx:
        qb = ctx.enter_context(nc.sbuf_tensor("qb", [128, 8, TPC], BF16))
        kt4 = [ctx.enter_context(nc.sbuf_tensor(f"kt{i}", [128, 512], BF16)) for i in range(2)]
        vt4 = [ctx.enter_context(nc.sbuf_tensor(f"vt{i}", [128, 4, 130], BF16)) for i in range(2)]
        wA = [ctx.enter_context(nc.sbuf_tensor(f"wA{i}", [128, 512], BF16)) for i in range(2)]
        wB = [ctx.enter_context(nc.sbuf_tensor(f"wB{i}", [128, 512], BF16)) for i in range(2)]
        rA = ctx.enter_context(nc.sbuf_tensor("rA", [1, 512], F32))
        rB = ctx.enter_context(nc.sbuf_tensor("rB", [1, 512], F32))
        bA_sb = ctx.enter_context(nc.sbuf_tensor("bA_sb", [64, 512], F32))
        bB_sb = ctx.enter_context(nc.sbuf_tensor("bB_sb", [64, 512], F32))
        ones64 = ctx.enter_context(nc.sbuf_tensor("ones64", [1, 64], F32))
        at_st = [ctx.enter_context(nc.sbuf_tensor(f"at_st{i}", [128, TPC], BF16)) for i in range(2)]
        psA = [ctx.enter_context(nc.psum_tensor(f"psA{i}", [128, 512], F32)) for i in range(2)]
        psB = [ctx.enter_context(nc.psum_tensor(f"psB{i}", [128, 512], F32)) for i in range(2)]
        oA = ctx.enter_context(nc.psum_tensor("oA", [65, 512], F32))
        oB = ctx.enter_context(nc.psum_tensor("oB", [65, 512], F32))
        bA_ps = ctx.enter_context(nc.psum_tensor("bA_ps", [64, 512], F32))
        bB_ps = ctx.enter_context(nc.psum_tensor("bB_ps", [64, 512], F32))
        block = ctx.enter_context(nc.Block())
        sems = Sems(nc, ctx, ["ld", "tm", "ev", "av", "rc", "tb", "bc", "no", "ms"])
        kv = DmaStream(nc, ctx, "kv")
        stp = DmaStream(nc, ctx, "stp")

        def g_of(hp, qc, kt):
            return (hp * 4 + qc) * NKT + kt

        @block.gpsimd
        def _(gp):
            sems.inc(gp.memset(ones64[:], 1.0), "ms")

        @block.sync
        def _(sync):
            for e in range(8):
                sems.inc(sync.dma_start(out=qb[:, e, :], in_=qTd[e * 128:(e + 1) * 128, :]), "ld", 16)
            for hp in range(8):
                for qc in range(4):
                    for kg in range(NKT // 4):
                        G = (hp * 4 + qc) * (NKT // 4) + kg
                        if G >= 2:
                            sync.wait_ge(sems.h["av"], 8 * (G - 1))
                        p = G % 2
                        sync.dma_start(out=kt4[p][:],
                                       in_=kTf[hp * 128:(hp + 1) * 128,
                                               kg * 512:(kg + 1) * 512]).then_inc(kv.s[p], 16)
                        va4 = vaugf[kg * 512:(kg + 1) * 512,
                                    130 * hp:130 * (hp + 1)].rearrange("(t p) c -> p t c", p=128)
                        sync.dma_start(out=vt4[p][:], in_=va4).then_inc(kv.s[p], 16)
                        kv.cnt[p] += 32

        @block.tensor
        def _(tensor):
            def emit_av(hp, qc, kt):
                u = hp * 4 + qc
                g = g_of(hp, qc, kt)
                p = (g // 4) % 2
                tensor.wait_ge(sems.h["ev"], 2 * (g + 1))
                if kt == 0 and u >= 1:
                    tensor.wait_ge(sems.h["no"], 2 * u)
                sems.inc(tensor.matmul(oA[:], vt4[p][:, kt % 4, 0:65], wA[g % 2][:],
                                       start=(kt == 0), stop=(kt == NKT - 1)), "av")
                sems.inc(tensor.matmul(oB[:], vt4[p][:, kt % 4, 65:130], wB[g % 2][:],
                                       start=(kt == 0), stop=(kt == NKT - 1)), "av")

            tensor.wait_ge(sems.h["ld"], 8 * 16)
            for hp in range(8):
                for qc in range(4):
                    u = hp * 4 + qc
                    for kt in range(NKT):
                        g = g_of(hp, qc, kt)
                        G = g // 4
                        p = G % 2
                        if kt % 4 == 0:
                            tensor.wait_ge(kv.s[p], 32 * (G // 2 + 1))
                        if g >= 2:
                            tensor.wait_ge(sems.h["ev"], 2 * (g - 1))
                        sems.inc(tensor.matmul(psA[g % 2][:], kt4[p][0:64, (kt % 4) * 128:(kt % 4 + 1) * 128],
                                               qb[0:64, hp, qc * 512:(qc + 1) * 512],
                                               start=True, stop=True, tile_position=(0, 0)), "tm")
                        sems.inc(tensor.matmul(psB[g % 2][:], kt4[p][64:128, (kt % 4) * 128:(kt % 4 + 1) * 128],
                                               qb[64:128, hp, qc * 512:(qc + 1) * 512],
                                               start=True, stop=True, tile_position=(64, 0)), "tm")
                        if kt >= 1:
                            emit_av(hp, qc, kt - 1)
                    emit_av(hp, qc, NKT - 1)
                    tensor.wait_ge(sems.h["rc"], 2 * (u + 1))
                    tensor.wait_ge(sems.h["ms"], 1)
                    if u >= 1:
                        tensor.wait_ge(sems.h["bc"], 2 * u)
                    sems.inc(tensor.matmul(bA_ps[:], ones64[:], rA[:], start=True, stop=True), "tb")
                    sems.inc(tensor.matmul(bB_ps[:], ones64[:], rB[:], start=True, stop=True), "tb")

        @block.scalar
        def _(scalar):
            for hp in range(8):
                for qc in range(4):
                    u = hp * 4 + qc
                    for kt in range(NKT):
                        g = g_of(hp, qc, kt)
                        scalar.wait_ge(sems.h["tm"], 2 * (g + 1))
                        if g >= 2:
                            scalar.wait_ge(sems.h["av"], 2 * (g - 1))
                        sems.inc(scalar.activation(wA[g % 2][:], psA[g % 2][:], AF.Exp, bias=0.0, scale=1.0), "ev")
                        sems.inc(scalar.activation(wB[g % 2][:], psB[g % 2][:], AF.Exp, bias=0.0, scale=1.0), "ev")
                    scalar.wait_ge(sems.h["tb"], 2 * (u + 1))
                    if u >= 1:
                        scalar.wait_ge(sems.h["no"], 2 * u)
                    sems.inc(scalar.copy(bA_sb[:], bA_ps[:]), "bc")
                    sems.inc(scalar.copy(bB_sb[:], bB_ps[:]), "bc")

        @block.vector
        def _(vector):
            for hp in range(8):
                for qc in range(4):
                    u = hp * 4 + qc
                    vector.wait_ge(sems.h["av"], 2 * NKT * (u + 1))
                    sems.inc(vector.reciprocal(rA[:], oA[64:65, :]), "rc")
                    sems.inc(vector.reciprocal(rB[:], oB[64:65, :]), "rc")
                    vector.wait_ge(sems.h["bc"], 2 * (u + 1))
                    if qc == 0 and hp >= 2:
                        stp.wait(vector, hp - 2)
                    sems.inc(vector.scalar_tensor_tensor(
                        at_st[hp % 2][0:64, qc * 512:(qc + 1) * 512], oA[0:64, :], 1.0 / 32.0,
                        bA_sb[:], ALU.mult, ALU.mult), "no")
                    sems.inc(vector.scalar_tensor_tensor(
                        at_st[hp % 2][64:128, qc * 512:(qc + 1) * 512], oB[0:64, :], 1.0 / 32.0,
                        bB_sb[:], ALU.mult, ALU.mult), "no")

        @block.gpsimd
        def _(gp):
            for hp in range(8):
                gp.wait_ge(sems.h["no"], 2 * (hp * 4 + 4))
                stp.issue(gp, hp, aT[hp * 128:(hp + 1) * 128, :], at_st[hp % 2][:])
            stp.wait_all(gp)


def host_prep_l0_attn(inputs, stage1_res):
    per_core = []
    kT_by_batch = []
    va_by_batch = []
    for b in range(B):
        kT_by_batch.append(np.concatenate(
            [stage1_res.results[b * 4 + i]["kT"] for i in range(4)], axis=1))
        va_by_batch.append(np.concatenate(
            [stage1_res.results[b * 4 + i]["vaug"] for i in range(4)], axis=0))
    for c in range(NC):
        b = c // 4
        per_core.append({"qT": stage1_res.results[c]["qT"],
                         "kTf": kT_by_batch[b], "vaugf": va_by_batch[b]})
    return per_core



def build_l1_attn(nc):
    from contextlib import ExitStack
    ctx = ExitStack()
    qTd = nc.dram_tensor("qT", [D, TPC], BF16, kind="ExternalInput").ap()
    kTf = nc.dram_tensor("kTf", [D, S], BF16, kind="ExternalInput").ap()
    vaugf = nc.dram_tensor("vaugf", [S, 16 * 65], BF16, kind="ExternalInput").ap()
    ident = nc.dram_tensor("ident", [128, 128], BF16, kind="ExternalInput").ap()
    aT = nc.dram_tensor("aT", [D, TPC], BF16, kind="ExternalOutput").ap()

    NEG = -3.0e38
    NU = 256
    with ctx:
        qb = ctx.enter_context(nc.sbuf_tensor("qb", [128, 8, TPC], BF16))
        Kres = ctx.enter_context(nc.sbuf_tensor("Kres", [128, S], BF16))
        Vres = ctx.enter_context(nc.sbuf_tensor("Vres", [128, 64, 130], BF16))
        scA = ctx.enter_context(nc.sbuf_tensor("scA", [128, S], BF16))
        scB = ctx.enter_context(nc.sbuf_tensor("scB", [128, S], BF16))
        tmpA = ctx.enter_context(nc.sbuf_tensor("tmpA", [128, S], BF16))
        tmpB = ctx.enter_context(nc.sbuf_tensor("tmpB", [128, S], BF16))
        wm = [ctx.enter_context(nc.sbuf_tensor(f"wm{j}", [128, S], BF16)) for j in range(2)]
        t8 = ctx.enter_context(nc.sbuf_tensor("t8", [128, 8], BF16))
        nt = ctx.enter_context(nc.sbuf_tensor("nt", [128, 2], F32))
        vscr = ctx.enter_context(nc.sbuf_tensor("vscr", [1, 128], F32))
        wTb = [ctx.enter_context(nc.sbuf_tensor(f"wTb{j}", [128, 512], BF16)) for j in range(2)]
        id_sb = ctx.enter_context(nc.sbuf_tensor("id_sb", [128, 128], BF16))
        rA = ctx.enter_context(nc.sbuf_tensor("rA", [1, 128], F32))
        bA_sb = ctx.enter_context(nc.sbuf_tensor("bA_sb", [64, 128], F32))
        ones64 = ctx.enter_context(nc.sbuf_tensor("ones64", [1, 64], F32))
        at_st = [ctx.enter_context(nc.sbuf_tensor(f"at_st{j}", [128, TPC], BF16)) for j in range(2)]
        psA = [ctx.enter_context(nc.psum_tensor(f"psA{j}", [128, 512], F32)) for j in range(2)]
        psB = [ctx.enter_context(nc.psum_tensor(f"psB{j}", [128, 512], F32)) for j in range(2)]
        tp_ps = [ctx.enter_context(nc.psum_tensor(f"tp_ps{j}", [128, 512], BF16)) for j in range(2)]
        o_ps = ctx.enter_context(nc.psum_tensor("o_ps", [65, 256], F32))
        b_ps = ctx.enter_context(nc.psum_tensor("b_ps", [64, 128], F32))
        block = ctx.enter_context(nc.Block())
        sems = Sems(nc, ctx, ["ld", "kv", "ms", "mm", "ev", "xt", "ex", "mk",
                              "tp", "tv", "av", "rc", "tb", "bc", "no"])
        stp = DmaStream(nc, ctx, "stp")
        gkb = [0]

        @block.gpsimd
        def _(gp):
            sems.inc(gp.memset(ones64[:], 1.0), "ms")

        @block.sync
        def _(sync):
            for e in range(8):
                sems.inc(sync.dma_start(out=qb[:, e, :], in_=qTd[e * 128:(e + 1) * 128, :]), "ld", 16)
            sems.inc(sync.dma_start(out=id_sb[:], in_=ident[:]), "ld", 16)

        def emit_kv_load(hp):
            @block.sync
            def _(sync):
                if hp > 0:
                    sync.wait_ge(sems.h["no"], 32 * hp)
                sems.inc(sync.dma_start(out=Kres[:], in_=kTf[hp * 128:(hp + 1) * 128, :]), "kv", 16)
                va_src = vaugf[:, 130 * hp:130 * (hp + 1)].rearrange("(kt p) c -> p kt c", p=128)
                for j2 in range(8):
                    sems.inc(sync.dma_start(out=Vres[:, j2 * 8:(j2 + 1) * 8, :],
                                            in_=va_src[:, j2 * 8:(j2 + 1) * 8, :]), "kv", 16)

        def emit_scores(p):
            hp, qt = p // 16, p % 16
            @block.tensor
            def _(tensor):
                if p == 0:
                    tensor.wait_ge(sems.h["ld"], 16 * 9)
                if qt == 0:
                    tensor.wait_ge(sems.h["kv"], 144 * (hp + 1))
                for kc in range(16):
                    tensor.wait_ge(sems.h["ev"], 32 * p + (2 * (kc - 1) if kc >= 2 else 0))
                    sems.inc(tensor.matmul(psA[kc % 2][:], qb[0:64, hp, qt * 128:(qt + 1) * 128],
                                           Kres[0:64, kc * 512:(kc + 1) * 512],
                                           start=True, stop=True, tile_position=(0, 0)), "mm")
                    sems.inc(tensor.matmul(psB[kc % 2][:], qb[64:128, hp, qt * 128:(qt + 1) * 128],
                                           Kres[64:128, kc * 512:(kc + 1) * 512],
                                           start=True, stop=True, tile_position=(64, 0)), "mm")

        def emit_evac(p):
            @block.scalar
            def _(scalar):
                for kc in range(16):
                    scalar.wait_ge(sems.h["mm"], 32 * p + 2 * (kc + 1))
                    if kc == 0 and p > 0:
                        scalar.wait_ge(sems.h["ex"], 2 * p)
                    sems.inc(scalar.activation(scA[:, kc * 512:(kc + 1) * 512], psA[kc % 2][:],
                                               AF.Identity, bias=0.0, scale=1.0), "ev")
                    sems.inc(scalar.activation(scB[:, kc * 512:(kc + 1) * 512], psB[kc % 2][:],
                                               AF.Identity, bias=0.0, scale=1.0), "ev")

        def emit_sel(i):
            p = i // 2
            sc = scA if i % 2 == 0 else scB
            @block.vector
            def _(vector):
                vector.wait_ge(sems.h["ev"], 32 * (p + 1))
                sems.inc(vector.max(t8[:], sc[:]), "xt")
                vector.wait_ge(sems.h["xt"], sems.n["xt"])
                if i > 0:
                    vector.wait_ge(sems.h["ex"], i)
                sems.inc(vector.match_replace(tmpA[:], t8[:], sc[:], NEG), "xt")
                vector.wait_ge(sems.h["xt"], sems.n["xt"])
                sems.inc(vector.max(t8[:], tmpA[:]), "xt")
                vector.wait_ge(sems.h["xt"], sems.n["xt"])
                sems.inc(vector.match_replace(tmpB[:], t8[:], tmpA[:], NEG), "xt")
                vector.wait_ge(sems.h["xt"], sems.n["xt"])
                sems.inc(vector.max(t8[:], tmpB[:]), "xt")
                vector.wait_ge(sems.h["xt"], sems.n["xt"])
                sems.inc(vector.match_replace(tmpA[:], t8[:], tmpB[:], NEG), "xt")
                vector.wait_ge(sems.h["xt"], sems.n["xt"])
                sems.inc(vector.max(t8[:], tmpA[:]), "xt")
                vector.wait_ge(sems.h["xt"], sems.n["xt"])
                sems.inc(vector.tensor_scalar(nt[:, i % 2:i % 2 + 1], t8[:, 7:8],
                                              -1.0, None, ALU.mult), "xt")

        def emit_exp(i):
            sc = scA if i % 2 == 0 else scB
            @block.scalar
            def _(scalar):
                scalar.wait_ge(sems.h["xt"], 8 * (i + 1))
                if i > 0:
                    scalar.wait_ge(sems.h["mk"], i)
                sems.inc(scalar.activation(tmpA[:], sc[:], AF.Exp,
                                           bias=nt[:, i % 2:i % 2 + 1], scale=1.0), "ex")

        def emit_mask(i):
            @block.vector
            def _(vector):
                vector.wait_ge(sems.h["ex"], i + 1)
                if i >= 2:
                    vector.wait_ge(sems.h["tp"], 64 * (i - 1))
                sems.inc(vector.scalar_tensor_tensor(wm[i % 2][:], tmpA[:], 1.0, tmpA[:],
                                                     ALU.is_ge, ALU.mult), "mk")

        def emit_av(u):
            head = u % 2
            vlo, vhi = (0, 65) if head == 0 else (65, 130)
            par = u % 2

            def emit_T(kb, g):
                @block.tensor
                def _(tensor):
                    if kb == 0:
                        tensor.wait_ge(sems.h["mk"], u + 1)
                    if g >= 2:
                        tensor.wait_ge(sems.h["tv"], g - 1)
                    for j2 in range(4):
                        sems.inc(tensor.transpose(tp_ps[g % 2][:, j2 * 128:(j2 + 1) * 128],
                                                  wm[par][:, (kb * 4 + j2) * 128:(kb * 4 + j2 + 1) * 128],
                                                  id_sb[:]), "tp")

            def emit_copy(kb, g):
                @block.scalar
                def _(scalar):
                    scalar.wait_ge(sems.h["tp"], 4 * (g + 1))
                    if g >= 2:
                        scalar.wait_ge(sems.h["av"], 4 * (g - 1))
                    sems.inc(scalar.copy(wTb[g % 2][:], tp_ps[g % 2][:]), "tv")

            def emit_AV(kb, g):
                @block.tensor
                def _(tensor):
                    tensor.wait_ge(sems.h["tv"], g + 1)
                    if kb == 0 and u >= 2:
                        tensor.wait_ge(sems.h["no"], u - 1)
                    for j2 in range(4):
                        sems.inc(tensor.matmul(o_ps[:, par * 128:(par + 1) * 128],
                                               Vres[:, kb * 4 + j2, vlo:vhi],
                                               wTb[g % 2][:, j2 * 128:(j2 + 1) * 128],
                                               start=(kb == 0 and j2 == 0),
                                               stop=(kb == 15 and j2 == 3)), "av")

            gs = []
            for kb in range(16):
                g = gkb[0]; gkb[0] += 1; gs.append(g)
                emit_T(kb, g)
                emit_copy(kb, g)
                if kb >= 1:
                    emit_AV(kb - 1, gs[kb - 1])
            emit_AV(15, gs[15])

        def emit_norm(u):
            head = u % 2
            par = u % 2
            hp, qt = u // 32, (u // 2) % 16
            prow = slice(0, 64) if head == 0 else slice(64, 128)

            @block.vector
            def _(vector):
                vector.wait_ge(sems.h["av"], 64 * (u + 1))
                if u >= 1:
                    vector.wait_ge(sems.h["tb"], u)
                sems.inc(vector.reciprocal(rA[:], o_ps[64:65, par * 128:(par + 1) * 128]), "rc")
                vector.wait_ge(sems.h["rc"], sems.n["rc"])
                sems.inc(vector.tensor_copy(vscr[0:1, :], rA[:]), "rc")

            @block.tensor
            def _(tensor):
                tensor.wait_ge(sems.h["rc"], 2 * (u + 1))
                if u == 0:
                    tensor.wait_ge(sems.h["ms"], 1)
                if u >= 1:
                    tensor.wait_ge(sems.h["bc"], u)
                sems.inc(tensor.matmul(b_ps[:], ones64[:], rA[:], start=True, stop=True), "tb")

            @block.scalar
            def _(scalar):
                scalar.wait_ge(sems.h["tb"], u + 1)
                if u >= 1:
                    scalar.wait_ge(sems.h["no"], u)
                sems.inc(scalar.copy(bA_sb[:], b_ps[:]), "bc")

            @block.vector
            def _(vector):
                vector.wait_ge(sems.h["bc"], u + 1)
                if u % 32 == 0 and hp >= 2:
                    stp.wait(vector, hp - 2)
                sems.inc(vector.scalar_tensor_tensor(
                    at_st[hp % 2][prow, qt * 128:(qt + 1) * 128],
                    o_ps[0:64, par * 128:(par + 1) * 128], 1.0,
                    bA_sb[:], ALU.mult, ALU.mult), "no")

            if u % 32 == 31:
                @block.gpsimd
                def _(gp):
                    gp.wait_ge(sems.h["no"], 32 * (hp + 1))
                    stp.issue(gp, hp, aT[hp * 128:(hp + 1) * 128, :], at_st[hp % 2][:])

        emit_kv_load(0)
        emit_scores(0)
        emit_evac(0)

        for i in range(NU):
            p, head = i // 2, i % 2
            boundary = (i % 32 == 0 and i > 0)
            if boundary:
                emit_av(i - 1)
                emit_norm(i - 1)
                emit_kv_load(i // 32)
                emit_scores(p)
            emit_sel(i)
            if head == 0 and p >= 1:
                emit_evac(p)
            if i >= 1 and not boundary:
                emit_av(i - 1)
                emit_norm(i - 1)
            if head == 1 and (p + 1) < 128 and (p + 1) % 16 != 0:
                emit_scores(p + 1)
            emit_exp(i)
            emit_mask(i)

        emit_av(NU - 1)
        emit_norm(NU - 1)

        @block.gpsimd
        def _(gp):
            stp.wait_all(gp)


def host_prep_l1_attn(inputs, res5):
    per_core = []
    kT_by_batch, va_by_batch = [], []
    for b in range(B):
        kT_by_batch.append(np.concatenate(
            [res5.results[b * 4 + i]["kT"] for i in range(4)], axis=1))
        va_by_batch.append(np.concatenate(
            [res5.results[b * 4 + i]["vaug"] for i in range(4)], axis=0))
    ident = np.eye(128, dtype=BF)
    for c in range(NC):
        b = c // 4
        per_core.append({"qT": res5.results[c]["qT"], "kTf": kT_by_batch[b],
                         "vaugf": va_by_batch[b], "ident": ident})
    return per_core



def emit_ln(nc, block, sems, ctx, yf, x_out_cb, eps=1e-5, nchunk=8, ntc=4, tw=512,
            onesc=None, onesr=None, tag="", psum_pool=None, src_cb=None, y2_dtype=F32,
            uniq="", pre_wait=(), bufs=None):
    D_ = 128 * nchunk
    if psum_pool is None:
        sum_ps = [ctx.enter_context(nc.psum_tensor(f"sum_ps{tag}{i}", [1, 512], F32)) for i in range(2)]
        sum2_ps = [ctx.enter_context(nc.psum_tensor(f"sum2_ps{tag}{i}", [1, 512], F32)) for i in range(2)]
        ab_full = ctx.enter_context(nc.psum_tensor(f"ab_ps{tag}", [128, 2, 512], F32))
        abA, abB = ab_full[:, 0, :], ab_full[:, 1, :]
    else:
        sum_ps = [psum_pool[0][0:1, :], psum_pool[1][0:1, :]]
        sum2_ps = [psum_pool[2][0:1, :], psum_pool[3][0:1, :]]
        abA, abB = psum_pool[4][:, :], psum_pool[5][:, :]
    if bufs is None:
        y2stg = [ctx.enter_context(nc.sbuf_tensor(f"y2stg{tag}{uniq}{i}", [128, 512], y2_dtype)) for i in range(2)]
        rowA = ctx.enter_context(nc.sbuf_tensor(f"rowA{tag}{uniq}", [1, 512], F32))
        rowB = ctx.enter_context(nc.sbuf_tensor(f"rowB{tag}{uniq}", [1, 512], F32))
        rtmp = ctx.enter_context(nc.sbuf_tensor(f"rtmp{tag}{uniq}", [1, 512], F32))
        rowS = ctx.enter_context(nc.sbuf_tensor(f"rowS{tag}{uniq}", [1, 512], F32))
        Ab = [ctx.enter_context(nc.sbuf_tensor(f"Ab{tag}{uniq}{i}", [128, 512], F32)) for i in range(2)]
        Bb = [ctx.enter_context(nc.sbuf_tensor(f"Bb{tag}{uniq}{i}", [128, 512], F32)) for i in range(2)]
        bufs = dict(y2stg=y2stg, rowA=rowA, rowB=rowB, rtmp=rtmp, rowS=rowS, Ab=Ab, Bb=Bb)
    else:
        y2stg, rowA, rowB, rtmp, rowS, Ab, Bb = (bufs["y2stg"], bufs["rowA"], bufs["rowB"],
                                                 bufs["rtmp"], bufs["rowS"], bufs["Ab"], bufs["Bb"])

    if src_cb is None:
        src_cb = lambda e, t: yf[:, e, t * tw:(t + 1) * tw]
    onesc_b = None
    for t in range(ntc):
        pt = t % 2
        @block.tensor
        def _(tensor):
            sems.wait(tensor, f"lnf{tag}")
            sems.wait(tensor, f"lnv{tag}")
            for pw in pre_wait:
                sems.wait(tensor, pw)
            for e in range(nchunk):
                sems.inc(tensor.matmul(sum_ps[pt][:, :tw], onesc[:], src_cb(e, t),
                                       start=(e == 0), stop=(e == nchunk - 1)), f"lnm{tag}")

        for e in range(nchunk):
            pe = e % 2
            @block.scalar
            def _(scalar):
                sems.wait(scalar, f"lnm{tag}")
                sems.inc(scalar.activation(y2stg[pe][:, :tw], src_cb(e, t),
                                           AF.Square, bias=0.0, scale=1.0), f"lnsq{tag}")

            @block.tensor
            def _(tensor):
                sems.wait(tensor, f"lnsq{tag}")
                sems.inc(tensor.matmul(sum2_ps[pt][:, :tw], onesc[:], y2stg[pe][:, :tw],
                                       start=(e == 0), stop=(e == nchunk - 1)), f"lnm{tag}")

        @block.vector
        def _(vector):
            sems.wait(vector, f"lnm{tag}")
            sems.inc(vector.tensor_scalar(rowA[:, :tw], sum_ps[pt][:, :tw], 1.0 / D_, None, ALU.mult), f"lnv{tag}")
            sems.inc(vector.tensor_scalar(rtmp[:, :tw], sum2_ps[pt][:, :tw], 1.0 / D_, eps, ALU.mult, ALU.add), f"lnv{tag}")
            sems.inc(vector.scalar_tensor_tensor(rowB[:, :tw], rowA[:, :tw], 1.0, rowA[:, :tw],
                                                 ALU.mult, ALU.mult), f"lnv{tag}")
            sems.inc(vector.tensor_tensor(rtmp[:, :tw], rtmp[:, :tw], rowB[:, :tw], ALU.subtract), f"lnv{tag}")

        @block.scalar
        def _(scalar):
            sems.wait(scalar, f"lnv{tag}")
            sems.inc(scalar.activation(rowS[:, :tw], rtmp[:, :tw], AF.Sqrt, bias=0.0, scale=1.0), f"lnst{tag}")

        @block.vector
        def _(vector):
            sems.wait(vector, f"lnst{tag}")
            sems.inc(vector.reciprocal(rtmp[:, :tw], rowS[:, :tw]), f"lnr{tag}")
            sems.inc(vector.scalar_tensor_tensor(rowB[:, :tw], rowA[:, :tw], -1.0, rtmp[:, :tw],
                                                 ALU.mult, ALU.mult), f"lnr{tag}")

        @block.tensor
        def _(tensor):
            sems.wait(tensor, f"lnr{tag}")
            sems.wait(tensor, f"lnbc{tag}")
            sems.inc(tensor.matmul(abA[:, :tw], onesr[:], rtmp[:, :tw], start=True, stop=True), f"lnb{tag}")
            sems.inc(tensor.matmul(abB[:, :tw], onesr[:], rowB[:, :tw], start=True, stop=True), f"lnb{tag}")

        @block.scalar
        def _(scalar):
            sems.wait(scalar, f"lnb{tag}")
            sems.wait(scalar, f"lnc{tag}")
            sems.inc(scalar.copy(Ab[pt][:, :tw], abA[:, :tw]), f"lnbc{tag}")
            sems.inc(scalar.copy(Bb[pt][:, :tw], abB[:, :tw]), f"lnbc{tag}")

        @block.vector
        def _(vector):
            sems.wait(vector, f"lnbc{tag}")
            for e in range(nchunk):
                x_out_cb(vector, e, t, Ab[pt], Bb[pt])


def build_l0_post(nc, with_pooled=True):
    from contextlib import ExitStack
    ctx = ExitStack()
    aT = nc.dram_tensor("aT", [D, TPC], BF16, kind="ExternalInput").ap()
    wout = nc.dram_tensor("wout", [D, D], F32, kind="ExternalInput").ap()
    bout = nc.dram_tensor("bout", [128, 8], F32, kind="ExternalInput").ap()
    xT = nc.dram_tensor("xT", [D, TPC], F32, kind="ExternalInput").ap()
    x1T = nc.dram_tensor("x1T", [D, TPC], F32, kind="ExternalOutput").ap()
    if with_pooled:
        pooled = nc.dram_tensor("pooled", [128, 8], F32, kind="ExternalOutput").ap()

    with ctx:
        ab = ctx.enter_context(nc.sbuf_tensor("ab", [128, 8, TPC], BF16))
        wb = ctx.enter_context(nc.sbuf_tensor("wb", [128, 8, D], BF16))
        stg = [ctx.enter_context(nc.sbuf_tensor(f"stg{i}", [128, D], F32)) for i in range(2)]
        xstg = [ctx.enter_context(nc.sbuf_tensor(f"xstg{i}", [128, 512], F32)) for i in range(2)]
        bo_sb = ctx.enter_context(nc.sbuf_tensor("bo_sb", [128, 8], F32))
        yf = ctx.enter_context(nc.sbuf_tensor("yf", [128, 8, TPC], F32))
        x1b = yf
        onesc = ctx.enter_context(nc.sbuf_tensor("onesc", [128, 1], F32))
        onesr = ctx.enter_context(nc.sbuf_tensor("onesr", [1, 128], F32))
        ptmp = ctx.enter_context(nc.sbuf_tensor("ptmp", [128, 8], F32))
        pacc = ctx.enter_context(nc.sbuf_tensor("pacc", [128, 8], F32))
        ps = [ctx.enter_context(nc.psum_tensor(f"ps{i}", [128, 512], F32)) for i in range(2)]
        block = ctx.enter_context(nc.Block())
        names = ["cv", "ms", "mm", "ev", "pr", "st"]
        names += ["lnf", "lnm", "lnsq", "lnv", "lnst", "lnr", "lnb", "lnbc", "lnc"]
        sems = Sems(nc, ctx, names)
        ld = DmaStream(nc, ctx, "ld", 1)
        wl = DmaStream(nc, ctx, "wl")
        xl = DmaStream(nc, ctx, "xl")

        @block.gpsimd
        def _(gp):
            sems.inc(gp.memset(onesc[:], 1.0), "ms")
            sems.inc(gp.memset(onesr[:], 1.0), "ms")

        @block.sync
        def _(sync):
            for e in range(8):
                ld.issue(sync, 0, ab[:, e, :], aT[e * 128:(e + 1) * 128, :])
            ld.issue(sync, 0, bo_sb[:], bout[:])
            for e in range(8):
                if e >= 2:
                    sync.wait_ge(sems.h["cv"], e - 1)
                wl.issue(sync, e, stg[e % 2][:], wout[e * 128:(e + 1) * 128, :])

        @block.vector
        def _(vector):
            for e in range(8):
                wl.wait(vector, e)
                sems.inc(vector.tensor_copy(wb[:, e, :], stg[e % 2][:]), "cv")

        for j in range(8):
            for t in range(4):
                g = j * 4 + t
                pj = g % 2

                @block.sync
                def _(sync):
                    if g >= 2:
                        sync.wait_ge(sems.h["ev"], g - 1)
                    xl.issue(sync, g, xstg[pj][:], xT[j * 128:(j + 1) * 128,
                                                     t * 512:(t + 1) * 512])

                @block.tensor
                def _(tensor):
                    if g == 0:
                        tensor.wait_ge(sems.h["cv"], 8)
                        ld.wait_all(tensor)
                    if g >= 2:
                        tensor.wait_ge(sems.h["ev"], g - 1)
                    for e in range(8):
                        sems.inc(tensor.matmul(ps[pj][:], wb[:, e, j * 128:(j + 1) * 128],
                                               ab[:, e, t * 512:(t + 1) * 512],
                                               start=(e == 0), stop=(e == 7)), "mm")

                @block.vector
                def _(vector):
                    sems.wait(vector, "mm")
                    xl.wait(vector, g)
                    sems.inc(vector.scalar_tensor_tensor(yf[:, j, t * 512:(t + 1) * 512], ps[pj][:],
                                                         bo_sb[:, j:j + 1], xstg[pj][:],
                                                         ALU.add, ALU.add), "ev")
        @block.vector
        def _(vector):
            sems.wait(vector, "ev")
            sems.inc(vector.tensor_copy(ptmp[:, 0:1], yf[:, 0, 0:1]), "lnf")

        def consume(vector, e, t, Abuf, Bbuf):
            sems.inc(vector.scalar_tensor_tensor(x1b[:, e, t * 512:(t + 1) * 512],
                                                 yf[:, e, t * 512:(t + 1) * 512], 1.0,
                                                 Abuf[:], ALU.mult, ALU.mult), "pr")
            sems.inc(vector.tensor_tensor(x1b[:, e, t * 512:(t + 1) * 512],
                                          x1b[:, e, t * 512:(t + 1) * 512], Bbuf[:], ALU.add), "lnc")

        emit_ln(nc, block, sems, ctx, yf, consume, onesc=onesc, onesr=onesr, tag="")

        if with_pooled:
            @block.vector
            def _(vector):
                sems.wait(vector, "lnc")
                for e in range(8):
                    for t in range(4):
                        sems.inc(vector.tensor_reduce(ptmp[:, t:t + 1],
                                                      x1b[:, e, t * 512:(t + 1) * 512],
                                                      mybir.AxisListType.X, ALU.add), "pr")
                    sems.wait(vector, "pr")
                    sems.inc(vector.tensor_reduce(pacc[:, e:e + 1], ptmp[:, 0:4],
                                                  mybir.AxisListType.X, ALU.add), "pr")
                sems.wait(vector, "pr")

        @block.gpsimd
        def _(gp):
            sems.wait(gp, "pr")
            sems.wait(gp, "lnc")
            for e in range(8):
                sems.inc(gp.dma_start(out=x1T[e * 128:(e + 1) * 128, :], in_=x1b[:, e, :]), "st", 16)
            if with_pooled:
                sems.inc(gp.dma_start(out=pooled[:], in_=pacc[:]), "st", 16)
            sems.wait(gp, "st")



def build_l0_moe(nc):
    from contextlib import ExitStack
    ctx = ExitStack()
    x1T = nc.dram_tensor("x1T", [D, TPC], F32, kind="ExternalInput").ap()
    pooled_in = [nc.dram_tensor(f"pooled{i}", [128, 8], F32, kind="ExternalInput").ap() for i in range(4)]
    gwT = nc.dram_tensor("gwT", [D, 8], F32, kind="ExternalInput").ap()
    dwT = nc.dram_tensor("dwT", [8, D, 64], F32, kind="ExternalInput").ap()
    db = nc.dram_tensor("db", [64, 8], F32, kind="ExternalInput").ap()
    upa = nc.dram_tensor("upa", [8, 65, D], F32, kind="ExternalInput").ap()
    x2T = nc.dram_tensor("x2T", [D, TPC], F32, kind="ExternalOutput").ap()
    dbg = nc.dram_tensor("dbg", [65, 64], F32, kind="ExternalOutput").ap()

    with ctx:
        stgx = [ctx.enter_context(nc.sbuf_tensor(f"stgx{i}", [128, TPC], F32)) for i in range(2)]
        xstg = [ctx.enter_context(nc.sbuf_tensor(f"xstg{i}", [128, 512], F32)) for i in range(2)]
        xb = ctx.enter_context(nc.sbuf_tensor("xb", [128, 8, TPC], BF16))
        pl = ctx.enter_context(nc.sbuf_tensor("pl", [128, 4, 8], F32))
        gw_sb = ctx.enter_context(nc.sbuf_tensor("gw_sb", [128, 8, 8], F32))
        dstg = ctx.enter_context(nc.sbuf_tensor("dstg", [128, 8, 64], F32))
        dwb = ctx.enter_context(nc.sbuf_tensor("dwb", [128, 8, 8, 64], BF16))
        ustg = ctx.enter_context(nc.sbuf_tensor("ustg", [65, D], F32))
        ub = ctx.enter_context(nc.sbuf_tensor("ub", [65, 8, D], BF16))
        db_sb = ctx.enter_context(nc.sbuf_tensor("db_sb", [64, 8], F32))
        pool_f = ctx.enter_context(nc.sbuf_tensor("pool_f", [128, 8], F32))
        glr = ctx.enter_context(nc.sbuf_tensor("glr", [1, 16], F32))
        gate = ctx.enter_context(nc.sbuf_tensor("gate", [1, 8], F32))
        srt = ctx.enter_context(nc.sbuf_tensor("srt", [1, 8], F32))
        mrow = ctx.enter_context(nc.sbuf_tensor("mrow", [1, 8], F32))
        me_sb = ctx.enter_context(nc.sbuf_tensor("me_sb", [64, 8], F32))
        onesr = ctx.enter_context(nc.sbuf_tensor("onesr", [1, 128], F32))
        onesc = ctx.enter_context(nc.sbuf_tensor("onesc", [128, 1], F32))
        ones512 = ctx.enter_context(nc.sbuf_tensor("ones512", [1, 512], BF16))
        gaug = ctx.enter_context(nc.sbuf_tensor("gaug", [65, 8, 512], BF16))
        pool = [ctx.enter_context(nc.psum_tensor(f"pp{i}", [128, 512], F32)) for i in range(8)]
        block = ctx.enter_context(nc.Block())
        names = ["cv", "ms", "gt", "mm", "ev", "sc", "up", "st",
                 "lnf", "lnm", "lnsq", "lnv", "lnst", "lnr", "lnb", "lnbc", "lnc"]
        sems = Sems(nc, ctx, names)
        sx = DmaStream(nc, ctx, "sx")
        lds = DmaStream(nc, ctx, "lds", 1)
        dl = DmaStream(nc, ctx, "dl", 1)
        ul = DmaStream(nc, ctx, "ul", 1)
        xl = DmaStream(nc, ctx, "xl")
        out_f = ctx.enter_context(nc.sbuf_tensor("out_f", [128, 8, TPC], F32))

        @block.gpsimd
        def _(gp):
            sems.inc(gp.memset(onesr[:], 1.0), "ms")
            sems.inc(gp.memset(onesc[:], 1.0), "ms")
            sems.inc(gp.memset(ones512[:], 1.0), "ms")

        @block.sync
        def _(sync):
            for e in range(8):
                if e >= 2:
                    sync.wait_ge(sems.h["cv"], e - 1)
                sx.issue(sync, e, stgx[e % 2][:], x1T[e * 128:(e + 1) * 128, :])
            for i in range(4):
                lds.issue(sync, 0, pl[:, i, :], pooled_in[i][:])
            lds.issue(sync, 0, gw_sb[:], gwT[:].rearrange("(c p) g -> p c g", p=128))
            lds.issue(sync, 0, db_sb[:], db[:])

        @block.vector
        def _(vector):
            for e in range(8):
                sx.wait(vector, e)
                sems.inc(vector.tensor_copy(xb[:, e, :], stgx[e % 2][:]), "cv")

        @block.sync
        def _(sync):
            for ex in range(8):
                if ex >= 1:
                    sync.wait_ge(sems.h["cv"], 8 + ex)
                dl.issue(sync, 0, dstg[:], dwT[ex].rearrange("(c p) v -> p c v", p=128))

        @block.vector
        def _(vector):
            for ex in range(8):
                vector.wait_ge(dl.s[0], 16 * (ex + 1))
                sems.inc(vector.tensor_copy(dwb[:, ex, :, :], dstg[:]), "cv")

        @block.sync
        def _(sync):
            for ex in range(8):
                if ex >= 1:
                    sync.wait_ge(sems.h["cv"], 16 + ex)
                ul.issue(sync, 0, ustg[:], upa[ex])

        @block.vector
        def _(vector):
            for ex in range(8):
                vector.wait_ge(ul.s[0], 16 * (ex + 1))
                sems.inc(vector.tensor_copy(ub[:, ex, :], ustg[:]), "cv")

        @block.vector
        def _(vector):
            lds.wait_all(vector)
            sems.inc(vector.tensor_tensor(pool_f[:], pl[:, 0, :], pl[:, 1, :], ALU.add), "gt")
            sems.wait(vector, "gt")
            sems.inc(vector.tensor_tensor(pool_f[:], pool_f[:], pl[:, 2, :], ALU.add), "gt")
            sems.wait(vector, "gt")
            sems.inc(vector.tensor_tensor(pool_f[:], pool_f[:], pl[:, 3, :], ALU.add), "gt")
            sems.wait(vector, "gt")
            sems.inc(vector.tensor_scalar(pool_f[:], pool_f[:], 1.0 / S, None, ALU.mult), "gt")
            sems.wait(vector, "gt")

        @block.tensor
        def _(tensor):
            sems.wait(tensor, "gt")
            lds.wait_all(tensor)
            for cch in range(8):
                sems.inc(tensor.matmul(pool[4][0:1, 0:8], pool_f[:, cch:cch + 1], gw_sb[:, cch, :],
                                       start=(cch == 0), stop=(cch == 7)), "gt")

        @block.scalar
        def _(scalar):
            sems.wait(scalar, "gt")
            sems.inc(scalar.copy(glr[:, 0:8], pool[4][0:1, 0:8]), "gt")

        @block.vector
        def _(vector):
            sems.wait(vector, "gt")
            sems.inc(vector.tensor_reduce(glr[:, 8:9], glr[:, 0:8], mybir.AxisListType.X, ALU.max), "gt")
            sems.wait(vector, "gt")
            sems.inc(vector.tensor_scalar(glr[:, 9:10], glr[:, 8:9], -1.0, None, ALU.mult), "gt")
            sems.wait(vector, "gt")

        @block.scalar
        def _(scalar):
            sems.wait(scalar, "gt")
            sems.inc(scalar.activation(gate[:], glr[:, 0:8], AF.Exp, bias=glr[:, 9:10], scale=1.0), "gt")

        @block.vector
        def _(vector):
            sems.wait(vector, "gt")
            sems.inc(vector.tensor_reduce(glr[:, 10:11], gate[:], mybir.AxisListType.X, ALU.add), "gt")
            sems.wait(vector, "gt")
            sems.inc(vector.reciprocal(glr[:, 11:12], glr[:, 10:11]), "gt")
            sems.wait(vector, "gt")
            sems.inc(vector.tensor_scalar(gate[:], gate[:], glr[:, 11:12], None, ALU.mult), "gt")
            sems.wait(vector, "gt")
            sems.inc(vector.max(srt[:], gate[:]), "gt")
            sems.wait(vector, "gt")
            sems.inc(vector.tensor_tensor(glr[:, 12:13], srt[:, 0:1], srt[:, 1:2], ALU.add), "gt")
            sems.wait(vector, "gt")
            sems.inc(vector.tensor_scalar(glr[:, 13:14], glr[:, 12:13], 1e-8, None, ALU.add), "gt")
            sems.wait(vector, "gt")
            sems.inc(vector.reciprocal(glr[:, 14:15], glr[:, 13:14]), "gt")
            sems.wait(vector, "gt")
            sems.inc(vector.tensor_scalar(mrow[:], gate[:], srt[:, 1:2], glr[:, 14:15],
                                          ALU.is_ge, ALU.mult), "gt")
            sems.wait(vector, "gt")
            sems.inc(vector.tensor_tensor(mrow[:], mrow[:], gate[:], ALU.mult), "gt")

        @block.tensor
        def _(tensor):
            sems.wait(tensor, "gt")
            sems.wait(tensor, "ms")
            sems.inc(tensor.matmul(pool[5][0:64, 0:8], onesr[:, 0:64], mrow[:], start=True, stop=True), "gt")

        @block.scalar
        def _(scalar):
            sems.wait(scalar, "gt")
            sems.inc(scalar.copy(me_sb[:], pool[5][0:64, 0:8]), "gt")

        for t in range(4):
            for ex in range(8):
                gidx = t * 8 + ex
                pd = gidx % 2

                @block.tensor
                def _(tensor):
                    if gidx == 0:
                        sems.wait(tensor, "cv")
                        sems.wait(tensor, "gt")
                    sems.wait(tensor, "ev")
                    for cch in range(8):
                        sems.inc(tensor.matmul(pool[pd][0:64, :], dwb[:, ex, cch, :],
                                               xb[:, cch, t * 512:(t + 1) * 512],
                                               start=(cch == 0), stop=(cch == 7)), "mm")

                @block.scalar
                def _(scalar):
                    sems.wait(scalar, "mm")
                    sems.wait(scalar, "up")
                    sems.inc(scalar.activation(gaug[0:64, ex, :], pool[pd][0:64, :], AF.Gelu,
                                               bias=db_sb[:, ex:ex + 1], scale=1.0), "ev")

                @block.vector
                def _(vector):
                    sems.wait(vector, "ev")
                    sems.inc(vector.tensor_scalar(gaug[0:64, ex, :], gaug[0:64, ex, :],
                                                  me_sb[:, ex:ex + 1], None, ALU.mult), "sc")
                    sems.inc(vector.tensor_scalar(gaug[64:65, ex, :], ones512[:],
                                                  mrow[:, ex:ex + 1], None, ALU.mult), "sc")

            for j in range(8):
                pu = 2 + (t * 8 + j) % 2

                @block.sync
                def _(sync):
                    sems.wait(sync, "ev")
                    xl.issue(sync, t * 8 + j, xstg[(t * 8 + j) % 2][:],
                             x1T[j * 128:(j + 1) * 128, t * 512:(t + 1) * 512])

                @block.tensor
                def _(tensor):
                    sems.wait(tensor, "sc")
                    sems.wait(tensor, "ev")
                    for ex in range(8):
                        sems.inc(tensor.matmul(pool[pu][:, :], ub[:, ex, j * 128:(j + 1) * 128],
                                               gaug[:, ex, :], start=(ex == 0), stop=(ex == 7)), "up")

                @block.vector
                def _(vector):
                    sems.wait(vector, "up")
                    xl.wait(vector, t * 8 + j)
                    sems.inc(vector.scalar_tensor_tensor(out_f[:, j, t * 512:(t + 1) * 512],
                                                         pool[pu][:, :], 0.0,
                                                         xstg[(t * 8 + j) % 2][:],
                                                         ALU.add, ALU.add), "ev")

        @block.vector
        def _(vector):
            sems.wait(vector, "ev")
            sems.inc(vector.tensor_copy(glr[:, 15:16], out_f[0:1, 0, 0:1]), "lnf")

        def consume(vector, e, t, Abuf, Bbuf):
            sems.inc(vector.scalar_tensor_tensor(out_f[:, e, t * 512:(t + 1) * 512],
                                                 out_f[:, e, t * 512:(t + 1) * 512], 1.0,
                                                 Abuf[:], ALU.mult, ALU.mult), "sc")
            sems.inc(vector.tensor_tensor(out_f[:, e, t * 512:(t + 1) * 512],
                                          out_f[:, e, t * 512:(t + 1) * 512], Bbuf[:], ALU.add), "lnc")

        emit_ln(nc, block, sems, ctx, out_f, consume, onesc=onesc, onesr=onesr, tag="",
                psum_pool=[pool[0], pool[1], pool[2], pool[3], pool[6], pool[7]])

        @block.gpsimd
        def _(gp):
            sems.wait(gp, "lnc")
            for e in range(8):
                sems.inc(gp.dma_start(out=x2T[e * 128:(e + 1) * 128, :], in_=out_f[:, e, :]), "st", 16)
            sems.inc(gp.dma_start(out=dbg[0:1, 0:8], in_=gate[:]), "st", 16)
            sems.inc(gp.dma_start(out=dbg[1:2, 0:8], in_=srt[:]), "st", 16)
            sems.inc(gp.dma_start(out=dbg[2:3, 0:8], in_=mrow[:]), "st", 16)
            sems.inc(gp.dma_start(out=dbg[3:4, 0:16], in_=glr[:]), "st", 16)
            sems.inc(gp.dma_start(out=dbg[4:5, 0:8], in_=me_sb[0:1, :]), "st", 16)
            sems.inc(gp.dma_start(out=dbg[5:6, 0:64], in_=gaug[0:1, 0, 0:64]), "st", 16)
            sems.wait(gp, "st")


def host_prep_l0_moe(inputs, res3):
    gwT = np.ascontiguousarray(inputs["L0_ffn_gate_w"].astype(np.float32).T)
    dwT = np.ascontiguousarray(inputs["L0_ffn_down_w"].astype(np.float32).transpose(0, 2, 1))
    db = np.ascontiguousarray(inputs["L0_ffn_down_b"].astype(np.float32).T)
    upw = inputs["L0_ffn_up_w"].astype(np.float32).transpose(0, 2, 1)
    upb = inputs["L0_ffn_up_b"].astype(np.float32)[:, None, :]
    upa = np.ascontiguousarray(np.concatenate([upw, upb], axis=1))
    per_core = []
    for c in range(NC):
        b = c // 4
        d = {"x1T": res3.results[c]["x1T"], "gwT": gwT, "dwT": dwT, "db": db, "upa": upa}
        for i in range(4):
            d[f"pooled{i}"] = res3.results[b * 4 + i]["pooled"]
        per_core.append(d)
    return per_core


def host_prep_l0_post(inputs, res1, res2):
    wout = np.ascontiguousarray(inputs["L0_attn_W_out"].astype(np.float32))
    bout = np.ascontiguousarray(inputs["L0_attn_b_out"].astype(np.float32).reshape(8, 128).T)
    x = inputs["L0_attn_x"]
    per_core = []
    for c in range(NC):
        b, s0 = c // 4, (c % 4) * TPC
        xT = np.ascontiguousarray(x[b, s0:s0 + TPC, :].T.astype(np.float32))
        per_core.append({"aT": res2.results[c]["aT"], "wout": wout, "bout": bout, "xT": xT})
    return per_core



def build_l1f_qkv(nc):
    from contextlib import ExitStack
    ctx = ExitStack()
    xT = nc.dram_tensor("xT", [D, TPC], F32, kind="ExternalInput").ap()
    xcol8 = nc.dram_tensor("xcol8", [128, 8], F32, kind="ExternalInput").ap()
    wqkv = nc.dram_tensor("wqkv", [D, 1536], F32, kind="ExternalInput").ap()
    qsr = nc.dram_tensor("qsr", [128, 2], F32, kind="ExternalInput").ap()
    tabs = nc.dram_tensor("tabs", [4, 128, TPC], F32, kind="ExternalInput").ap()
    selm = nc.dram_tensor("selm", [128, 4, 8], F32, kind="ExternalInput").ap()
    selmT = nc.dram_tensor("selmT", [8, 4, 128], F32, kind="ExternalInput").ap()
    qrT = nc.dram_tensor("qrT", [512, TPC], BF16, kind="ExternalOutput").ap()
    krT = nc.dram_tensor("krT", [512, TPC], BF16, kind="ExternalOutput").ap()
    vaug = nc.dram_tensor("vaug", [TPC, 520], BF16, kind="ExternalOutput").ap()

    with ctx:
        xsb = ctx.enter_context(nc.sbuf_tensor("xsb", [128, 8, TPC + 1], BF16))
        stg = [ctx.enter_context(nc.sbuf_tensor(f"stg{i}", [128, TPC], F32)) for i in range(2)]
        wb = ctx.enter_context(nc.sbuf_tensor("wb", [128, 8, 1536], BF16))
        xc_f = ctx.enter_context(nc.sbuf_tensor("xc_f", [128, 8], F32))
        qs_sb = ctx.enter_context(nc.sbuf_tensor("qs_sb", [128, 2], F32))
        tab_b = ctx.enter_context(nc.sbuf_tensor("tab_b", [128, 4, TPC], BF16))
        sel_sb = ctx.enter_context(nc.sbuf_tensor("sel_sb", [128, 4, 8], F32))
        selT_sb = ctx.enter_context(nc.sbuf_tensor("selT_sb", [8, 4, 128], F32))
        qkb = ctx.enter_context(nc.sbuf_tensor("qkb", [128, 8, TPC], BF16))
        q2stg = [ctx.enter_context(nc.sbuf_tensor(f"q2stg{i}", [128, 512], F32)) for i in range(2)]
        rsb = ctx.enter_context(nc.sbuf_tensor("rsb", [8, 2, TPC], F32))
        rbsb = ctx.enter_context(nc.sbuf_tensor("rbsb", [128, 512], F32))
        qnb = ctx.enter_context(nc.sbuf_tensor("qnb", [128, 512], BF16))
        r1b = ctx.enter_context(nc.sbuf_tensor("r1b", [128, 512], BF16))
        r2b = ctx.enter_context(nc.sbuf_tensor("r2b", [128, 512], BF16))
        qout = ctx.enter_context(nc.sbuf_tensor("qout", [128, 4, TPC], BF16))
        kout = ctx.enter_context(nc.sbuf_tensor("kout", [128, 4, TPC], BF16))
        va = ctx.enter_context(nc.sbuf_tensor("va", [128, 16, 520], BF16))
        ps = [ctx.enter_context(nc.psum_tensor(f"ps{i}", [128, 512], F32)) for i in range(2)]
        ss_ps = ctx.enter_context(nc.psum_tensor("ss_ps", [8, 512], F32))
        rb_ps = ctx.enter_context(nc.psum_tensor("rb_ps", [128, 512], F32))
        block = ctx.enter_context(nc.Block())
        sems = Sems(nc, ctx, ["cv", "ms", "mm", "ev", "sq", "sm", "rv", "rb", "rt", "st"])
        sg = DmaStream(nc, ctx, "sg")
        lds = DmaStream(nc, ctx, "lds", 1)

        def xs(e, lo, hi):
            off = 1 if e < 4 else 0
            return xsb[:, e, lo + off:hi + off]

        @block.gpsimd
        def _(gp):
            va_h = va[:].rearrange("p t (h c) -> p t h c", c=65)
            sems.inc(gp.memset(va_h[:, :, :, 64:65], 1.0), "ms")

        @block.sync
        def _(sync):
            for i in range(8):
                if i >= 2:
                    sync.wait_ge(sems.h["cv"], i - 1)
                sg.issue(sync, i, stg[i % 2][:], xT[i * 128:(i + 1) * 128, :])
            lds.issue(sync, 0, xc_f[:], xcol8[:])
            lds.issue(sync, 0, qs_sb[:], qsr[:])
            lds.issue(sync, 0, sel_sb[:], selm[:])
            lds.issue(sync, 0, selT_sb[:], selmT[:])
            for i in range(8):
                sync.wait_ge(sems.h["cv"], (7 + i) if i < 2 else (16 + i - 1))
                sg.issue(sync, 8 + i, stg[i % 2][:, :1536], wqkv[i * 128:(i + 1) * 128, :])
            for i in range(4):
                sync.wait_ge(sems.h["cv"], (23 + i) if i < 2 else (24 + i - 1))
                sg.issue(sync, 16 + i, stg[i % 2][:], tabs[i])

        @block.vector
        def _(vector):
            for i in range(8):
                sg.wait(vector, i)
                sems.inc(vector.tensor_copy(xsb[:, i, 1:TPC + 1], stg[i % 2][:]), "cv")
            lds.wait_all(vector)
            for e in range(8):
                sems.inc(vector.tensor_copy(xsb[:, e, 0:1], xc_f[:, e:e + 1]), "cv")
            for i in range(8):
                sg.wait(vector, 8 + i)
                sems.inc(vector.tensor_copy(wb[:, i, :], stg[i % 2][:, :1536]), "cv")
            for i in range(4):
                sg.wait(vector, 16 + i)
                sems.inc(vector.tensor_copy(tab_b[:, i, :], stg[i % 2][:]), "cv")

        for j in range(8):
            for t in range(4):
                g = j * 4 + t

                @block.tensor
                def _(tensor):
                    if g == 0:
                        sems.wait(tensor, "cv")
                    sems.wait(tensor, "ev")
                    for e in range(8):
                        sems.inc(tensor.matmul(ps[g % 2][:], wb[:, e, j * 128:(j + 1) * 128],
                                               xs(e, t * 512, (t + 1) * 512),
                                               start=(e == 0), stop=(e == 7)), "mm")

                @block.scalar
                def _(scalar):
                    sems.wait(scalar, "mm")
                    sems.inc(scalar.activation(qkb[:, j, t * 512:(t + 1) * 512], ps[g % 2][:],
                                               AF.Identity, bias=0.0, scale=1.0), "ev")

        for tt in range(16):
            g = 32 + tt

            @block.tensor
            def _(tensor):
                sems.wait(tensor, "ev")
                for e in range(8):
                    sems.inc(tensor.matmul(ps[g % 2][:], xs(e, tt * 128, (tt + 1) * 128),
                                           wb[:, e, 1024:1536],
                                           start=(e == 0), stop=(e == 7)), "mm")

            @block.scalar
            def _(scalar):
                sems.wait(scalar, "mm")
                va_h = va[:, tt, :].rearrange("p (h c) -> p h c", c=65)
                sems.inc(scalar.activation(va_h[:, :, 0:64], ps[g % 2][:],
                                           AF.Identity, bias=0.0, scale=1.0), "ev")

        for qk in range(2):
            for t in range(4):
                for c in range(4):
                    @block.scalar
                    def _(scalar):
                        sems.wait(scalar, "sm")
                        sems.inc(scalar.activation(q2stg[c % 2][:], qkb[:, qk * 4 + c, t * 512:(t + 1) * 512],
                                                   AF.Square, bias=0.0, scale=1.0), "sq")

                    @block.tensor
                    def _(tensor):
                        sems.wait(tensor, "sq")
                        if c == 0:
                            sems.wait(tensor, "rv")
                        sems.inc(tensor.matmul(ss_ps[:], sel_sb[:, c, :], q2stg[c % 2][:],
                                               start=(c == 0), stop=(c == 3)), "sm")

                @block.scalar
                def _(scalar):
                    sems.wait(scalar, "sm")
                    sems.wait(scalar, "rv")
                    sems.inc(scalar.activation(rsb[:, qk, t * 512:(t + 1) * 512], ss_ps[:],
                                               AF.Sqrt, bias=0.0, scale=1.0), "rv")

                @block.vector
                def _(vector):
                    sems.wait(vector, "rv")
                    sems.inc(vector.tensor_scalar(rsb[:, qk, t * 512:(t + 1) * 512],
                                                  rsb[:, qk, t * 512:(t + 1) * 512],
                                                  1e-12, None, ALU.max), "rv")
                    sems.wait(vector, "rv")
                    sems.inc(vector.reciprocal(rsb[:, qk, t * 512:(t + 1) * 512],
                                               rsb[:, qk, t * 512:(t + 1) * 512]), "rv")

        for qk in range(2):
            tabC = 0 if qk == 0 else 2
            outb = qout if qk == 0 else kout
            for c in range(4):
                for t in range(4):
                    @block.tensor
                    def _(tensor):
                        sems.wait(tensor, "rv")
                        sems.wait(tensor, "rb")
                        sems.inc(tensor.matmul(rb_ps[:], selT_sb[:, c, :],
                                               rsb[:, qk, t * 512:(t + 1) * 512],
                                               start=True, stop=True), "rt")

                    @block.scalar
                    def _(scalar):
                        sems.wait(scalar, "rt")
                        sems.inc(scalar.copy(rbsb[:], rb_ps[:]), "rb")

                    @block.vector
                    def _(vector):
                        sems.wait(vector, "rb")
                        sems.inc(vector.scalar_tensor_tensor(
                            qnb[:], qkb[:, qk * 4 + c, t * 512:(t + 1) * 512],
                            qs_sb[:, qk:qk + 1], rbsb[:], ALU.mult, ALU.mult), "rt")
                        sems.wait(vector, "rt")
                        sems.inc(vector.tensor_tensor(r1b[:], qnb[:],
                                                      tab_b[:, tabC, t * 512:(t + 1) * 512], ALU.mult), "rt")
                        for sblk in range(4):
                            srcp = (sblk ^ 1) * 32
                            dstp = sblk * 32
                            sems.inc(vector.tensor_copy(r2b[dstp:dstp + 32, :],
                                                        qnb[srcp:srcp + 32, :]), "rt")
                        sems.wait(vector, "rt")
                        sems.inc(vector.tensor_tensor(
                            r2b[:], r2b[:], tab_b[:, tabC + 1, t * 512:(t + 1) * 512], ALU.mult), "rt")
                        sems.wait(vector, "rt")
                        sems.inc(vector.tensor_tensor(outb[:, c, t * 512:(t + 1) * 512],
                                                      r1b[:], r2b[:], ALU.add), "rt")

        @block.gpsimd
        def _(gp):
            sems.wait(gp, "rt")
            sems.wait(gp, "ev")
            sems.wait(gp, "ms")
            for c in range(4):
                sems.inc(gp.dma_start(out=qrT[c * 128:(c + 1) * 128, :], in_=qout[:, c, :]), "st", 16)
                sems.inc(gp.dma_start(out=krT[c * 128:(c + 1) * 128, :], in_=kout[:, c, :]), "st", 16)
            sems.inc(gp.dma_start(out=vaug[:].rearrange("(t p) c -> p t c", p=128), in_=va[:]), "st", 16)
            sems.wait(gp, "st")


SERIAL_TILES = False


def build_l1f_attn(nc):
    from contextlib import ExitStack
    ctx = ExitStack()
    qrT = nc.dram_tensor("qrT", [512, TPC], BF16, kind="ExternalInput").ap()
    krT = nc.dram_tensor("krT", [512, TPC], BF16, kind="ExternalInput").ap()
    vaug = nc.dram_tensor("vaug", [TPC, 520], BF16, kind="ExternalInput").ap()
    kb4 = nc.dram_tensor("kb4", [128, 4, 512], BF16, kind="ExternalInput").ap()
    vb4 = nc.dram_tensor("vb4", [128, 4, 520], BF16, kind="ExternalInput").ap()
    mask0 = nc.dram_tensor("mask0", [4, 128, 1024], F32, kind="ExternalInput").ap()
    mask1 = nc.dram_tensor("mask1", [4, 128, 1024], F32, kind="ExternalInput").ap()
    xT = nc.dram_tensor("xT", [D, TPC], F32, kind="ExternalInput").ap()
    wo = nc.dram_tensor("wo", [512, D], F32, kind="ExternalInput").ap()
    ident = nc.dram_tensor("ident", [128, 128], BF16, kind="ExternalInput").ap()
    xpaT = nc.dram_tensor("xpaT", [D, TPC], F32, kind="ExternalOutput").ap()

    with ctx:
        qrx = ctx.enter_context(nc.sbuf_tensor("qrx", [128, 4, TPC], BF16))
        krx = ctx.enter_context(nc.sbuf_tensor("krx", [128, 4, TPC + 512], BF16))
        vx = ctx.enter_context(nc.sbuf_tensor("vx", [128, 20, 520], BF16))
        m0 = ctx.enter_context(nc.sbuf_tensor("m0", [128, 4, 1024], F32))
        m1 = ctx.enter_context(nc.sbuf_tensor("m1", [128, 4, 1024], F32))
        msk = ctx.enter_context(nc.sbuf_tensor("msk", [128, 1024], F32))
        wloc32 = ctx.enter_context(nc.sbuf_tensor("wloc32", [128, 1024], F32))
        mx = ctx.enter_context(nc.sbuf_tensor("mx", [128, 2], F32))
        wloc = ctx.enter_context(nc.sbuf_tensor("wloc", [128, 1024], BF16))
        wT2 = [ctx.enter_context(nc.sbuf_tensor(f"wT2{i}", [128, 512], BF16)) for i in range(2)]
        id_sb = ctx.enter_context(nc.sbuf_tensor("id_sb", [128, 128], BF16))
        rA = ctx.enter_context(nc.sbuf_tensor("rA", [1, 128], F32))
        bA_sb = ctx.enter_context(nc.sbuf_tensor("bA_sb", [64, 128], F32))
        ones64 = ctx.enter_context(nc.sbuf_tensor("ones64", [1, 64], F32))
        vscr = ctx.enter_context(nc.sbuf_tensor("vscr", [128, 128], F32))
        oT = ctx.enter_context(nc.sbuf_tensor("oT", [128, 4, TPC], BF16))
        wstg = [ctx.enter_context(nc.sbuf_tensor(f"wstg{i}", [128, D], F32)) for i in range(2)]
        wob = ctx.enter_context(nc.sbuf_tensor("wob", [128, 4, D], BF16))
        xstg = [ctx.enter_context(nc.sbuf_tensor(f"xstg{i}", [128, 512], F32)) for i in range(2)]
        xout = ctx.enter_context(nc.sbuf_tensor("xout", [128, 8, TPC], F32))
        ps_loc = ctx.enter_context(nc.psum_tensor("ps_loc", [128, 1024], F32))
        tp2 = [ctx.enter_context(nc.psum_tensor(f"tp2{i}", [128, 512], F32)) for i in range(2)]
        id32 = ctx.enter_context(nc.sbuf_tensor("id32", [128, 128], F32))
        o_ps = ctx.enter_context(nc.psum_tensor("o_ps", [65, 128], F32))
        b_ps = ctx.enter_context(nc.psum_tensor("b_ps", [64, 128], F32))
        ps = [ctx.enter_context(nc.psum_tensor(f"ps{i}", [128, 512], F32)) for i in range(2)]
        block = ctx.enter_context(nc.Block())
        sems = Sems(nc, ctx, ["ld", "cv", "ms", "mm", "mk", "ex", "tp", "tv", "av",
                              "rc", "tb", "bc", "no", "pm", "pe", "xev"])
        wl = DmaStream(nc, ctx, "wl")
        xl = DmaStream(nc, ctx, "xl")
        stp = DmaStream(nc, ctx, "stp", 1)

        @block.gpsimd
        def _(gp):
            sems.inc(gp.memset(ones64[:], 1.0), "ms")

        @block.sync
        def _(sync):
            for c in range(4):
                sems.inc(sync.dma_start(out=qrx[:, c, :], in_=qrT[c * 128:(c + 1) * 128, :]), "ld", 16)
                sems.inc(sync.dma_start(out=krx[:, c, 512:], in_=krT[c * 128:(c + 1) * 128, :]), "ld", 16)
            sems.inc(sync.dma_start(out=krx[:, :, 0:512], in_=kb4[:]), "ld", 16)
            sems.inc(sync.dma_start(out=vx[:, 0:4, :], in_=vb4[:]), "ld", 16)
            va_src = vaug[:].rearrange("(kt p) c -> p kt c", p=128)
            for i in range(2):
                sems.inc(sync.dma_start(out=vx[:, 4 + i * 8:4 + (i + 1) * 8, :],
                                        in_=va_src[:, i * 8:(i + 1) * 8, :]), "ld", 16)
            for v in range(4):
                sems.inc(sync.dma_start(out=m0[:, v, :], in_=mask0[v]), "ld", 16)
                sems.inc(sync.dma_start(out=m1[:, v, :], in_=mask1[v]), "ld", 16)
            sems.inc(sync.dma_start(out=id_sb[:], in_=ident[:]), "ld", 16)
            for c in range(4):
                if c >= 2:
                    sync.wait_ge(sems.h["cv"], c)
                wl.issue(sync, c, wstg[c % 2][:], wo[c * 128:(c + 1) * 128, :])

        @block.vector
        def _(vector):
            vector.wait_ge(sems.h["ld"], 16 * 21)
            sems.inc(vector.tensor_copy(id32[:], id_sb[:]), "cv")
            for c in range(4):
                wl.wait(vector, c)
                sems.inc(vector.tensor_copy(wob[:, c, :], wstg[c % 2][:]), "cv")

        gkbf = [0]
        for h in range(8):
            hb = (h % 2) * 64
            ch = h // 2
            for w in range(4):
                mbuf = m1 if w == 0 else m0
                for qt in range(4):
                    q0 = w * 512 + qt * 128

                    @block.tensor
                    def _(tensor):
                        if h == 0 and w == 0 and qt == 0:
                            tensor.wait_ge(sems.h["ld"], 16 * 21)
                        if SERIAL_TILES:
                            sems.wait(tensor, "no")
                        sems.wait(tensor, "pm")
                        sems.inc(tensor.matmul(ps_loc[:, 0:512], qrx[hb:hb + 64, ch, q0:q0 + 128],
                                               krx[hb:hb + 64, ch, w * 512:w * 512 + 512],
                                               start=True, stop=True,
                                               tile_position=(hb, 0)), "mm")
                        sems.inc(tensor.matmul(ps_loc[:, 512:1024], qrx[hb:hb + 64, ch, q0:q0 + 128],
                                               krx[hb:hb + 64, ch, w * 512 + 512:w * 512 + 1024],
                                               start=True, stop=True,
                                               tile_position=(hb, 0)), "mm")

                    @block.vector
                    def _(vector):
                        sems.wait(vector, "mm")
                        sems.wait(vector, "pe")
                        sems.inc(vector.scalar_tensor_tensor(msk[:], ps_loc[:], 1.0,
                                                             mbuf[:, qt, :], ALU.mult, ALU.add), "pm")
                        sems.wait(vector, "pm")
                        sems.inc(vector.tensor_reduce(mx[:, 0:1], msk[:],
                                                      mybir.AxisListType.X, ALU.max), "pm")
                        sems.wait(vector, "pm")
                        sems.inc(vector.tensor_scalar(mx[:, 1:2], mx[:, 0:1], -1.0, None, ALU.mult), "pm")
                        sems.wait(vector, "pm")
                        sems.inc(vector.tensor_copy(vscr[:, 0:1], mx[:, 1:2]), "pm")

                    @block.scalar
                    def _(scalar):
                        sems.wait(scalar, "pm")
                        sems.wait(scalar, "tp")
                        sems.inc(scalar.activation(wloc32[:], msk[:], AF.Exp,
                                                   bias=mx[:, 1:2], scale=1.0), "pe")

                    g0 = gkbf[0]
                    for kb in range(2):
                        g = gkbf[0]; gkbf[0] += 1

                        @block.tensor
                        def _(tensor, kb=kb, g=g):
                            sems.wait(tensor, "pe")
                            if g >= 2:
                                sems.wait(tensor, "tv", g - 1)
                            if kb == 0 and h == 0 and w == 0 and qt == 0:
                                tensor.wait_ge(sems.h["cv"], 1)
                            for j2 in range(4):
                                sems.inc(tensor.transpose(
                                    tp2[g % 2][:, j2 * 128:(j2 + 1) * 128],
                                    wloc32[:, (kb * 4 + j2) * 128:(kb * 4 + j2 + 1) * 128],
                                    id32[:]), "tp")

                        @block.scalar
                        def _(scalar, kb=kb, g=g):
                            sems.wait(scalar, "tp", 4 * (g + 1))
                            if g >= 2:
                                sems.wait(scalar, "av", 4 * (g - 1))
                            sems.inc(scalar.copy(wT2[g % 2][:], tp2[g % 2][:]), "tv")

                    for kb in range(2):
                        g = g0 + kb

                        @block.tensor
                        def _(tensor, kb=kb, g=g):
                            sems.wait(tensor, "tv", g + 1)
                            if kb == 0:
                                sems.wait(tensor, "no")
                            for j2 in range(4):
                                sems.inc(tensor.matmul(
                                    o_ps[:], vx[:, w * 4 + kb * 4 + j2, h * 65:h * 65 + 65],
                                    wT2[g % 2][:, j2 * 128:(j2 + 1) * 128],
                                    start=(kb == 0 and j2 == 0),
                                    stop=(kb == 1 and j2 == 3)), "av")

                    @block.vector
                    def _(vector):
                        sems.wait(vector, "av")
                        sems.inc(vector.reciprocal(rA[:], o_ps[64:65, :]), "rc")
                        sems.wait(vector, "rc")
                        sems.inc(vector.tensor_copy(vscr[0:1, :128], rA[:]), "rc")

                    @block.tensor
                    def _(tensor):
                        sems.wait(tensor, "rc")
                        sems.wait(tensor, "ms")
                        sems.wait(tensor, "bc")
                        sems.inc(tensor.matmul(b_ps[:], ones64[:], rA[:], start=True, stop=True), "tb")

                    @block.scalar
                    def _(scalar):
                        sems.wait(scalar, "tb")
                        sems.wait(scalar, "no")
                        sems.inc(scalar.copy(bA_sb[:], b_ps[:]), "bc")

                    @block.vector
                    def _(vector):
                        sems.wait(vector, "bc")
                        sems.inc(vector.scalar_tensor_tensor(
                            oT[hb:hb + 64, ch, q0:q0 + 128], o_ps[0:64, :], 1.0,
                            bA_sb[:], ALU.mult, ALU.mult), "no")

        for j in range(8):
            for t in range(4):
                g = j * 4 + t

                @block.sync
                def _(sync):
                    sems.wait(sync, "xev", max(0, g - 1))
                    xl.issue(sync, g, xstg[g % 2][:],
                             xT[j * 128:(j + 1) * 128, t * 512:(t + 1) * 512])

                @block.tensor
                def _(tensor):
                    if g == 0:
                        sems.wait(tensor, "no")
                        sems.wait(tensor, "cv")
                    sems.wait(tensor, "xev", max(0, g - 1))
                    for c in range(4):
                        sems.inc(tensor.matmul(ps[g % 2][:], wob[:, c, j * 128:(j + 1) * 128],
                                               oT[:, c, t * 512:(t + 1) * 512],
                                               start=(c == 0), stop=(c == 3)), "mm")

                @block.vector
                def _(vector):
                    sems.wait(vector, "mm")
                    xl.wait(vector, g)
                    sems.inc(vector.scalar_tensor_tensor(xout[:, j, t * 512:(t + 1) * 512],
                                                         ps[g % 2][:], 0.0, xstg[g % 2][:],
                                                         ALU.add, ALU.add), "xev")

        @block.gpsimd
        def _(gp):
            sems.wait(gp, "xev")
            for e in range(8):
                stp.issue(gp, 0, xpaT[e * 128:(e + 1) * 128, :], xout[:, e, :])
            stp.wait_all(gp)


def build_l1f_mlp(nc):
    from contextlib import ExitStack
    ctx = ExitStack()
    HP = 2816
    xT = nc.dram_tensor("xT", [D, TPC], F32, kind="ExternalInput").ap()
    xcol8 = nc.dram_tensor("xcol8", [128, 8], F32, kind="ExternalInput").ap()
    lgb = nc.dram_tensor("lgb", [128, 16], F32, kind="ExternalInput").ap()
    w1a = nc.dram_tensor("w1a", [D, HP], F32, kind="ExternalInput").ap()
    w1g = nc.dram_tensor("w1g", [D, HP], F32, kind="ExternalInput").ap()
    b1ag = nc.dram_tensor("b1ag", [128, 44], F32, kind="ExternalInput").ap()
    w2p = nc.dram_tensor("w2p", [HP, D], F32, kind="ExternalInput").ap()
    b2 = nc.dram_tensor("b2", [128, 8], F32, kind="ExternalInput").ap()
    xoutT = nc.dram_tensor("xoutT", [D, TPC], F32, kind="ExternalOutput").ap()

    NJ = 22
    with ctx:
        xsb = ctx.enter_context(nc.sbuf_tensor("xsb", [128, 8, TPC + 1], BF16))
        stg = [ctx.enter_context(nc.sbuf_tensor(f"stg{i}", [128, TPC], F32)) for i in range(2)]
        xc_f = ctx.enter_context(nc.sbuf_tensor("xc_f", [128, 8], F32))
        lgb_sb = ctx.enter_context(nc.sbuf_tensor("lgb_sb", [128, 16], F32))
        b1_sb = ctx.enter_context(nc.sbuf_tensor("b1_sb", [128, 44], F32))
        b2_sb = ctx.enter_context(nc.sbuf_tensor("b2_sb", [128, 8], F32))
        hn = ctx.enter_context(nc.sbuf_tensor("hn", [128, 8, 512], BF16))
        w1stg = [ctx.enter_context(nc.sbuf_tensor(f"w1stg{i}", [128, 8, 128], F32)) for i in range(2)]
        w1b = [ctx.enter_context(nc.sbuf_tensor(f"w1b{i}", [128, 8, 128], BF16)) for i in range(2)]
        aab = ctx.enter_context(nc.sbuf_tensor("aab", [128, NJ, 512], BF16))
        ggb = ctx.enter_context(nc.sbuf_tensor("ggb", [128, NJ, 512], BF16))
        w2stg = [ctx.enter_context(nc.sbuf_tensor(f"w2stg{i}", [128, D], F32)) for i in range(2)]
        w2b = [ctx.enter_context(nc.sbuf_tensor(f"w2b{i}", [128, D], BF16)) for i in range(2)]
        xstg = [ctx.enter_context(nc.sbuf_tensor(f"xstg{i}", [128, 512], F32)) for i in range(2)]
        onesc_b = ctx.enter_context(nc.sbuf_tensor("onesc_b", [128, 1], BF16))
        onesr = ctx.enter_context(nc.sbuf_tensor("onesr", [1, 128], F32))
        xo = ctx.enter_context(nc.sbuf_tensor("xo", [128, 8, 512], F32))
        pool = [ctx.enter_context(nc.psum_tensor(f"pp{i}", [128, 512], F32)) for i in range(8)]
        block = ctx.enter_context(nc.Block())
        names = ["cv", "ms", "w1c", "mm", "ev", "gg", "w2c",
                 "up", "uev", "st",
                 "lnf", "lnm", "lnsq", "lnv", "lnst", "lnr", "lnb", "lnbc", "lnc"]
        sems = Sems(nc, ctx, names)
        sx = DmaStream(nc, ctx, "sx")
        lds = DmaStream(nc, ctx, "lds", 1)
        w1s = DmaStream(nc, ctx, "w1s")
        w2s = DmaStream(nc, ctx, "w2s")
        xl = DmaStream(nc, ctx, "xl")

        def xs(e, lo, hi):
            off = 1 if e < 4 else 0
            return xsb[:, e, lo + off:hi + off]

        @block.gpsimd
        def _(gp):
            sems.inc(gp.memset(onesc_b[:], 1.0), "ms")
            sems.inc(gp.memset(onesr[:], 1.0), "ms")

        @block.sync
        def _(sync):
            for i in range(8):
                if i >= 2:
                    sync.wait_ge(sems.h["cv"], i - 1)
                sx.issue(sync, i, stg[i % 2][:], xT[i * 128:(i + 1) * 128, :])
            lds.issue(sync, 0, xc_f[:], xcol8[:])
            lds.issue(sync, 0, lgb_sb[:], lgb[:])
            lds.issue(sync, 0, b1_sb[:], b1ag[:])
            lds.issue(sync, 0, b2_sb[:], b2[:])

        @block.vector
        def _(vector):
            for i in range(8):
                sx.wait(vector, i)
                sems.inc(vector.tensor_copy(xsb[:, i, 1:TPC + 1], stg[i % 2][:]), "cv")
            lds.wait_all(vector)
            for e in range(8):
                sems.inc(vector.tensor_copy(xsb[:, e, 0:1], xc_f[:, e:e + 1]), "cv")

        @block.vector
        def _(vector):
            sems.wait(vector, "cv")
            sems.inc(vector.tensor_copy(xo[:, 0, 0:1], xsb[:, 0, 0:1]), "lnf")

        ln_bufs = dict(
            y2stg=[ctx.enter_context(nc.sbuf_tensor(f"y2s{i}", [128, 512], BF16)) for i in range(2)],
            rowA=ctx.enter_context(nc.sbuf_tensor("rowA_", [1, 512], F32)),
            rowB=ctx.enter_context(nc.sbuf_tensor("rowB_", [1, 512], F32)),
            rtmp=ctx.enter_context(nc.sbuf_tensor("rtmp_", [1, 512], F32)),
            rowS=ctx.enter_context(nc.sbuf_tensor("rowS_", [1, 512], F32)),
            Ab=[ctx.enter_context(nc.sbuf_tensor(f"Ab_{i}", [128, 512], F32)) for i in range(2)],
            Bb=[ctx.enter_context(nc.sbuf_tensor(f"Bb_{i}", [128, 512], F32)) for i in range(2)],
        )
        for t in range(4):
            def consume(vector, e, tt_, Abuf, Bbuf, t=t):
                if e == 0:
                    sems.wait(vector, "mm")
                sems.inc(vector.scalar_tensor_tensor(hn[:, e, :], xs(e, t * 512, (t + 1) * 512),
                                                     1.0, Abuf[:], ALU.mult, ALU.mult), "gg")
                sems.wait(vector, "gg")
                sems.inc(vector.tensor_tensor(hn[:, e, :], hn[:, e, :], Bbuf[:], ALU.add), "gg")
                sems.wait(vector, "gg")
                sems.inc(vector.tensor_scalar(hn[:, e, :], hn[:, e, :], lgb_sb[:, e:e + 1],
                                              lgb_sb[:, 8 + e:8 + e + 1], ALU.mult, ALU.add), "lnc")

            emit_ln(nc, block, sems, ctx, None, consume,
                    nchunk=8, ntc=1, tw=512, onesc=onesc_b, onesr=onesr, tag="",
                    psum_pool=[pool[0], pool[1], pool[2], pool[3], pool[4], pool[5]],
                    src_cb=lambda e, tt_, t=t: xs(e, t * 512, (t + 1) * 512), y2_dtype=BF16,
                    uniq=f"t{t}", pre_wait=("uev", "ms"), bufs=ln_bufs)

            for part in range(2):
                wsrc = w1a if part == 0 else w1g
                outb = aab if part == 0 else ggb
                for j in range(NJ):
                    gi = (part * NJ + j)
                    pj = gi % 2

                    gidx_w1 = t * 44 + part * NJ + j

                    @block.sync
                    def _(sync):
                        sems.wait(sync, "w1c", max(0, sems.n["w1c"] - 1))
                        w1s.issue(sync, gidx_w1, w1stg[gi % 2][:],
                                  wsrc[:, j * 128:(j + 1) * 128].rearrange("(e p) j -> p e j", p=128))

                    @block.vector
                    def _(vector):
                        w1s.wait(vector, gidx_w1)
                        sems.wait(vector, "mm", max(0, (sems.n["mm"]) - 8))
                        sems.inc(vector.tensor_copy(w1b[gi % 2][:], w1stg[gi % 2][:]), "w1c")

                    @block.tensor
                    def _(tensor):
                        sems.wait(tensor, "w1c")
                        sems.wait(tensor, "lnc")
                        sems.wait(tensor, "ev", max(0, sems.n["ev"] - 1))
                        for e in range(8):
                            sems.inc(tensor.matmul(pool[6 + pj][:], w1b[gi % 2][:, e, :], hn[:, e, :],
                                                   start=(e == 0), stop=(e == 7)), "mm")

                    @block.scalar
                    def _(scalar):
                        sems.wait(scalar, "mm")
                        fn = AF.Identity if part == 0 else AF.Gelu
                        sems.inc(scalar.activation(outb[:, j, :], pool[6 + pj][:], fn,
                                                   bias=b1_sb[:, part * NJ + j:part * NJ + j + 1],
                                                   scale=1.0), "ev")

            @block.vector
            def _(vector):
                sems.wait(vector, "ev")
                for j in range(NJ):
                    sems.inc(vector.tensor_tensor(aab[:, j, :], aab[:, j, :], ggb[:, j, :],
                                                  ALU.mult), "gg")

            for c in range(NJ):
                gidx_w2 = t * NJ + c

                @block.sync
                def _(sync):
                    sems.wait(sync, "w2c", max(0, sems.n["w2c"] - 1))
                    w2s.issue(sync, gidx_w2, w2stg[c % 2][:], w2p[c * 128:(c + 1) * 128, :])

                @block.vector
                def _(vector):
                    w2s.wait(vector, gidx_w2)
                    sems.wait(vector, "up", max(0, sems.n["up"] - 8))
                    sems.inc(vector.tensor_copy(w2b[c % 2][:], w2stg[c % 2][:]), "w2c")

                @block.tensor
                def _(tensor):
                    sems.wait(tensor, "w2c")
                    if c == 0:
                        sems.wait(tensor, "gg")
                        sems.wait(tensor, "uev")
                        sems.wait(tensor, "lnv")
                        sems.wait(tensor, "lnbc")
                    for j in range(8):
                        sems.inc(tensor.matmul(pool[j][:], w2b[c % 2][:, j * 128:(j + 1) * 128],
                                               aab[:, c, :], start=(c == 0), stop=(c == NJ - 1)), "up")

            for j in range(8):
                gidx_x = t * 8 + j

                @block.sync
                def _(sync):
                    sems.wait(sync, "uev", max(0, sems.n["uev"] - 1))
                    xl.issue(sync, gidx_x, xstg[j % 2][:],
                             xT[j * 128:(j + 1) * 128, t * 512:(t + 1) * 512])

                @block.vector
                def _(vector):
                    sems.wait(vector, "up")
                    xl.wait(vector, gidx_x)
                    if j == 0:
                        sems.wait(vector, "st")
                    sems.inc(vector.scalar_tensor_tensor(xo[:, j, :], pool[j][:],
                                                         b2_sb[:, j:j + 1], xstg[j % 2][:],
                                                         ALU.add, ALU.add), "uev")

            @block.gpsimd
            def _(gp):
                sems.wait(gp, "uev")
                for j in range(8):
                    sems.inc(gp.dma_start(out=xoutT[j * 128:(j + 1) * 128, t * 512:(t + 1) * 512],
                                          in_=xo[:, j, :]), "st", 16)
                gp.wait_ge(sems.h["st"], sems.n["st"])

        @block.gpsimd
        def _(gp):
            gp.wait_ge(sems.h["st"], sems.n["st"])


def build_final(nc):
    from contextlib import ExitStack
    ctx = ExitStack()
    x3T = nc.dram_tensor("x3T", [D, TPC], F32, kind="ExternalInput").ap()
    f1T = nc.dram_tensor("f1T", [D, TPC], F32, kind="ExternalInput").ap()
    x4T = nc.dram_tensor("x4T", [D, TPC], F32, kind="ExternalOutput").ap()
    with ctx:
        yf = ctx.enter_context(nc.sbuf_tensor("yf", [128, 8, TPC], F32))
        fstg = [ctx.enter_context(nc.sbuf_tensor(f"fstg{i}", [128, TPC], F32)) for i in range(2)]
        onesc = ctx.enter_context(nc.sbuf_tensor("onesc", [128, 1], F32))
        onesr = ctx.enter_context(nc.sbuf_tensor("onesr", [1, 128], F32))
        block = ctx.enter_context(nc.Block())
        names = ["ms", "ad", "st",
                 "lnf", "lnm", "lnsq", "lnv", "lnst", "lnr", "lnb", "lnbc", "lnc"]
        sems = Sems(nc, ctx, names)
        ld = DmaStream(nc, ctx, "ld", 1)
        fl = DmaStream(nc, ctx, "fl")

        @block.gpsimd
        def _(gp):
            sems.inc(gp.memset(onesc[:], 1.0), "ms")
            sems.inc(gp.memset(onesr[:], 1.0), "ms")

        @block.sync
        def _(sync):
            for e in range(8):
                ld.issue(sync, 0, yf[:, e, :], x3T[e * 128:(e + 1) * 128, :])
            for e in range(8):
                if e >= 2:
                    sync.wait_ge(sems.h["ad"], e - 1)
                fl.issue(sync, e, fstg[e % 2][:], f1T[e * 128:(e + 1) * 128, :])

        @block.vector
        def _(vector):
            ld.wait_all(vector)
            for e in range(8):
                fl.wait(vector, e)
                sems.inc(vector.tensor_tensor(yf[:, e, :], yf[:, e, :], fstg[e % 2][:], ALU.add), "ad")
            sems.wait(vector, "ad")
            sems.inc(vector.tensor_copy(fstg[0][:, 0:1], yf[:, 0, 0:1]), "lnf")

        def consume(vector, e, t, Abuf, Bbuf):
            sems.inc(vector.scalar_tensor_tensor(yf[:, e, t * 512:(t + 1) * 512],
                                                 yf[:, e, t * 512:(t + 1) * 512], 1.0,
                                                 Abuf[:], ALU.mult, ALU.mult), "ad")
            sems.inc(vector.tensor_tensor(yf[:, e, t * 512:(t + 1) * 512],
                                          yf[:, e, t * 512:(t + 1) * 512], Bbuf[:], ALU.add), "lnc")

        emit_ln(nc, block, sems, ctx, yf, consume, onesc=onesc, onesr=onesr)

        @block.gpsimd
        def _(gp):
            sems.wait(gp, "lnc")
            for e in range(8):
                sems.inc(gp.dma_start(out=x4T[e * 128:(e + 1) * 128, :], in_=yf[:, e, :]), "st", 16)
            sems.wait(gp, "st")


def rotary_tables(s0):
    DIM_HEAD, WINDOW, QK = 64, 512, 8.0
    half = np.arange(0, DIM_HEAD, 2, dtype=np.float32)
    inv_freq = 1.0 / 10000.0 ** (half / DIM_HEAD)
    t = np.arange(s0, s0 + TPC, dtype=np.float32)
    fr = t[:, None] * inv_freq[None, :]
    freqs = np.concatenate([fr, fr], axis=-1)
    base = (half + 0.4 * DIM_HEAD) / (1.4 * DIM_HEAD)
    power = (t - S // 2) / float(WINDOW // 2)
    scale = (base[None, :] ** power[:, None]).astype(np.float32)
    scale = np.concatenate([scale, scale], axis=-1)
    cos, sin = np.cos(freqs), np.sin(freqs)
    sgn = np.concatenate([-np.ones(32, np.float32), np.ones(32, np.float32)])
    Cq = (cos * scale * QK).T
    Sq = (sin * scale * QK * sgn[:, None].T).T
    Ck = (cos / scale).T
    Sk = (sin / scale * sgn[:, None].T).T
    out = []
    for tab in (Cq, Sq, Ck, Sk):
        out.append(np.ascontiguousarray(np.concatenate([tab, tab], axis=0).astype(np.float32)))
    return out


def local_masks(first):
    m = np.zeros((4, 128, 1024), np.float32)
    for qt in range(4):
        i = np.arange(128)[:, None] + qt * 128 + 512
        j = np.arange(1024)[None, :]
        bad = (j > i) | (j <= i - 512)
        if first:
            bad = bad | (j < 512)
        m[qt][bad] = -1e30
    return m


def host_prep_l1f_qkv(inputs, li, xT_by_core):
    wqkv = np.ascontiguousarray(inputs["L1_ffn_wqkv"][li].astype(np.float32))
    qs = inputs["L1_ffn_q_scale"][li].astype(np.float32)
    ks = inputs["L1_ffn_k_scale"][li].astype(np.float32)
    qsr = np.ascontiguousarray(np.stack([np.tile(qs, 2), np.tile(ks, 2)], axis=1))
    selmT = np.zeros((8, 4, 128), np.float32)
    for c in range(4):
        selmT[2 * c, c, 0:64] = 1.0
        selmT[2 * c + 1, c, 64:128] = 1.0
    selm = np.ascontiguousarray(selmT.transpose(2, 1, 0))
    per_core = []
    for c in range(NC):
        tabs = np.stack(rotary_tables((c % 4) * TPC), axis=0)
        if c % 4 == 0:
            xcol = np.zeros((128, 8), np.float32)
        else:
            xcol = np.ascontiguousarray(xT_by_core[c - 1][:, -1].reshape(8, 128).T)
        per_core.append({"xT": xT_by_core[c], "xcol8": xcol, "wqkv": wqkv,
                         "qsr": qsr, "tabs": tabs, "selm": selm, "selmT": selmT})
    return per_core


def host_prep_l1f_attn(inputs, li, res_qkv, xT_by_core):
    wo = np.ascontiguousarray(inputs["L1_ffn_wo"][li].astype(np.float32))
    ident = np.eye(128, dtype=BF)
    m0 = local_masks(False)
    m1n = local_masks(False)
    m1f = local_masks(True)
    per_core = []
    for c in range(NC):
        if c % 4 == 0:
            kb4 = np.zeros((128, 4, 512), BF)
            vb4 = np.zeros((128, 4, 520), BF)
            m1 = m1f
        else:
            krN = res_qkv.results[c - 1]["krT"]
            kb4 = np.ascontiguousarray(krN[:, -512:].reshape(4, 128, 512).transpose(1, 0, 2))
            vaN = res_qkv.results[c - 1]["vaug"]
            vb4 = np.ascontiguousarray(vaN[-512:, :].reshape(4, 128, 520).transpose(1, 0, 2))
            m1 = m1n
        per_core.append({"qrT": res_qkv.results[c]["qrT"], "krT": res_qkv.results[c]["krT"],
                         "vaug": res_qkv.results[c]["vaug"], "kb4": kb4, "vb4": vb4,
                         "mask0": m0, "mask1": m1, "xT": xT_by_core[c], "wo": wo, "ident": ident})
    return per_core


def host_prep_l1f_mlp(inputs, li, xpa_by_core):
    HP = 2816
    HID = 2730
    w1 = inputs["L1_ffn_w1"][li].astype(np.float32)
    b1 = inputs["L1_ffn_b1"][li].astype(np.float32)
    w2 = inputs["L1_ffn_w2"][li].astype(np.float32)
    b2 = inputs["L1_ffn_b2"][li].astype(np.float32)
    lg = inputs["L1_ffn_ln_g"][li].astype(np.float32)
    lb = inputs["L1_ffn_ln_b"][li].astype(np.float32)
    w1a = np.zeros((D, HP), np.float32); w1a[:, :HID] = w1[:, :HID]
    w1g = np.zeros((D, HP), np.float32); w1g[:, :HID] = w1[:, HID:]
    b1a = np.zeros(HP, np.float32); b1a[:HID] = b1[:HID]
    b1g = np.zeros(HP, np.float32); b1g[:HID] = b1[HID:]
    b1ag = np.ascontiguousarray(np.concatenate(
        [b1a.reshape(22, 128).T, b1g.reshape(22, 128).T], axis=1))
    w2p = np.zeros((HP, D), np.float32); w2p[:HID] = w2
    lgb = np.ascontiguousarray(np.concatenate(
        [lg.reshape(8, 128).T, lb.reshape(8, 128).T], axis=1))
    b2c = np.ascontiguousarray(b2.reshape(8, 128).T)
    per_core = []
    for c in range(NC):
        if c % 4 == 0:
            xcol = np.zeros((128, 8), np.float32)
        else:
            xcol = np.ascontiguousarray(xpa_by_core[c - 1][:, -1].reshape(8, 128).T)
        per_core.append({"xT": xpa_by_core[c], "xcol8": xcol, "lgb": lgb,
                         "w1a": w1a, "w1g": w1g, "b1ag": b1ag, "w2p": w2p, "b2": b2c})
    return per_core


def build_l1_post(nc):
    build_l0_post(nc, with_pooled=False)


def host_prep_l1_post(inputs, res4, res6):
    wout = np.ascontiguousarray(inputs["L1_attn_Wo"].astype(np.float32).T)
    bout = np.ascontiguousarray(inputs["L1_attn_bo"].astype(np.float32).reshape(8, 128).T)
    per_core = []
    for c in range(NC):
        per_core.append({"aT": res6.results[c]["aT"], "wout": wout, "bout": bout,
                         "xT": res4.results[c]["x2T"]})
    return per_core


def host_prep_l0_qkv(inputs):
    x = inputs["L0_attn_x"]
    wqkv = np.ascontiguousarray(inputs["L0_attn_W_qkv"].astype(np.float32))
    bqkv = inputs["L0_attn_b_qkv"].astype(np.float32)
    bq16 = np.ascontiguousarray(bqkv[:2048].reshape(16, 128).T)
    bv_row = np.ascontiguousarray(bqkv[2048:].reshape(1, 1024))
    per_core = []
    for c in range(NC):
        b, s0 = c // 4, (c % 4) * TPC
        xT = np.ascontiguousarray(x[b, s0:s0 + TPC, :].T.astype(np.float32))
        per_core.append({"xT": xT, "wqkv": wqkv, "bqkv": bq16, "bv_row": bv_row})
    return per_core


def kernel(**inputs):
    r1 = run_stage(build_l0_qkv, host_prep_l0_qkv(inputs))
    r2 = run_stage(build_l0_attn, host_prep_l0_attn(inputs, r1))
    r3 = run_stage(build_l0_post, host_prep_l0_post(inputs, r1, r2))
    r4 = run_stage(build_l0_moe, host_prep_l0_moe(inputs, r3))
    r5 = run_stage(build_l1_qkv, host_prep_l1_qkv(inputs, r4))
    r6 = run_stage(build_l1_attn, host_prep_l1_attn(inputs, r5))
    r7 = run_stage(build_l1_post, host_prep_l1_post(inputs, r4, r6))
    xT = [r7.results[c]["x1T"] for c in range(NC)]
    for li in range(2):
        rq = run_stage(build_l1f_qkv, host_prep_l1f_qkv(inputs, li, xT))
        ra = run_stage(build_l1f_attn, host_prep_l1f_attn(inputs, li, rq, xT))
        xpa = [ra.results[c]["xpaT"] for c in range(NC)]
        rm = run_stage(build_l1f_mlp, host_prep_l1f_mlp(inputs, li, xpa))
        xT = [rm.results[c]["xoutT"] for c in range(NC)]
    x3 = [r7.results[c]["x1T"] for c in range(NC)]
    r9 = run_stage(build_final, [{"x3T": x3[c], "f1T": xT[c]} for c in range(NC)])
    out = np.empty((B, S, D), np.float32)
    for c in range(NC):
        b, s0 = c // 4, (c % 4) * TPC
        out[b, s0:s0 + TPC, :] = r9.results[c]["x4T"].T
    return out
